# version 27
# speedup vs baseline: 1.0039x; 1.0039x over previous
import contextlib
import numpy as np
import concourse.bass as bass
import concourse.mybir as mybir
from concourse.bass import IndirectOffsetOnAxis

F32 = mybir.dt.float32
BF16 = mybir.dt.bfloat16
U32 = mybir.dt.uint32
AF = mybir.ActivationFunctionType
ALU = mybir.AluOpType
AX = mybir.AxisListType

ENGINES = ("sync", "tensor", "vector", "scalar", "gpsimd")

D = 1024
DC = 8
T_FULL = 16384
N_CORES = 8
T_SH = 2048
NT_SH = 16
NT_FULL = 128
E = 8
FF = 4096
FT = 32
CAP0 = 2048
CAP1 = 5120
OOBV = 10**9
IDMA_INC = 16


class _CountInstr:
    def __init__(self, c):
        self._c = c

    def then_inc(self, h, k=1):
        self._c[id(h)] = self._c.get(id(h), 0) + k
        return self


class _CountEng:

    def __init__(self, counts):
        self._c = counts

    def register(self, name):
        @contextlib.contextmanager
        def _cm():
            yield object()
        return _cm()

    def __getattr__(self, name):
        def call(*a, **kw):
            return _CountInstr(self._c)
        return call


class _EngProxy:

    def __init__(self, eng, bases, pooled_ids):
        self._eng = eng
        self._bases = bases
        self._pooled = pooled_ids
        self._phase = 0

    def __getattr__(self, k):
        return getattr(self._eng, k)

    def advance(self):
        self._phase += 1

    def wait_ge(self, h, v):
        b = 0
        if id(h) in self._pooled and self._phase < len(self._bases):
            b = self._bases[self._phase].get(id(h), 0)
        return self._eng.wait_ge(h, b + v)


class Sem:
    def __init__(self, handle):
        self.h = handle
        self.n = 0

    def inc(self, k=1):
        self.n += k
        return self.n


class Prog:
    def __init__(self, name="nc", pooled=False):
        self.nc = bass.Bass("TRN2", target_bir_lowering=False, debug=False,
                            detect_race_conditions=False)
        self.es = contextlib.ExitStack()
        self.ops = {e: [] for e in ENGINES}
        self._n = 0
        self.pooled = pooled
        self._pool = []
        self._cursor = 0
        self._dpool = {}
        self._dcursor = {}
        self._all_sems = []
        self._phase_es = None
        self._bar = None
        self.tcarrier = None
        if pooled:
            self._init_bar()

    def _mk_sem(self, name):
        s = Sem(self.es.enter_context(self.nc.semaphore(name)))
        self._all_sems.append(s)
        return s

    def sem(self, name=None):
        self._n += 1
        if self.pooled and self._phase_es is not None:
            if self._cursor == len(self._pool):
                self._pool.append(self._mk_sem(f"pool{len(self._pool)}"))
            s = self._pool[self._cursor]
            self._cursor += 1
            return s
        return self._mk_sem(name or f"s{self._n}")

    def dsem(self, queue):
        self._n += 1
        if not (self.pooled and self._phase_es is not None):
            return self._mk_sem(f"d{self._n}")
        pool = self._dpool.setdefault(queue, [])
        cur = self._dcursor.get(queue, 0)
        if cur == len(pool):
            pool.append(self._mk_sem(f"dp_{queue}{len(pool)}"))
        self._dcursor[queue] = cur + 1
        return pool[cur]

    def sb(self, name, shape, dt):
        st = self._phase_es if self._phase_es is not None else self.es
        return st.enter_context(self.nc.sbuf_tensor(name, shape, dt))

    def ps(self, name, shape, dt=F32):
        st = self._phase_es if self._phase_es is not None else self.es
        return st.enter_context(self.nc.psum_tensor(name, shape, dt))

    def dram(self, name, shape, dt, kind="Internal"):
        return self.nc.dram_tensor(name, shape, dt, kind=kind)

    def on(self, eng, f):
        self.ops[eng].append(f)

    def _init_bar(self):
        if self._bar is None:
            bs_v = self._mk_sem("barsem_v")
            bs_g = self._mk_sem("barsem_g")
            bs_t = self._mk_sem("barsem_t")
            bs_sy = self._mk_sem("barsem_sy")
            bs_sc = self._mk_sem("barsem_sc")
            bdr = self.nc.dram_tensor("bar_dram", [1, 8], F32, kind="Internal")
            scr = {e: self.es.enter_context(
                self.nc.sbuf_tensor(f"barscr_{e}", [1, 8], F32))
                for e in ("sync", "vector", "scalar", "gpsimd", "init")}
            binit = self._mk_sem("barinit")

            def g0(eng):
                eng.memset(scr["init"][:], 0.0)
                eng.drain()
                eng.dma_start(out=bdr[0:1, :],
                              in_=scr["init"][:]).then_inc(binit.h, 16)
            self.on("gpsimd", g0)
            binit.inc(16)
            self._bar = (bs_v, bs_g, bs_t, bs_sy, bs_sc, bdr, scr, binit)

    def barrier(self):
        self._init_bar()
        bs_v, bs_g, bs_t, bs_sy, bs_sc, bdr, scr, binit = self._bar
        t_v = bs_v.inc(1)
        t_g = bs_g.inc(1)
        t_t = bs_t.inc(1)
        t_sy = bs_sy.inc(16)
        t_sc = bs_sc.inc(16)

        def waits(eng):
            eng.wait_ge(bs_sy.h, t_sy)
            eng.wait_ge(bs_sc.h, t_sc)
            eng.wait_ge(bs_v.h, t_v)
            eng.wait_ge(bs_g.h, t_g)
            eng.wait_ge(bs_t.h, t_t)

        for e, bsd in (("sync", bs_sy), ("scalar", bs_sc)):
            def f_d(eng, e=e, bsd=bsd):
                if e == "scalar":
                    eng.drain()
                eng.wait_ge(binit.h, 16)
                eng.dma_start(out=scr[e][:], in_=bdr[0:1, :]).then_inc(bsd.h, 16)
                waits(eng)
            self.on(e, f_d)

        for e, bse in (("vector", bs_v), ("gpsimd", bs_g)):
            def f_c(eng, e=e, bse=bse):
                eng.drain()
                eng.memset(scr[e][:], 0.0).then_inc(bse.h, 1)
                waits(eng)
                eng.drain()
            self.on(e, f_c)

        def f_t(eng):
            if self.tcarrier is not None:
                ins = self.tcarrier(eng)
                ins.then_inc(bs_t.h, 1)
            else:
                eng.sem_inc(bs_t.h, 1)
            waits(eng)
        self.on("tensor", f_t)

    @contextlib.contextmanager
    def phase(self, name=""):
        outer = self._phase_es
        outer_cursor = self._cursor
        outer_dcursor = dict(self._dcursor)
        self._phase_es = contextlib.ExitStack()
        self._cursor = 0
        self._dcursor = {}
        try:
            yield
        finally:
            self._phase_es.close()
            self._phase_es = outer
            self._cursor = outer_cursor
            self._dcursor = outer_dcursor
            if self.pooled:
                self.ops["gpsimd"].append(("DRAIN",))
            self.barrier()
            if self.pooled:
                for e in ENGINES:
                    self.ops[e].append(("REBASE", None))
                for s in self._pool:
                    s.n = 0
                for pool in self._dpool.values():
                    for s in pool:
                        s.n = 0

    def finalize(self):
        pooled_ids = {id(s.h) for s in self._pool}
        for pool in self._dpool.values():
            pooled_ids |= {id(s.h) for s in pool}
        n_phases = 1 + max(
            (sum(1 for f in self.ops[e]
                 if isinstance(f, tuple) and f[0] == "REBASE")
             for e in ENGINES), default=0)
        per_phase = [dict() for _ in range(n_phases)]
        snap = [(s, s.n) for s in self._all_sems]
        for e in ENGINES:
            ph = 0
            for f in self.ops[e]:
                if isinstance(f, tuple):
                    if f[0] == "REBASE":
                        ph += 1
                else:
                    f(_CountEng(per_phase[ph]))
        for s, n in snap:
            s.n = n
        bases = []
        acc = {}
        for k in range(n_phases):
            bases.append(dict(acc))
            for hid, c in per_phase[k].items():
                acc[hid] = acc.get(hid, 0) + c
        dma_sems = [s for pool in self._dpool.values() for s in pool]
        dma_sems = dma_sems + list(self._pool)

        block = self.es.enter_context(self.nc.Block())

        def run(eng_name):
            def _run(eng):
                proxy = _EngProxy(eng, bases, pooled_ids)
                for f in self.ops[eng_name]:
                    if isinstance(f, tuple):
                        if f[0] == "REBASE":
                            proxy.advance()
                        elif f[0] == "DRAIN":
                            ph = proxy._phase
                            import os as _os
                            for s in dma_sems:
                                act = per_phase[ph].get(id(s.h), 0)
                                if act > 0:
                                    if _os.environ.get("DBG_DRAIN"):
                                        print("DRAIN", eng_name, "ph", ph,
                                              getattr(s.h, 'name', '?'),
                                              bases[ph].get(id(s.h), 0) + act)
                                    eng.wait_ge(s.h, bases[ph].get(id(s.h), 0) + act)
                    else:
                        f(proxy)
            return _run
        block.sync(run("sync"))
        block.tensor(run("tensor"))
        block.vector(run("vector"))
        block.scalar(run("scalar"))
        block.gpsimd(run("gpsimd"))
        self.es.close()
        return self.nc


class Consts:

    def __init__(self, p: Prog, need=("ident", "tri", "iota8", "ones")):
        self.p = p
        self.sem = p.sem("const")
        need = set(need)
        self.ident_bf = p.sb("c_ident_bf", [128, 128], BF16) if "ident" in need else None
        self.ident_f32 = p.sb("c_ident_f", [128, 128], F32) if "ident" in need else None
        self.tri_bf = p.sb("c_tri_bf", [128, 128], BF16) if "tri" in need else None
        self.iota8 = p.sb("c_iota8", [128, 8], F32) if "iota8" in need else None
        self.ones_bf = p.sb("c_ones_bf", [128, 1], BF16) if "ones" in need else None
        self.ones_f32 = p.sb("c_ones_f", [128, 1], F32) if "ones" in need else None
        tgt = self.sem.inc()

        def g(eng):
            if self.ident_f32 is not None:
                eng.memset(self.ident_f32[:], 1.0)
                eng.drain()
                eng.affine_select(out=self.ident_f32[:], in_=self.ident_f32[:],
                                  compare_op=ALU.is_ge, fill=0.0, base=0,
                                  pattern=[[-1, 128]], channel_multiplier=1)
                eng.drain()
                eng.affine_select(out=self.ident_f32[:], in_=self.ident_f32[:],
                                  compare_op=ALU.is_ge, fill=0.0, base=0,
                                  pattern=[[1, 128]], channel_multiplier=-1)
                eng.drain()
                eng.tensor_copy(self.ident_bf[:], self.ident_f32[:])
                eng.drain()
            if self.tri_bf is not None:
                eng.memset(self.tri_bf[:], 0.0)
                eng.drain()
                eng.affine_select(out=self.tri_bf[:], in_=self.tri_bf[:],
                                  compare_op=ALU.is_ge, fill=1.0, base=0,
                                  pattern=[[-1, 128]], channel_multiplier=1)
                eng.drain()
            if self.iota8 is not None:
                eng.iota(self.iota8[:], pattern=[[1, 8]], base=0,
                         channel_multiplier=0,
                         allow_small_or_imprecise_dtypes=True)
            if self.ones_bf is not None:
                eng.memset(self.ones_bf[:], 1.0)
            eng.drain()
            eng.memset(self.ones_f32[:], 1.0).then_inc(self.sem.h, 1)
        p.on("gpsimd", g)
        self.tgt = tgt


def dma_in(p, eng_name, pairs, sem, wait=None):
    def f(eng):
        if wait is not None:
            eng.wait_ge(wait[0].h, wait[1])
        for dst, src in pairs:
            eng.dma_start(out=dst, in_=src).then_inc(sem.h, 16)
    p.on(eng_name, f)
    return sem.inc(16 * len(pairs))


def emit_ln_sweep(p, n_tiles, get_x, get_out, sems, scratch, wait_v=None,
                  wait_tile=None, name=""):
    bn6, stats, rstd = scratch
    sem_v2s, sem_s2v, sem_done = sems

    def v1(eng):
        if wait_v is not None:
            eng.wait_ge(wait_v[0].h, wait_v[1])
        for j in range(n_tiles):
            if wait_tile is not None:
                eng.wait_ge(wait_tile[j][0].h, wait_tile[j][1])
            eng.bn_stats(bn6[:, 0:6], get_x(j)[:, 0:512])
            eng.bn_stats(bn6[:, 6:12], get_x(j)[:, 512:1024])
            eng.drain()
            eng.bn_aggr(stats[:, 2 * j:2 * j + 2], bn6[:])
            eng.drain()
        eng.tensor_scalar(rstd[:], stats[:, 1::2], 1e-5, None, op0=ALU.add)
        eng.drain()
        eng.reciprocal(rstd[:], rstd[:]).then_inc(sem_v2s.h, 1)
        eng.drain()
    p.on("vector", v1)
    t1 = sem_v2s.inc()

    def s1(eng):
        eng.wait_ge(sem_v2s.h, t1)
        eng.activation(rstd[:], rstd[:], AF.Sqrt).then_inc(sem_s2v.h, 1)
        eng.drain()
    p.on("scalar", s1)
    t2 = sem_s2v.inc()

    def v2(eng):
        eng.wait_ge(sem_s2v.h, t2)
        for j in range(n_tiles):
            ins = eng.tensor_scalar(get_out(j), get_x(j), stats[:, 2 * j:2 * j + 1],
                                    rstd[:, j:j + 1], op0=ALU.subtract, op1=ALU.mult)
        ins.then_inc(sem_done.h, 1)
        eng.drain()
    p.on("vector", v2)
    return sem_done.inc()


def emit_route_l0(p, cst, ln_view, rw_sb, n_tiles=NT_SH, psb=None, rpfx="r"):
    if psb is None:
        ptr = [p.ps(f"r_ptr{i}", [128, 1024], F32) for i in range(2)]
        plog = p.ps("r_plog", [128, n_tiles * 8], F32)
        pcnt = p.ps("r_pcnt", [1, n_tiles * 8], F32)
        pwcum = p.ps("r_pwcum", [128, n_tiles * 8], F32)
    else:
        ptr = [psb["F1024"][:, 0:1024], psb["F1024"][:, 0:1024]]
        plog = psb["F0"][:, 0:n_tiles * 8]
        pcnt = psb["F1"][0:1, 0:n_tiles * 8]
        pwcum = psb["F1"][:, 256:256 + n_tiles * 8]
    stage = [p.sb(rpfx + f"_stage{i}", [128, 1024], F32) for i in range(2)]
    logit_sb = p.sb(rpfx + "_logit", [128, n_tiles * 8], F32)
    vm8 = p.sb(rpfx + "_vm8", [128, 8], F32)
    idx8 = p.sb(rpfx + "_idx8", [128, 8], U32)
    vmax_all = p.sb(rpfx + "_vmax", [128, n_tiles], F32)
    negm_all = p.sb(rpfx + "_negm", [128, n_tiles], F32)
    exp_all = p.sb(rpfx + "_exp", [128, n_tiles * 8], F32)
    sume_all = p.sb(rpfx + "_sume", [128, n_tiles], F32)
    gate_all = p.sb(rpfx + "_gate", [128, n_tiles], F32)
    idxu_all = p.sb(rpfx + "_idxu", [128, n_tiles], U32)
    idxf_all = p.sb(rpfx + "_idxf", [128, n_tiles], F32)
    emask_f = p.sb(rpfx + "_emf", [128, n_tiles * 8], F32)
    emask_bf = p.sb(rpfx + "_embf", [128, n_tiles * 8], BF16)
    wtmp = p.sb(rpfx + "_wtmp", [128, 8], F32)
    wcum_all = p.sb(rpfx + "_wcum", [128, n_tiles], F32)
    counts_sb = p.sb(rpfx + "_cnt", [1, n_tiles * 8], F32)

    semT = p.sem()
    semV = p.sem()
    semT2 = p.sem()
    semV2 = p.sem()
    semS = p.sem()
    semVe = p.sem()
    semT3 = p.sem()
    sem_done = p.sem()

    single = psb is not None

    def t1(eng):
        eng.wait_ge(cst.sem.h, cst.tgt)
        for j in range(n_tiles):
            if j >= (1 if single else 2):
                eng.wait_ge(semV.h, j - (0 if single else 1))
            for dc in range(DC):
                ins = eng.transpose(ptr[j % 2][:, 128 * dc:128 * (dc + 1)],
                                    ln_view(j)[:, 128 * dc:128 * (dc + 1)],
                                    cst.ident_f32[:])
            ins.then_inc(semT.h, 1)
            semT.inc()
            if j >= 1:
                eng.wait_ge(semV.h, j)
                for dc in range(DC):
                    eng.matmul(plog[:, 8 * (j - 1):8 * j],
                               stage[(j - 1) % 2][:, 128 * dc:128 * (dc + 1)],
                               rw_sb[:, 8 * dc:8 * (dc + 1)],
                               start=(dc == 0), stop=(dc == DC - 1))
        eng.wait_ge(semV.h, n_tiles)
        for dc in range(DC):
            ins = eng.matmul(plog[:, 8 * (n_tiles - 1):8 * n_tiles],
                             stage[(n_tiles - 1) % 2][:, 128 * dc:128 * (dc + 1)],
                             rw_sb[:, 8 * dc:8 * (dc + 1)],
                             start=(dc == 0), stop=(dc == DC - 1))
        ins.then_inc(semT2.h, 1)
    p.on("tensor", t1)
    tT2 = semT2.inc()

    def v1(eng):
        for j in range(n_tiles):
            eng.wait_ge(semT.h, j + 1)
            eng.tensor_copy(stage[j % 2][:], ptr[j % 2][:]).then_inc(semV.h, 1)
            eng.drain()
        semV.inc(n_tiles)
        eng.wait_ge(semT2.h, tT2)
        eng.tensor_copy(logit_sb[:], plog[:])
        eng.drain()
        for j in range(n_tiles):
            sl = logit_sb[:, 8 * j:8 * (j + 1)]
            eng.max(vm8[:], sl)
            eng.drain()
            eng.max_index(idx8[:], vm8[:], sl)
            eng.drain()
            eng.tensor_copy(idxu_all[:, j:j + 1], idx8[:, 0:1])
            eng.tensor_copy(vmax_all[:, j:j + 1], vm8[:, 0:1])
            eng.drain()
        eng.tensor_scalar(negm_all[:], vmax_all[:], -1.0, None,
                          op0=ALU.mult).then_inc(semV2.h, 1)
        eng.drain()
    p.on("vector", v1)
    tV2 = semV2.inc()

    def s1(eng):
        eng.wait_ge(semV2.h, tV2)
        for j in range(n_tiles):
            ins = eng.activation(exp_all[:, 8 * j:8 * (j + 1)],
                                 logit_sb[:, 8 * j:8 * (j + 1)], AF.Exp,
                                 bias=negm_all[:, j:j + 1], scale=1.0,
                                 accum_out=sume_all[:, j:j + 1])
        ins.then_inc(semS.h, 1)
        eng.drain()
    p.on("scalar", s1)
    tS = semS.inc()

    def v2(eng):
        eng.wait_ge(semS.h, tS)
        eng.reciprocal(gate_all[:], sume_all[:])
        eng.tensor_copy(idxf_all[:], idxu_all[:])
        eng.drain()
        for j in range(n_tiles):
            eng.tensor_scalar(emask_f[:, 8 * j:8 * (j + 1)], cst.iota8[:],
                              idxf_all[:, j:j + 1], None, op0=ALU.is_equal)
        eng.drain()
        eng.tensor_copy(emask_bf[:], emask_f[:]).then_inc(semVe.h, 1)
        eng.drain()
    p.on("vector", v2)
    tVe = semVe.inc()

    def t2(eng):
        eng.wait_ge(semVe.h, tVe)
        for j in range(n_tiles):
            eng.matmul(pcnt[0:1, 8 * j:8 * (j + 1)], cst.ones_bf[:],
                       emask_bf[:, 8 * j:8 * (j + 1)], start=True, stop=True)
            ins = eng.matmul(pwcum[:, 8 * j:8 * (j + 1)], cst.tri_bf[:],
                             emask_bf[:, 8 * j:8 * (j + 1)], start=True, stop=True)
        ins.then_inc(semT3.h, 1)
    p.on("tensor", t2)
    tT3 = semT3.inc()

    def v3(eng):
        eng.wait_ge(semT3.h, tT3)
        eng.tensor_copy(counts_sb[:], pcnt[:])
        for j in range(n_tiles):
            eng.tensor_tensor(wtmp[:], pwcum[:, 8 * j:8 * (j + 1)],
                              emask_f[:, 8 * j:8 * (j + 1)], op=ALU.mult)
            eng.drain()
            ins = eng.reduce_sum(wcum_all[:, j:j + 1], wtmp[:], axis=AX.X)
            eng.drain()
        ins.then_inc(sem_done.h, 1)
    p.on("vector", v3)
    t_done = sem_done.inc()

    return dict(counts_sb=counts_sb, idxf_all=idxf_all, gate_all=gate_all,
                wcum_all=wcum_all, emask_bf=emask_bf, emask_f=emask_f,
                sem_done=sem_done, tgt_done=t_done, logit_sb=logit_sb)


def build_route_l0_launch():
    p = Prog()
    nc = p.nc
    x_in = p.dram("x_sh", [T_SH, D], F32, kind="ExternalInput")
    rw_in = p.dram("router_w", [D, E], F32, kind="ExternalInput")
    ln_out = p.dram("ln_bf", [T_SH, D], BF16, kind="ExternalOutput")
    cnt_out = p.dram("counts", [1, NT_SH * E], F32, kind="ExternalOutput")
    idx_out = p.dram("idx_o", [128, NT_SH], F32, kind="ExternalOutput")
    gate_out = p.dram("gate_o", [128, NT_SH], F32, kind="ExternalOutput")
    wcum_out = p.dram("wcum_o", [128, NT_SH], F32, kind="ExternalOutput")

    cst = Consts(p)
    x_sb = p.sb("x_sb", [128, NT_SH * D], F32)
    ln_sb = p.sb("ln_sb", [128, NT_SH * D], F32)
    rw_sb = p.sb("rw_sb", [128, DC * E], F32)
    bn6 = p.sb("bn6", [128, 12], F32)
    stats = p.sb("stats", [128, 2 * NT_SH], F32)
    rstd = p.sb("rstd", [128, NT_SH], F32)

    sem_ld = p.sem("ld")
    sem_v2s = p.sem(); sem_s2v = p.sem(); sem_lnd = p.sem()
    sem_st = p.sem("st")

    x_t = x_in.ap().rearrange("(j p) d -> p j d", p=128)
    rw_t = rw_in.ap().rearrange("(c p) e -> p c e", p=128)
    x_d = x_sb[:].rearrange("p (j d) -> p j d", d=D)
    rw_d = rw_sb[:].rearrange("p (c e) -> p c e", e=E)
    t_ld = dma_in(p, "sync", [(x_d, x_t), (rw_d, rw_t)], sem_ld)

    def xv(j):
        return x_sb[:, D * j:D * (j + 1)]

    def lv(j):
        return ln_sb[:, D * j:D * (j + 1)]

    t_ln = emit_ln_sweep(p, NT_SH, xv, lv, (sem_v2s, sem_s2v, sem_lnd),
                         (bn6, stats, rstd), wait_v=(sem_ld, t_ld))

    def tw(eng):
        eng.wait_ge(sem_lnd.h, t_ln)
    p.on("tensor", tw)

    r = emit_route_l0(p, cst, lv, rw_sb, n_tiles=NT_SH)

    def g_out(eng):
        eng.wait_ge(sem_lnd.h, t_ln)
        eng.dma_start(out=ln_out.ap().rearrange("(j p) d -> p j d", p=128),
                      in_=ln_sb[:].rearrange("p (j d) -> p j d", d=D)
                      ).then_inc(sem_st.h, 16)
        eng.wait_ge(r["sem_done"].h, r["tgt_done"])
        eng.dma_start(out=cnt_out[:, :], in_=r["counts_sb"][:]).then_inc(sem_st.h, 16)
        eng.dma_start(out=idx_out[:, :], in_=r["idxf_all"][:]).then_inc(sem_st.h, 16)
        eng.dma_start(out=gate_out[:, :], in_=r["gate_all"][:]).then_inc(sem_st.h, 16)
        eng.dma_start(out=wcum_out[:, :], in_=r["wcum_all"][:]).then_inc(sem_st.h, 16)
        eng.wait_ge(sem_st.h, 16 * 5)
    p.on("gpsimd", g_out)
    sem_st.inc(16 * 5)

    p.finalize()
    return nc


def emit_pos_scatter(p, cst, pre, own_range, n_tiles=NT_FULL, cap=CAP0,
                     slot_tab=None, tile_base=0):
    ppos = pre["ppos"]
    semB = p.sem()
    semVp = p.sem()
    semZF = p.dsem("gpsimd")
    semSC = p.dsem("gpsimd")
    ident8 = cst.ident_f32[0:8, 0:8]
    offsT_bf = pre["offsT_bf"]
    lo, hi = own_range

    def t_b(eng):
        eng.wait_ge(pre["sem_offs"].h, pre["tgt_offs"])
        for j in range(n_tiles):
            if j >= 2:
                eng.wait_ge(semVp.h, j - 1)
            col = offsT_bf[:, j:j + 1]
            col_b = bass.AP(col.tensor, col.offset, [[col.ap[0][0], 8], [0, 128]])
            eng.matmul(ppos[j % 2][:], col_b, ident8,
                       start=True, stop=True).then_inc(semB.h, 1)
            semB.inc()
    p.on("tensor", t_b)

    emask = pre["emask"]; postmp = pre["postmp"]
    pos_c = pre["pos_c"]; keep_c = pre["keep_c"]; slot_f = pre["slot_f"]
    slot_u4 = pre["slot_u4"]; tok_stage = pre["tok_stage"]; sel_c = pre["sel_c"]

    def v_p(eng):
        for j in range(n_tiles):
            eng.wait_ge(semB.h, j + 1)
            if j >= 4:
                eng.wait_ge(semSC.h, IDMA_INC * (j - 3))
            eng.tensor_scalar(emask[:], cst.iota8[:], pre["idx_sb"][:, j:j + 1],
                              None, op0=ALU.is_equal)
            eng.drain()
            eng.tensor_tensor(postmp[:], ppos[j % 2][:], emask[:], op=ALU.mult)
            eng.drain()
            eng.reduce_sum(pos_c[:], postmp[:], axis=AX.X)
            eng.drain()
            eng.tensor_tensor(pos_c[:], pos_c[:], pre["wcum_sb"][:, j:j + 1],
                              op=ALU.add)
            eng.drain()
            eng.tensor_scalar(keep_c[:], pos_c[:], float(cap), None, op0=ALU.is_lt)
            eng.tensor_scalar(sel_c[:], pre["idx_sb"][:, j:j + 1], pre["ecol"][:],
                              None, op0=ALU.is_equal)
            eng.drain()
            eng.tensor_tensor(sel_c[:], sel_c[:], keep_c[:], op=ALU.mult)
            eng.drain()
            eng.tensor_copy(pre["sel_u"][:], sel_c[:])
            eng.drain()
            eng.select(slot_f[:], pre["sel_u"][:], pos_c[:], pre["pcol_oob"][:], add_drain=True)
            eng.drain()
            eng.tensor_copy(slot_u4[:, j % 4:j % 4 + 1], slot_f[:])
            eng.tensor_scalar(tok_stage[:, 2 * (j % 4):2 * (j % 4) + 1],
                              pre["pcol"][:], float(128 * (tile_base + j)),
                              None, op0=ALU.add)
            eng.tensor_copy(tok_stage[:, 2 * (j % 4) + 1:2 * (j % 4) + 2],
                            pre["gate_sb"][:, j:j + 1])
            eng.drain()
            if lo <= j < hi:
                eng.tensor_scalar(postmp[:, 0:1], pre["idx_sb"][:, j:j + 1],
                                  float(cap), pos_c[:], op0=ALU.mult, op1=ALU.add)
                eng.tensor_copy(pre["keep_u"][:], keep_c[:])
                eng.drain()
                eng.select(pre["fp_sb"][:, j - lo:j - lo + 1], pre["keep_u"][:],
                           postmp[:, 0:1], pre["pcol_oob"][:], add_drain=True)
            ins = eng.tensor_copy(sel_c[:], keep_c[:])
            ins.then_inc(semVp.h, 1)
            semVp.inc()
            eng.drain()
    p.on("vector", v_p)

    def g_s(eng):
        eng.wait_ge(pre["sem_zero"].h, pre["tgt_zero"])
        eng.dma_start(out=slot_tab.ap().rearrange("(j p) x -> p j x", p=128),
                      in_=pre["zero_cap"][:].rearrange("p (j x) -> p j x", x=2)
                      ).then_inc(semZF.h, 16)
        eng.wait_ge(semZF.h, 16)
        with eng.register("bc_scat_" + slot_tab.name) as bc:
            eng.reg_mov(bc, cap - 1)
            for j in range(n_tiles):
                eng.wait_ge(semVp.h, j + 1)
                if j >= 3:
                    eng.wait_ge(semSC.h, IDMA_INC * (j - 2))
                eng.indirect_dma_start(
                    out=slot_tab[:, :],
                    out_offset=IndirectOffsetOnAxis(slot_u4[:, j % 4:j % 4 + 1], 0),
                    in_=tok_stage[:, 2 * (j % 4):2 * (j % 4) + 2], in_offset=None,
                    bounds_check=bc, oob_is_err=False,
                ).then_inc(semSC.h, 16)
    p.on("gpsimd", g_s)
    semZF.inc(16)
    t_sc = semSC.inc(IDMA_INC * n_tiles)
    return semSC, t_sc


def build_expert_l0_launch(cc=256):
    p = Prog()
    nc = p.nc
    ln_full = p.dram("ln_full", [T_FULL, D], BF16, kind="ExternalInput")
    cnt_in = p.dram("counts_full", [NT_FULL, E], F32, kind="ExternalInput")
    idx_in = p.dram("idx_full", [128, NT_FULL], F32, kind="ExternalInput")
    gate_in = p.dram("gate_full", [128, NT_FULL], F32, kind="ExternalInput")
    wcum_in = p.dram("wcum_full", [128, NT_FULL], F32, kind="ExternalInput")
    w1_in = p.dram("w1", [D, FF], F32, kind="ExternalInput")
    b1_in = p.dram("b1", [1, FF], F32, kind="ExternalInput")
    w2_in = p.dram("w2", [FF, D], F32, kind="ExternalInput")
    b2_in = p.dram("b2", [1, D], F32, kind="ExternalInput")
    ecol_in = p.dram("e_col", [128, 1], F32, kind="ExternalInput")
    y_out = p.dram("y_out", [CAP0, D], F32, kind="ExternalOutput")
    fp_out = p.dram("flatpos_o", [128, NT_FULL], F32, kind="ExternalOutput")
    slot_tab = p.dram("slot_tab", [CAP0, 2], F32)

    cst = Consts(p)
    w1_sb = p.sb("w1_sb", [128, DC * FF], BF16)
    w2_sb = p.sb("w2_sb", [128, FT * D], BF16)
    xeT = p.sb("xeT", [128, DC * CAP0], BF16)
    h_sb = p.sb("h_sb", [128, FT * cc], BF16)
    xe_stg = [p.sb(f"xe{i}", [128, D], BF16) for i in range(2)]
    y_stg = [p.sb(f"ystg{i}", [128, D], F32) for i in range(2)]
    pre = {}
    pre["cnt_sb"] = p.sb("cnt_sb", [128, E], F32)
    pre["cnt_bf"] = p.sb("cnt_bf", [128, E], BF16)
    pre["offsT_bf"] = p.sb("offsT_bf", [8, NT_FULL], F32)
    pre["idx_sb"] = p.sb("idx_sb", [128, NT_FULL], F32)
    pre["gate_sb"] = p.sb("gate_sb", [128, NT_FULL], F32)
    pre["wcum_sb"] = p.sb("wcum_sb", [128, NT_FULL], F32)
    pre["ecol"] = p.sb("ecol", [128, 1], F32)
    pre["pcol"] = p.sb("pcol", [128, 1], F32)
    pre["pcol_oob"] = p.sb("pcol_oob", [128, 1], F32)
    pre["emask"] = p.sb("emask", [128, E], F32)
    pre["postmp"] = p.sb("postmp", [128, E], F32)
    pre["pos_c"] = p.sb("pos_c", [128, 1], F32)
    pre["keep_c"] = p.sb("keep_c", [128, 1], F32)
    pre["slot_f"] = p.sb("slot_f", [128, 1], F32)
    pre["sel_c"] = p.sb("sel_c", [128, 1], F32)
    pre["slot_u4"] = p.sb("slot_u4", [128, 4], U32)
    pre["sel_u"] = p.sb("sel_u", [128, 1], U32)
    pre["keep_u"] = p.sb("keep_u", [128, 1], U32)
    pre["tok_stage"] = p.sb("tok_stage", [128, 8], F32)
    pre["fp_sb"] = p.sb("fp_sb", [128, NT_FULL], F32)
    pre["zero_cap"] = p.sb("zero_cap", [128, 32], F32)
    gidx_u = p.sb("gidx_u", [128, NT_SH], U32)
    sl_sb = p.sb("sl_sb", [128, NT_SH * 2], F32)
    b1col = p.sb("b1col", [128, FT], F32)
    b1row = p.sb("b1row", [32, 128], F32)
    b2_bf = p.sb("b2_bf", [1, D], BF16)
    ones_row_bf = p.sb("ones_row_bf", [1, 128], BF16)

    psm = p.ps("psm", [128, 256], F32)
    ptr = [p.ps(f"ptr{i}", [128, D], BF16) for i in range(2)]
    ph = [p.ps(f"ph{i}", [128, cc], F32) for i in range(2)]
    py = [p.ps(f"py{i}", [128, 512], F32) for i in range(2)]
    pre["ppos"] = [ph[0][:, 0:8], ph[1][:, 0:8]]

    sem_ld = p.sem("ld")
    sem_w = p.sem("w")
    semV = p.sem()
    semP = p.sem()
    sem_offs = p.sem()
    sem_zero = p.sem()
    semSL = p.sem()
    semGs = [p.sem() for _ in range(2)]
    semTT = p.sem()
    semXT = p.sem()
    semH = p.sem()
    semHS = p.sem()
    semY = p.sem()
    semYE = p.sem()
    sem_yd = [p.sem() for _ in range(2)]
    sem_st = p.sem()

    def d_in(eng):
        eng.dma_start(out=pre["cnt_sb"][:], in_=cnt_in[:, :]).then_inc(sem_ld.h, 16)
        eng.dma_start(out=pre["idx_sb"][:], in_=idx_in[:, :]).then_inc(sem_ld.h, 16)
        eng.dma_start(out=pre["gate_sb"][:], in_=gate_in[:, :]).then_inc(sem_ld.h, 16)
        eng.dma_start(out=pre["wcum_sb"][:], in_=wcum_in[:, :]).then_inc(sem_ld.h, 16)
        eng.dma_start(out=pre["ecol"][:], in_=ecol_in[:, :]).then_inc(sem_ld.h, 16)
        eng.dma_start(out=b1row[:], in_=b1_in.ap().rearrange("x (a b) -> (x a) b", b=128)).then_inc(sem_ld.h, 16)
    p.on("sync", d_in)
    t_ld = sem_ld.inc(16 * 6)

    def d_w(eng):
        eng.dma_start(out=w1_sb[:].rearrange("p (c f) -> p c f", f=FF),
                      in_=w1_in.ap().rearrange("(c p) f -> p c f", p=128)
                      ).then_inc(sem_w.h, 16)
        eng.dma_start(out=w2_sb[:].rearrange("p (c d) -> p c d", d=D),
                      in_=w2_in.ap().rearrange("(c p) d -> p c d", p=128)
                      ).then_inc(sem_w.h, 16)
        eng.dma_start(out=b2_bf[:], in_=b2_in[:, :]).then_inc(sem_w.h, 16)
    p.on("gpsimd", d_w)
    t_w = sem_w.inc(16 * 3)

    def g_prep(eng):
        eng.memset(ones_row_bf[:], 1.0)
        eng.drain()
        eng.iota(pre["pcol"][:], pattern=[[0, 1]], base=0, channel_multiplier=1,
                 allow_small_or_imprecise_dtypes=True)
        eng.drain()
        eng.memset(pre["zero_cap"][:], 0.0).then_inc(sem_zero.h, 1)
        eng.drain()
    p.on("gpsimd", g_prep)
    pre["sem_zero"] = sem_zero
    pre["tgt_zero"] = sem_zero.inc()

    def v_prep(eng):
        eng.wait_ge(sem_ld.h, t_ld)
        eng.wait_ge(sem_zero.h, pre["tgt_zero"])
        eng.tensor_copy(pre["cnt_bf"][:], pre["cnt_sb"][:])
        ins = eng.tensor_scalar(pre["pcol_oob"][:], pre["pcol"][:], float(OOBV),
                                None, op0=ALU.add)
        ins.then_inc(semV.h, 1)
        eng.drain()
    p.on("vector", v_prep)
    tVP = semV.inc()

    def t_prep(eng):
        eng.wait_ge(cst.sem.h, cst.tgt)
        eng.wait_ge(semV.h, tVP)
        eng.matmul(psm[0:8, 0:NT_FULL], pre["cnt_bf"][:], cst.tri_bf[:],
                   start=True, stop=True)
        ins = eng.transpose(psm[:, 128:160], b1row[:], cst.ident_f32[0:32, 0:32])
        ins.then_inc(semP.h, 1)
    p.on("tensor", t_prep)
    tP = semP.inc()

    def v_prep2(eng):
        eng.wait_ge(semP.h, tP)
        eng.tensor_copy(pre["offsT_bf"][:], psm[0:8, 0:NT_FULL])
        ins = eng.tensor_copy(b1col[:], psm[:, 128:160])
        ins.then_inc(sem_offs.h, 1)
        eng.drain()
    p.on("vector", v_prep2)
    pre["sem_offs"] = sem_offs
    pre["tgt_offs"] = sem_offs.inc()

    semSC, t_sc = emit_pos_scatter(p, cst, pre, (0, NT_FULL), n_tiles=NT_FULL,
                                   cap=CAP0, slot_tab=slot_tab, tile_base=0)

    def g_gather(eng):
        eng.wait_ge(semSC.h, t_sc)
        eng.dma_start(out=sl_sb[:].rearrange("p (j x) -> p j x", x=2),
                      in_=slot_tab.ap().rearrange("(j p) x -> p j x", p=128)
                      ).then_inc(semSL.h, 16)
        eng.wait_ge(semSL.h, 16)
        eng.wait_ge(semV.h, tVP + 1)
        for j in range(NT_SH):
            if j >= 2:
                eng.wait_ge(semTT.h, j - 1)
            eng.indirect_dma_start(
                out=xe_stg[j % 2][:], out_offset=None,
                in_=ln_full[:, :],
                in_offset=IndirectOffsetOnAxis(gidx_u[:, j:j + 1], 0),
            ).then_inc(semGs[j % 2].h, 16)
    p.on("gpsimd", g_gather)
    semSL.inc(16)
    t_gs = [semGs[s].inc(IDMA_INC * (NT_SH // 2)) for s in range(2)]

    def v_gidx(eng):
        eng.wait_ge(semSL.h, 16)
        ins = eng.tensor_copy(gidx_u[:], sl_sb[:].rearrange("p (j x) -> p j x", x=2)[:, :, 0])
        ins.then_inc(semV.h, 1)
        eng.drain()
    p.on("vector", v_gidx)
    semV.inc()

    def t_tr(eng):
        for j in range(NT_SH):
            eng.wait_ge(semGs[j % 2].h, IDMA_INC * (j // 2 + 1))
            if j >= 2:
                eng.wait_ge(semXT.h, j - 1)
            for dc in range(DC):
                ins = eng.transpose(ptr[j % 2][:, 128 * dc:128 * (dc + 1)],
                                    xe_stg[j % 2][:, 128 * dc:128 * (dc + 1)],
                                    cst.ident_bf[:])
            ins.then_inc(semTT.h, 1)
            semTT.inc()
    p.on("tensor", t_tr)

    def v_xt(eng):
        for j in range(NT_SH):
            eng.wait_ge(semTT.h, j + 1)
            ins = eng.tensor_copy(
                xeT[:].rearrange("p (c t) -> p c t", t=CAP0)[:, :, 128 * j:128 * (j + 1)],
                ptr[j % 2][:].rearrange("p (c t) -> p c t", t=128))
            ins.then_inc(semXT.h, 1)
            eng.drain()
        semXT.inc(NT_SH)
    p.on("vector", v_xt)

    NCC = CAP0 // cc
    CT = cc // 128

    def t_ffn(eng):
        eng.wait_ge(semXT.h, NT_SH)
        eng.wait_ge(sem_w.h, t_w)
        eng.wait_ge(sem_offs.h, 1)
        for c in range(NCC):
            for ft in range(FT):
                if c * FT + ft >= 2:
                    eng.wait_ge(semHS.h, c * FT + ft - 1)
                for dc in range(DC):
                    ins = eng.matmul(
                        ph[ft % 2][:, 0:cc],
                        w1_sb[:, dc * FF + 128 * ft:dc * FF + 128 * (ft + 1)],
                        xeT[:, dc * CAP0 + c * cc:dc * CAP0 + (c + 1) * cc],
                        start=(dc == 0), stop=(dc == DC - 1))
                ins.then_inc(semH.h, 1)
                semH.inc()
            eng.wait_ge(semHS.h, (c + 1) * FT)
            for ct in range(CT):
                if c * CT + ct >= 1:
                    eng.wait_ge(semYE.h, c * CT + ct)
                for dh in range(2):
                    for ft in range(FT):
                        eng.matmul(
                            py[dh][:],
                            h_sb[:, ft * cc + 128 * ct:ft * cc + 128 * (ct + 1)],
                            w2_sb[:, ft * D + 512 * dh:ft * D + 512 * (dh + 1)],
                            start=(ft == 0), stop=False)
                    ins = eng.matmul(py[dh][:], ones_row_bf[0:1, 0:128],
                                     b2_bf[0:1, 512 * dh:512 * (dh + 1)],
                                     start=False, stop=True)
                ins.then_inc(semY.h, 1)
                semY.inc()
    p.on("tensor", t_ffn)

    def s_relu(eng):
        for c in range(NCC):
            for ft in range(FT):
                eng.wait_ge(semH.h, c * FT + ft + 1)
                ins = eng.activation(h_sb[:, ft * cc:(ft + 1) * cc] if False else
                                     h_sb[:, ft * cc + 0:ft * cc + cc],
                                     ph[ft % 2][:, 0:cc], AF.Relu,
                                     bias=b1col[:, ft:ft + 1], scale=1.0)
                ins.then_inc(semHS.h, 1)
                eng.drain()
        semHS.inc(NCC * FT)
    p.on("scalar", s_relu)

    def v_ye(eng):
        for c in range(NCC):
            for ct in range(CT):
                i = c * CT + ct
                eng.wait_ge(semY.h, i + 1)
                if i >= 2:
                    eng.wait_ge(sem_yd[i % 2].h, 16 * (i // 2))
                eng.tensor_copy(y_stg[i % 2][:, 0:512], py[0][:])
                ins = eng.tensor_copy(y_stg[i % 2][:, 512:1024], py[1][:])
                ins.then_inc(semYE.h, 1)
                eng.drain()
        semYE.inc(NCC * CT)
    p.on("vector", v_ye)

    def g_yout(eng):
        for c in range(NCC):
            for ct in range(CT):
                i = c * CT + ct
                eng.wait_ge(semYE.h, i + 1)
                eng.dma_start(out=y_out[128 * i:128 * (i + 1), :],
                              in_=y_stg[i % 2][:]).then_inc(sem_yd[i % 2].h, 16)
        for s in range(2):
            eng.wait_ge(sem_yd[s].h, 16 * (NCC * CT // 2))
        eng.dma_start(out=fp_out[:, :], in_=pre["fp_sb"][:]).then_inc(sem_st.h, 16)
        eng.wait_ge(sem_st.h, 16)
    p.on("gpsimd", g_yout)
    for s in range(2):
        sem_yd[s].inc(16 * (NCC * CT // 2))
    sem_st.inc(16)

    p.finalize()
    return nc


def build_combine_launch(nadd=1):
    p = Prog()
    nc = p.nc
    x_in = p.dram("x_sh", [T_SH, D], F32, kind="ExternalInput")
    ye_in = p.dram("ye_all", [N_CORES * (CAP0 if nadd == 1 else CAP1), D], F32,
                   kind="ExternalInput")
    fps = [p.dram(f"fp{i}", [128, NT_SH], F32, kind="ExternalInput")
           for i in range(nadd)]
    gs = [p.dram(f"g{i}", [128, NT_SH], F32, kind="ExternalInput")
          for i in range(nadd)]
    xo = p.dram("xo", [T_SH, D], F32, kind="ExternalOutput")
    nrows = ye_in.shape[0]

    x_sb = p.sb("x_sb", [128, NT_SH * D], F32)
    yg = [p.sb(f"yg{i}", [128, D], F32) for i in range(2)]
    fp_sb = [p.sb(f"fp_sb{i}", [128, NT_SH], F32) for i in range(nadd)]
    fp_u = [p.sb(f"fp_u{i}", [128, NT_SH], U32) for i in range(nadd)]
    g_sb = [p.sb(f"g_sb{i}", [128, NT_SH], F32) for i in range(nadd)]
    hg = [p.sb(f"hg{i}", [128, NT_SH], F32) for i in range(nadd)]

    sem_ld = p.sem("ld")
    semVr = p.sem()
    semZ = p.sem()
    semGd = [p.sem() for _ in range(2)]
    semAcc = p.sem()
    sem_st = p.sem("st")

    pairs = [(x_sb[:].rearrange("p (j d) -> p j d", d=D),
              x_in.ap().rearrange("(j p) d -> p j d", p=128))]
    for i in range(nadd):
        pairs.append((fp_sb[i][:], fps[i][:, :]))
        pairs.append((g_sb[i][:], gs[i][:, :]))
    t_ld = dma_in(p, "sync", pairs, sem_ld)

    def v_prep(eng):
        eng.wait_ge(sem_ld.h, t_ld)
        for i in range(nadd):
            eng.tensor_copy(fp_u[i][:], fp_sb[i][:])
            eng.tensor_scalar(hg[i][:], g_sb[i][:], 0.5, None, op0=ALU.mult)
        eng.drain()
        ins = eng.tensor_copy(yg[0][:, 0:1], x_sb[:, 0:1])
        ins.then_inc(semVr.h, 1)
        eng.drain()
    p.on("vector", v_prep)
    tVr = semVr.inc()

    NG = NT_SH * nadd

    def g_go(eng):
        eng.wait_ge(semVr.h, tVr)
        with eng.register("bc_c") as bc:
            eng.reg_mov(bc, nrows - 1)
            for u in range(NG):
                j, i = u // nadd, u % nadd
                if u >= 2:
                    eng.wait_ge(semAcc.h, u - 1)
                eng.memset(yg[u % 2][:], 0.0)
                eng.drain()
                ins = eng.indirect_dma_start(
                    out=yg[u % 2][:], out_offset=None,
                    in_=ye_in[:, :],
                    in_offset=IndirectOffsetOnAxis(fp_u[i][:, j:j + 1], 0),
                    bounds_check=bc, oob_is_err=False)
                ins.then_inc(semGd[u % 2].h, 16)
    p.on("gpsimd", g_go)
    for s in range(2):
        semGd[s].inc(IDMA_INC * ((NG + 1 - s) // 2))

    def v_acc(eng):
        for u in range(NG):
            j, i = u // nadd, u % nadd
            eng.wait_ge(semGd[u % 2].h, IDMA_INC * (u // 2 + 1))
            xt = x_sb[:, D * j:D * (j + 1)]
            ins = eng.scalar_tensor_tensor(
                xt, yg[u % 2][:], hg[i][:, j:j + 1], xt,
                op0=ALU.mult, op1=ALU.add)
            ins.then_inc(semAcc.h, 1)
            eng.drain()
        semAcc.inc(NG)
    p.on("vector", v_acc)

    def g_out(eng):
        eng.wait_ge(semAcc.h, NG)
        eng.dma_start(out=xo.ap().rearrange("(j p) d -> p j d", p=128),
                      in_=x_sb[:].rearrange("p (j d) -> p j d", d=D)
                      ).then_inc(sem_st.h, 16)
        eng.wait_ge(sem_st.h, 16)
    p.on("gpsimd", g_out)
    sem_st.inc(16)

    p.finalize()
    return nc


def emit_route_l1(p, cst, ln_view, rw_sb, n_tiles=NT_SH, psb=None, rpfx="q"):
    if psb is None:
        ptr = [p.ps(f"q_ptr{i}", [128, 1024], F32) for i in range(2)]
        plog = p.ps("q_plog", [128, n_tiles * 8], F32)
    else:
        ptr = [psb["F1024"][:, 0:1024], psb["F1024"][:, 0:1024]]
        plog = psb["F0"][:, 0:n_tiles * 8]
    stage = [p.sb(rpfx + f"_stage{i}", [128, 1024], F32) for i in range(2)]
    logit_sb = p.sb(rpfx + "_logit", [128, n_tiles * 8], F32)
    vm8 = p.sb(rpfx + "_vm8", [128, 8], F32)
    idx8 = p.sb(rpfx + "_idx8", [128, 8], U32)
    negm = p.sb(rpfx + "_negm", [128, n_tiles], F32)
    l2_all = p.sb(rpfx + "_l2", [128, n_tiles], F32)
    e2_all = p.sb(rpfx + "_e2", [128, n_tiles], F32)
    exp_all = p.sb(rpfx + "_exp", [128, n_tiles * 8], F32)
    s_all = p.sb(rpfx + "_s", [128, n_tiles], F32)
    p1_all = p.sb(rpfx + "_p1", [128, n_tiles], F32)
    p2_all = p.sb(rpfx + "_p2", [128, n_tiles], F32)
    g1_all = p.sb(rpfx + "_g1", [128, n_tiles], F32)
    g2_all = p.sb(rpfx + "_g2", [128, n_tiles], F32)
    i1f = p.sb(rpfx + "_i1f", [128, n_tiles], F32)
    i2f = p.sb(rpfx + "_i2f", [128, n_tiles], F32)
    iu = p.sb(rpfx + "_iu", [128, 2 * n_tiles], U32)
    gates = p.sb(rpfx + "_gates", [128, n_tiles * 8], F32)
    gtmp = p.sb(rpfx + "_gtmp", [128, 8], F32)

    semT = p.sem(); semV = p.sem(); semT2 = p.sem()
    semV2 = p.sem(); semS = p.sem(); sem_done = p.sem()

    single = psb is not None

    def t1(eng):
        eng.wait_ge(cst.sem.h, cst.tgt)
        for j in range(n_tiles):
            if j >= (1 if single else 2):
                eng.wait_ge(semV.h, j - (0 if single else 1))
            for dc in range(DC):
                ins = eng.transpose(ptr[j % 2][:, 128 * dc:128 * (dc + 1)],
                                    ln_view(j)[:, 128 * dc:128 * (dc + 1)],
                                    cst.ident_f32[:])
            ins.then_inc(semT.h, 1)
            semT.inc()
            if j >= 1:
                eng.wait_ge(semV.h, j)
                for dc in range(DC):
                    eng.matmul(plog[:, 8 * (j - 1):8 * j],
                               stage[(j - 1) % 2][:, 128 * dc:128 * (dc + 1)],
                               rw_sb[:, 8 * dc:8 * (dc + 1)],
                               start=(dc == 0), stop=(dc == DC - 1))
        eng.wait_ge(semV.h, n_tiles)
        for dc in range(DC):
            ins = eng.matmul(plog[:, 8 * (n_tiles - 1):8 * n_tiles],
                             stage[(n_tiles - 1) % 2][:, 128 * dc:128 * (dc + 1)],
                             rw_sb[:, 8 * dc:8 * (dc + 1)],
                             start=(dc == 0), stop=(dc == DC - 1))
        ins.then_inc(semT2.h, 1)
    p.on("tensor", t1)
    tT2 = semT2.inc()

    def v1(eng):
        for j in range(n_tiles):
            eng.wait_ge(semT.h, j + 1)
            eng.tensor_copy(stage[j % 2][:], ptr[j % 2][:]).then_inc(semV.h, 1)
            eng.drain()
        semV.inc(n_tiles)
        eng.wait_ge(semT2.h, tT2)
        eng.tensor_copy(logit_sb[:], plog[:])
        eng.drain()
        for j in range(n_tiles):
            sl = logit_sb[:, 8 * j:8 * (j + 1)]
            eng.max(vm8[:], sl)
            eng.drain()
            eng.max_index(idx8[:], vm8[:], sl)
            eng.drain()
            eng.tensor_scalar(negm[:, j:j + 1], vm8[:, 0:1], -1.0, None,
                              op0=ALU.mult)
            eng.tensor_copy(l2_all[:, j:j + 1], vm8[:, 1:2])
            eng.tensor_copy(iu[:, 2 * j:2 * j + 2], idx8[:, 0:2])
            eng.drain()
        eng.tensor_copy(i1f[:], iu[:, 0::2])
        ins = eng.tensor_copy(i2f[:], iu[:, 1::2])
        ins.then_inc(semV2.h, 1)
        eng.drain()
    p.on("vector", v1)
    tV2 = semV2.inc()

    def s1(eng):
        eng.wait_ge(semV2.h, tV2)
        for j in range(n_tiles):
            eng.activation(exp_all[:, 8 * j:8 * (j + 1)],
                           logit_sb[:, 8 * j:8 * (j + 1)], AF.Exp,
                           bias=negm[:, j:j + 1], scale=1.0,
                           accum_out=s_all[:, j:j + 1])
        eng.drain()
        for j in range(n_tiles):
            ins = eng.activation(e2_all[:, j:j + 1], l2_all[:, j:j + 1], AF.Exp,
                                 bias=negm[:, j:j + 1], scale=1.0)
        ins.then_inc(semS.h, 1)
        eng.drain()
    p.on("scalar", s1)
    tS = semS.inc()

    def v2(eng):
        eng.wait_ge(semS.h, tS)
        eng.reciprocal(p1_all[:], s_all[:])
        eng.drain()
        eng.tensor_tensor(p2_all[:], e2_all[:], p1_all[:], op=ALU.mult)
        eng.drain()
        eng.tensor_tensor(g1_all[:], p1_all[:], p2_all[:], op=ALU.add)
        eng.drain()
        eng.tensor_scalar(g1_all[:], g1_all[:], 1e-9, None, op0=ALU.add)
        eng.drain()
        eng.reciprocal(g2_all[:], g1_all[:])
        eng.drain()
        eng.tensor_tensor(g1_all[:], p1_all[:], g2_all[:], op=ALU.mult)
        eng.tensor_tensor(g2_all[:], p2_all[:], g2_all[:], op=ALU.mult)
        eng.drain()
        for j in range(n_tiles):
            eng.tensor_scalar(gates[:, 8 * j:8 * (j + 1)], cst.iota8[:],
                              i1f[:, j:j + 1], g1_all[:, j:j + 1],
                              op0=ALU.is_equal, op1=ALU.mult)
            eng.drain()
            eng.tensor_scalar(gtmp[:], cst.iota8[:], i2f[:, j:j + 1],
                              g2_all[:, j:j + 1], op0=ALU.is_equal, op1=ALU.mult)
            eng.drain()
            ins = eng.tensor_tensor(gates[:, 8 * j:8 * (j + 1)],
                                    gates[:, 8 * j:8 * (j + 1)], gtmp[:],
                                    op=ALU.add)
            eng.drain()
        ins.then_inc(sem_done.h, 1)
    p.on("vector", v2)
    t_done = sem_done.inc()
    return dict(gates=gates, g1=g1_all, g2=g2_all, i1f=i1f, i2f=i2f,
                sem_done=sem_done, tgt_done=t_done)


def build_route_l1_launch():
    p = Prog()
    nc = p.nc
    x_in = p.dram("x_sh", [T_SH, D], F32, kind="ExternalInput")
    rw_in = p.dram("router_gate", [D, E], F32, kind="ExternalInput")
    ln_out = p.dram("ln_bf", [T_SH, D], BF16, kind="ExternalOutput")
    gates_out = p.dram("gates_o", [128, NT_SH * 8], F32, kind="ExternalOutput")
    g1_out = p.dram("g1_o", [128, NT_SH], F32, kind="ExternalOutput")
    g2_out = p.dram("g2_o", [128, NT_SH], F32, kind="ExternalOutput")
    i1_out = p.dram("i1_o", [128, NT_SH], F32, kind="ExternalOutput")
    i2_out = p.dram("i2_o", [128, NT_SH], F32, kind="ExternalOutput")

    cst = Consts(p)
    x_sb = p.sb("x_sb", [128, NT_SH * D], F32)
    ln_sb = p.sb("ln_sb", [128, NT_SH * D], F32)
    rw_sb = p.sb("rw_sb", [128, DC * E], F32)
    bn6 = p.sb("bn6", [128, 12], F32)
    stats = p.sb("stats", [128, 2 * NT_SH], F32)
    rstd = p.sb("rstd", [128, NT_SH], F32)

    sem_ld = p.sem("ld")
    sem_v2s = p.sem(); sem_s2v = p.sem(); sem_lnd = p.sem()
    sem_st = p.sem("st")

    t_ld = dma_in(p, "sync", [
        (x_sb[:].rearrange("p (j d) -> p j d", d=D),
         x_in.ap().rearrange("(j p) d -> p j d", p=128)),
        (rw_sb[:].rearrange("p (c e) -> p c e", e=E),
         rw_in.ap().rearrange("(c p) e -> p c e", p=128))], sem_ld)

    def xv(j):
        return x_sb[:, D * j:D * (j + 1)]

    def lv(j):
        return ln_sb[:, D * j:D * (j + 1)]

    t_ln = emit_ln_sweep(p, NT_SH, xv, lv, (sem_v2s, sem_s2v, sem_lnd),
                         (bn6, stats, rstd), wait_v=(sem_ld, t_ld))

    def tw(eng):
        eng.wait_ge(sem_lnd.h, t_ln)
    p.on("tensor", tw)

    r = emit_route_l1(p, cst, lv, rw_sb, n_tiles=NT_SH)

    def g_out(eng):
        eng.wait_ge(sem_lnd.h, t_ln)
        eng.dma_start(out=ln_out.ap().rearrange("(j p) d -> p j d", p=128),
                      in_=ln_sb[:].rearrange("p (j d) -> p j d", d=D)
                      ).then_inc(sem_st.h, 16)
        eng.wait_ge(r["sem_done"].h, r["tgt_done"])
        eng.dma_start(out=gates_out[:, :], in_=r["gates"][:]).then_inc(sem_st.h, 16)
        eng.dma_start(out=g1_out[:, :], in_=r["g1"][:]).then_inc(sem_st.h, 16)
        eng.dma_start(out=g2_out[:, :], in_=r["g2"][:]).then_inc(sem_st.h, 16)
        eng.dma_start(out=i1_out[:, :], in_=r["i1f"][:]).then_inc(sem_st.h, 16)
        eng.dma_start(out=i2_out[:, :], in_=r["i2f"][:]).then_inc(sem_st.h, 16)
        eng.wait_ge(sem_st.h, 16 * 6)
    p.on("gpsimd", g_out)
    sem_st.inc(16 * 6)

    p.finalize()
    return nc


def build_expert_l1_launch(cc=256, niter=28):
    p = Prog()
    nc = p.nc
    ln_full = p.dram("ln_full", [T_FULL, D], BF16, kind="ExternalInput")
    gates_in = p.dram("gates_full", [128, NT_FULL * 8], F32, kind="ExternalInput")
    i1_in = p.dram("i1_full", [128, NT_FULL], F32, kind="ExternalInput")
    i2_in = p.dram("i2_full", [128, NT_FULL], F32, kind="ExternalInput")
    gw_in = p.dram("gate_w", [D, FF], F32, kind="ExternalInput")
    uw_in = p.dram("up_w", [D, FF], F32, kind="ExternalInput")
    dw_in = p.dram("down_w", [FF, D], F32, kind="ExternalInput")
    ecol_in = p.dram("e_col", [128, 1], F32, kind="ExternalInput")
    y_out = p.dram("y_out", [CAP1, D], F32, kind="ExternalOutput")
    fp1_out = p.dram("fp1_o", [128, NT_FULL], F32, kind="ExternalOutput")
    fp2_out = p.dram("fp2_o", [128, NT_FULL], F32, kind="ExternalOutput")
    slot_tab = p.dram("slot_tab", [CAP1, 2], F32)
    cnt_rt = p.dram("cnt_rt", [NT_FULL, E], F32)
    xeT_dram = p.dram("xeT_d", [DC * 128, CAP1], BF16)
    h_dram = p.dram("h_d", [FF, CAP1], BF16)

    cst = Consts(p)
    NCC = CAP1 // cc
    CT = CAP1 // 128
    gw_sb = p.sb("gw_sb", [128, DC * FF], BF16)
    uw_sb = p.sb("uw_sb", [128, DC * FF], BF16)
    gates_sb = p.sb("gates_sb", [128, NT_FULL * 8], F32)
    i1_sb = p.sb("i1_sb", [128, NT_FULL], F32)
    i2_sb = p.sb("i2_sb", [128, NT_FULL], F32)
    ecol = p.sb("ecol", [128, 1], F32)
    pcol = p.sb("pcol", [128, 1], F32)
    pcol_oob = p.sb("pcol_oob", [128, 1], F32)
    ones_row = p.sb("ones_row", [1, 128], BF16)
    ones_row_f = p.sb("ones_row_f", [1, 128], F32)
    lohi = p.sb("lohi", [1, 24], F32)
    bdbg = p.sb("bdbg", [1, 3 * 28], F32)
    lohi2 = p.sb("lohi2", [1, 16], F32)
    thbc = p.sb("thbc", [128, 8], F32)
    cmp_s = p.sb("cmp_s", [128, 128], F32)
    cntp = p.sb("cntp", [128, 8], F32)
    cnt_row = p.sb("cnt_row", [1, 8], F32)
    over_u = p.sb("over_u", [1, 8], U32)
    keep_t = p.sb("keep_t", [128, 8], F32)
    keep_bf = p.sb("keep_bf", [128, 8], BF16)
    cnt_tile = p.sb("cnt_tile", [1, NT_FULL * 8], F32)
    cnt_pm = p.sb("cnt_pm", [128, 8], F32)
    cnt_pm_bf = p.sb("cnt_pm_bf", [128, 8], BF16)
    offsT = p.sb("offsT", [8, NT_FULL], F32)
    wc_sb = p.sb("wc_sb", [128, 8], F32)
    pos_all = p.sb("pos_all", [128, 8], F32)
    emask = p.sb("emask", [128, 8], F32)
    postmp = p.sb("postmp", [128, 8], F32)
    pos_c = p.sb("pos_c", [128, 1], F32)
    keep_c = p.sb("keep_c", [128, 1], F32)
    sel_c = p.sb("sel_c", [128, 1], F32)
    sel_u = p.sb("sel_u", [128, 1], U32)
    keep_u = p.sb("keep_u", [128, 1], U32)
    slot_f = p.sb("slot_f", [128, 1], F32)
    slot_u4 = p.sb("slot_u4", [128, 8], U32)
    tok_stage = p.sb("tok_stage", [128, 16], F32)
    fp_sb = [p.sb(f"fp_sb{i}", [128, NT_FULL], F32) for i in range(2)]
    zero_cap = p.sb("zero_cap", [128, (CAP1 * 2 + 127) // 128], F32)
    gidx_u = p.sb("gidx_u", [128, CT], U32)
    sl_sb = p.sb("sl_sb", [128, CT * 2], F32)
    xc_stg = [p.sb(f"xc{i}", [128, DC * cc], BF16) for i in range(2)]
    s1_stg = p.sb("s1_stg", [128, 2 * cc], BF16)
    hcc_stg = [p.sb(f"hcc{i}", [128, (FT // 2) * cc], BF16) for i in range(2)]
    hcol_stg = [p.sb(f"hc{i}", [128, FT * 128], BF16) for i in range(2)]
    xe_stg = [hcol_stg[i][:, 0:D] for i in range(2)]
    xt_stg = [hcol_stg[i][:, D:2 * D] for i in range(2)]
    y_stg = [p.sb(f"ystg{i}", [128, D], F32) for i in range(2)]
    psm = p.ps("psm", [128, 256], F32)
    ptr = [p.ps(f"ptr{i}", [128, D], BF16) for i in range(2)]
    ph1 = p.ps("ph1", [128, cc], F32)
    ph2 = p.ps("ph2", [128, cc], F32)
    py = [p.ps(f"py{i}", [128, 512], F32) for i in range(2)]
    ppos = [ph1[:, 0:8], ph2[:, 0:8]]

    sem_ld = p.sem("ld")
    sem_w = p.sem("w")
    semV = p.sem()
    semT = p.sem()
    semB = p.sem()
    semVp = p.sem()
    semZP = p.sem()
    semZF = p.sem()
    semSC = p.sem()
    semSL = p.sem()
    semGs = [p.sem() for _ in range(2)]
    semTT = p.sem()
    semXTd = [p.sem() for _ in range(2)]
    semXC = [p.sem() for _ in range(2)]
    semH = p.sem()
    semHEs = p.sem()
    semHEv = p.sem()
    semHD = [p.sem() for _ in range(2)]
    semDW = p.sem()
    semHC = [p.sem() for _ in range(2)]
    semY = p.sem()
    semYE = p.sem()
    sem_yd = [p.sem() for _ in range(2)]
    sem_st = p.sem()

    t_ld = dma_in(p, "sync", [
        (gates_sb[:], gates_in[:, :]),
        (i1_sb[:], i1_in[:, :]),
        (i2_sb[:], i2_in[:, :]),
        (ecol[:], ecol_in[:, :]),
    ], sem_ld)

    def d_w(eng):
        eng.dma_start(out=gw_sb[:].rearrange("p (c f) -> p c f", f=FF),
                      in_=gw_in.ap().rearrange("(c p) f -> p c f", p=128)
                      ).then_inc(sem_w.h, 16)
        eng.dma_start(out=uw_sb[:].rearrange("p (c f) -> p c f", f=FF),
                      in_=uw_in.ap().rearrange("(c p) f -> p c f", p=128)
                      ).then_inc(sem_w.h, 16)
    p.on("gpsimd", d_w)
    t_w = sem_w.inc(32)

    def g_prep(eng):
        eng.memset(ones_row[:], 1.0)
        eng.memset(ones_row_f[:], 1.0)
        eng.drain()
        eng.iota(pcol[:], pattern=[[0, 1]], base=0, channel_multiplier=1,
                 allow_small_or_imprecise_dtypes=True)
        eng.drain()
        eng.memset(zero_cap[:], 0.0)
        eng.drain()
        eng.memset(lohi[:, 0:8], 0.0)
        eng.drain()
        ins = eng.memset(lohi[:, 8:16], 1.0)
        ins.then_inc(semZP.h, 1)
        eng.drain()
    p.on("gpsimd", g_prep)
    tZP = semZP.inc()

    def v_bis(eng):
        eng.wait_ge(sem_ld.h, t_ld)
        eng.wait_ge(semZP.h, tZP)
        eng.tensor_scalar(pcol_oob[:], pcol[:], float(OOBV), None, op0=ALU.add)
        eng.drain()
        for it in range(niter):
            eng.tensor_tensor(lohi[:, 16:24], lohi[:, 0:8], lohi[:, 8:16],
                              op=ALU.add)
            eng.drain()
            eng.tensor_scalar(lohi[:, 16:24], lohi[:, 16:24], 0.5, None,
                              op0=ALU.mult).then_inc(semV.h, 1)
            eng.drain()
            eng.wait_ge(semT.h, 2 * it + 1)
            eng.tensor_copy(thbc[:], psm[:, 0:8])
            eng.drain()
            for e in range(E):
                ins = eng.tensor_scalar(
                    cmp_s[:], gates_sb[:].rearrange("p (j e) -> p e j", e=8)[:, e],
                    thbc[:, e:e + 1], None, op0=ALU.is_gt, op1=ALU.add,
                    accum_out=cntp[:, e:e + 1])
                eng.drain()
            ins.then_inc(semV.h, 1)
            eng.wait_ge(semT.h, 2 * it + 2)
            eng.tensor_copy(cnt_row[:], psm[0:1, 8:16])
            eng.drain()
            eng.tensor_scalar(over_u[:], cnt_row[:], float(CAP1), None,
                              op0=ALU.is_gt)
            eng.drain()
            eng.tensor_copy(bdbg[:, 3 * it:3 * it + 1], lohi[:, 16:17])
            eng.tensor_copy(bdbg[:, 3 * it + 1:3 * it + 2], cnt_row[:, 0:1])
            eng.drain()
            eng.select(lohi2[:, 0:8], over_u[:], lohi[:, 16:24], lohi[:, 0:8],
                       add_drain=True)
            eng.drain()
            eng.select(lohi2[:, 8:16], over_u[:], lohi[:, 8:16], lohi[:, 16:24],
                       add_drain=True)
            eng.drain()
            eng.tensor_copy(lohi[:, 0:16], lohi2[:])
            eng.drain()
        eng.tensor_copy(lohi[:, 16:24], lohi[:, 8:16]).then_inc(semV.h, 1)
        eng.drain()
        eng.wait_ge(semT.h, 2 * niter + 1)
        ins = eng.tensor_copy(thbc[:], psm[:, 0:8])
        ins.then_inc(semV.h, 1)
        eng.drain()
    p.on("vector", v_bis)
    semV.inc(2 * niter + 2)

    def t_bis(eng):
        eng.wait_ge(cst.sem.h, cst.tgt)
        for it in range(niter):
            eng.wait_ge(semV.h, 2 * it + 1)
            eng.matmul(psm[:, 0:8], ones_row_f[0:1, 0:128], lohi[0:1, 16:24],
                       start=True, stop=True).then_inc(semT.h, 1)
            eng.wait_ge(semV.h, 2 * it + 2)
            eng.matmul(psm[0:1, 8:16], cst.ones_f32[:], cntp[:],
                       start=True, stop=True).then_inc(semT.h, 1)
        eng.wait_ge(semV.h, 2 * niter + 1)
        eng.matmul(psm[:, 0:8], ones_row_f[0:1, 0:128], lohi[0:1, 16:24],
                   start=True, stop=True).then_inc(semT.h, 1)
        semT.inc(2 * niter + 1)
    p.on("tensor", t_bis)

    semKT = p.sem()
    semKC = p.sem()
    semKD = p.sem()
    keep_all = p.sb("keep_all_bf", [128, NT_FULL * 8], BF16)

    def v_keep(eng):
        eng.wait_ge(semV.h, 2 * niter + 2)
        for j in range(NT_FULL):
            eng.tensor_tensor(keep_t[:], gates_sb[:, 8 * j:8 * (j + 1)],
                              thbc[:], op=ALU.is_gt)
            eng.drain()
            ins = eng.tensor_copy(keep_all[:, 8 * j:8 * (j + 1)], keep_t[:])
            ins.then_inc(semKT.h, 1)
            eng.drain()
        semKT.inc(NT_FULL)
    p.on("vector", v_keep)

    def t_keep(eng):
        for g in range(NT_FULL // 8):
            if g >= 1:
                eng.wait_ge(semKC.h, 2 * g)
            for jj in range(8):
                j = 8 * g + jj
                eng.wait_ge(semKT.h, j + 1)
                ins = eng.matmul(psm[0:1, 32 + 8 * jj:40 + 8 * jj],
                                 cst.ones_bf[:], keep_all[:, 8 * j:8 * (j + 1)],
                                 start=True, stop=True)
            ins.then_inc(semKC.h, 1)
            semKC.inc()
    p.on("tensor", t_keep)

    def v_keep2(eng):
        for g in range(NT_FULL // 8):
            eng.wait_ge(semKC.h, 2 * g + 1)
            ins = eng.tensor_copy(cnt_tile[0:1, 64 * g:64 * (g + 1)],
                                  psm[0:1, 32:96])
            ins.then_inc(semKC.h, 1)
            semKC.inc()
            eng.drain()
    p.on("vector", v_keep2)

    def g_keep(eng):
        eng.wait_ge(semKC.h, 2 * (NT_FULL // 8))
        eng.dma_start(out=cnt_rt.ap().rearrange("(o j) e -> o (j e)", o=1),
                      in_=cnt_tile[:]).then_inc(semKD.h, 16)
        eng.wait_ge(semKD.h, 16)
        eng.dma_start(out=cnt_pm[:], in_=cnt_rt[:, :]).then_inc(semKD.h, 16)
        eng.wait_ge(semKD.h, 32)
    p.on("gpsimd", g_keep)
    semKD.inc(32)

    semOF = p.sem()

    def v_off(eng):
        eng.wait_ge(semKD.h, 32)
        ins = eng.tensor_copy(cnt_pm_bf[:], cnt_pm[:])
        ins.then_inc(semOF.h, 1)
        eng.drain()
    p.on("vector", v_off)
    tOF1 = semOF.inc()

    def t_off(eng):
        eng.wait_ge(semOF.h, tOF1)
        eng.matmul(psm[0:8, 128:256], cnt_pm_bf[:], cst.tri_bf[:],
                   start=True, stop=True).then_inc(semOF.h, 1)
    p.on("tensor", t_off)
    tOF2 = semOF.inc()

    def v_off2(eng):
        eng.wait_ge(semOF.h, tOF2)
        ins = eng.tensor_copy(offsT[:], psm[0:8, 128:256])
        ins.then_inc(semOF.h, 1)
        eng.drain()
    p.on("vector", v_off2)
    tOF3 = semOF.inc()

    wcum_ps = [ph1[:, 8:16], ph2[:, 8:16]]

    def t_pos(eng):
        eng.wait_ge(semOF.h, tOF3)
        for j in range(NT_FULL):
            if j >= 2:
                eng.wait_ge(semVp.h, 2 * (j - 1))
            col = offsT[:, j:j + 1]
            col_b = bass.AP(col.tensor, col.offset, [[col.ap[0][0], 8], [0, 128]])
            eng.matmul(ppos[j % 2][:], col_b, cst.ident_f32[0:8, 0:8],
                       start=True, stop=True)
            eng.matmul(wcum_ps[j % 2][:], cst.tri_bf[:],
                       keep_all[:, 8 * j:8 * (j + 1)],
                       start=True, stop=True).then_inc(semB.h, 1)
            semB.inc()
    p.on("tensor", t_pos)

    def v_pos(eng):
        for j in range(NT_FULL):
            eng.wait_ge(semB.h, j + 1)
            if j >= 4:
                eng.wait_ge(semSC.h, 16 * (2 * j - 6))
            eng.tensor_copy(wc_sb[:], wcum_ps[j % 2][:])
            eng.drain()
            eng.tensor_tensor(pos_all[:], ppos[j % 2][:], wc_sb[:], op=ALU.add)
            eng.drain()
            for s in range(2):
                isf = (i1_sb if s == 0 else i2_sb)[:, j:j + 1]
                eng.tensor_scalar(emask[:], cst.iota8[:], isf, None,
                                  op0=ALU.is_equal)
                eng.drain()
                eng.tensor_tensor(postmp[:], pos_all[:], emask[:], op=ALU.mult)
                eng.drain()
                eng.reduce_sum(pos_c[:], postmp[:], axis=AX.X)
                eng.drain()
                eng.tensor_tensor(keep_t[:], gates_sb[:, 8 * j:8 * (j + 1)],
                                  thbc[:], op=ALU.is_gt)
                eng.drain()
                eng.tensor_tensor(postmp[:], keep_t[:], emask[:], op=ALU.mult)
                eng.drain()
                eng.reduce_sum(keep_c[:], postmp[:], axis=AX.X)
                eng.drain()
                eng.tensor_scalar(sel_c[:], isf, ecol[:], None, op0=ALU.is_equal)
                eng.drain()
                eng.tensor_tensor(sel_c[:], sel_c[:], keep_c[:], op=ALU.mult)
                eng.drain()
                eng.tensor_copy(sel_u[:], sel_c[:])
                eng.drain()
                eng.select(slot_f[:], sel_u[:], pos_c[:], pcol_oob[:],
                           add_drain=True)
                eng.drain()
                eng.tensor_copy(slot_u4[:, (2 * j + s) % 8:(2 * j + s) % 8 + 1],
                                slot_f[:])
                eng.tensor_scalar(tok_stage[:, 2 * ((2 * j + s) % 8):2 * ((2 * j + s) % 8) + 1],
                                  pcol[:], float(128 * j), None, op0=ALU.add)
                eng.tensor_copy(tok_stage[:, 2 * ((2 * j + s) % 8) + 1:2 * ((2 * j + s) % 8) + 2],
                                slot_f[:])
                eng.drain()
                eng.tensor_scalar(postmp[:, 0:1], isf, float(CAP1), pos_c[:],
                                  op0=ALU.mult, op1=ALU.add)
                eng.tensor_copy(keep_u[:], keep_c[:])
                eng.drain()
                ins = eng.select(fp_sb[s][:, j:j + 1], keep_u[:],
                                 postmp[:, 0:1], pcol_oob[:], add_drain=True)
                ins.then_inc(semVp.h, 1)
                semVp.inc(1)
                eng.drain()
    p.on("vector", v_pos)

    def g_scat(eng):
        eng.wait_ge(semZP.h, tZP)
        eng.dma_start(out=slot_tab.ap().rearrange("(j p) x -> p j x", p=128),
                      in_=zero_cap[:].rearrange("p (j x) -> p j x", x=2)[:, 0:CAP1 // 128, :]
                      ).then_inc(semZF.h, 16)
        eng.wait_ge(semZF.h, 16)
        with eng.register("bc_s1") as bc:
            eng.reg_mov(bc, CAP1 - 1)
            for u in range(2 * NT_FULL):
                eng.wait_ge(semVp.h, u + 1)
                if u >= 7:
                    eng.wait_ge(semSC.h, IDMA_INC * (u - 6))
                eng.indirect_dma_start(
                    out=slot_tab[:, :],
                    out_offset=IndirectOffsetOnAxis(slot_u4[:, u % 8:u % 8 + 1], 0),
                    in_=tok_stage[:, 2 * (u % 8):2 * (u % 8) + 2], in_offset=None,
                    bounds_check=bc, oob_is_err=False,
                ).then_inc(semSC.h, 16)
    p.on("gpsimd", g_scat)
    semZF.inc(16)
    t_sc = semSC.inc(IDMA_INC * 2 * NT_FULL)

    def g_gath(eng):
        eng.wait_ge(semSC.h, t_sc)
        eng.dma_start(out=sl_sb[:].rearrange("p (j x) -> p j x", x=2),
                      in_=slot_tab.ap().rearrange("(j p) x -> p j x", p=128)
                      ).then_inc(semSL.h, 16)
        eng.wait_ge(semSL.h, 16)
        eng.wait_ge(semOF.h, tOF3 + 1)
        for j in range(CT + 2):
            if j < CT:
                if j >= 2:
                    eng.wait_ge(semTT.h, j - 1)
                eng.indirect_dma_start(
                    out=xe_stg[j % 2], out_offset=None,
                    in_=ln_full[:, :],
                    in_offset=IndirectOffsetOnAxis(gidx_u[:, j:j + 1], 0),
                ).then_inc(semGs[j % 2].h, 16)
            if j >= 2:
                jo = j - 2
                eng.wait_ge(semXC[jo % 2].h, jo // 2 + 1)
                eng.dma_start(
                    out=xeT_dram.ap().rearrange("(c p) t -> p c t", p=128)[:, :, 128 * jo:128 * (jo + 1)],
                    in_=xt_stg[jo % 2].rearrange("p (c t) -> p c t", t=128)
                    ).then_inc(semXTd[jo % 2].h, 16)
    p.on("gpsimd", g_gath)
    semSL.inc(16)
    for s in range(2):
        semGs[s].inc(IDMA_INC * ((CT + 1 - s) // 2))
        semXTd[s].inc(16 * ((CT + 1 - s) // 2))

    def v_gidx2(eng):
        eng.wait_ge(semSL.h, 16)
        ins = eng.tensor_copy(gidx_u[:], sl_sb[:].rearrange("p (j x) -> p j x", x=2)[:, :, 0])
        ins.then_inc(semOF.h, 1)
        eng.drain()
    p.on("vector", v_gidx2)
    semOF.inc()

    def t_tr2(eng):
        for j in range(CT):
            eng.wait_ge(semGs[j % 2].h, IDMA_INC * (j // 2 + 1))
            if j >= 2:
                eng.wait_ge(semXC[j % 2].h, j // 2)
            for dc in range(DC):
                ins = eng.transpose(ptr[j % 2][:, 128 * dc:128 * (dc + 1)],
                                    xe_stg[j % 2][:, 128 * dc:128 * (dc + 1)],
                                    cst.ident_bf[:])
            ins.then_inc(semTT.h, 1)
            semTT.inc()
    p.on("tensor", t_tr2)

    def v_xt2(eng):
        for j in range(CT):
            eng.wait_ge(semTT.h, j + 1)
            if j >= 2:
                eng.wait_ge(semXTd[j % 2].h, 16 * (j // 2))
            ins = eng.tensor_copy(xt_stg[j % 2], ptr[j % 2][:])
            ins.then_inc(semXC[j % 2].h, 1)
            eng.drain()
    p.on("vector", v_xt2)
    for s in range(2):
        semXC[s].inc((CT + 1 - s) // 2)

    HB = FT // 2
    sem_xc = [p.sem() for _ in range(2)]
    sem_hd = [p.sem() for _ in range(2)]

    def s_xcl(eng):
        for s in range(2):
            eng.wait_ge(semXTd[s].h, 16 * ((CT + 1 - s) // 2))
        for c in range(NCC):
            if c >= 2:
                eng.wait_ge(semH.h, (c - 1) * FT)
            eng.dma_start(
                out=xc_stg[c % 2][:].rearrange("p (k t) -> p k t", t=cc),
                in_=xeT_dram.ap().rearrange("(k p) t -> p k t", p=128)[:, :, cc * c:cc * (c + 1)]
                ).then_inc(sem_xc[c % 2].h, 16)
    p.on("sync", s_xcl)
    for s in range(2):
        sem_xc[s].inc(16 * ((NCC + 1 - s) // 2))

    def t_ffn1(eng):
        for s in range(2):
            eng.wait_ge(semXC[s].h, (CT + 1 - s) // 2)
        eng.wait_ge(sem_w.h, t_w)
        for c in range(NCC):
            eng.wait_ge(sem_xc[c % 2].h, 16 * (c // 2 + 1))
            for ft in range(FT):
                pg = ph1 if ft % 2 == 0 else py[0][:, 0:cc]
                pu = ph2 if ft % 2 == 0 else py[1][:, 0:cc]
                u = c * FT + ft
                if u >= 2:
                    eng.wait_ge(semHEs.h, u - 1)
                    eng.wait_ge(semHEv.h, u - 1)
                for dc in range(DC):
                    eng.matmul(pg[:, 0:cc] if ft % 2 == 0 else pg,
                               gw_sb[:, dc * FF + 128 * ft:dc * FF + 128 * (ft + 1)],
                               xc_stg[c % 2][:, dc * cc:(dc + 1) * cc],
                               start=(dc == 0), stop=(dc == DC - 1))
                for dc in range(DC):
                    ins = eng.matmul(pu[:, 0:cc] if ft % 2 == 0 else pu,
                                     uw_sb[:, dc * FF + 128 * ft:dc * FF + 128 * (ft + 1)],
                                     xc_stg[c % 2][:, dc * cc:(dc + 1) * cc],
                                     start=(dc == 0), stop=(dc == DC - 1))
                ins.then_inc(semH.h, 1)
                semH.inc()
    p.on("tensor", t_ffn1)

    def s_silu(eng):
        for c in range(NCC):
            for ft in range(FT):
                u = c * FT + ft
                pg = ph1 if ft % 2 == 0 else py[0][:, 0:cc]
                eng.wait_ge(semH.h, u + 1)
                if u >= 2:
                    eng.wait_ge(semHEv.h, u - 1)
                ins = eng.activation(s1_stg[:, cc * (ft % 2):cc * (ft % 2 + 1)],
                                     pg[:, 0:cc] if ft % 2 == 0 else pg,
                                     AF.Silu)
                ins.then_inc(semHEs.h, 1)
                eng.drain()
        semHEs.inc(NCC * FT)
    p.on("scalar", s_silu)

    def v_hmul(eng):
        for c in range(NCC):
            for ft in range(FT):
                u = c * FT + ft
                hb = (ft // HB) % 2
                pu = ph2 if ft % 2 == 0 else py[1][:, 0:cc]
                eng.wait_ge(semHEs.h, u + 1)
                half = ft // HB
                if ft % HB == 0 and (2 * c + half) >= 2:
                    eng.wait_ge(sem_hd[(2 * c + half) % 2].h,
                                16 * ((2 * c + half) // 2))
                ins = eng.tensor_tensor(
                    hcc_stg[(2 * c + half) % 2][:, cc * (ft % HB):cc * (ft % HB + 1)],
                    s1_stg[:, cc * (ft % 2):cc * (ft % 2 + 1)],
                    pu[:, 0:cc] if ft % 2 == 0 else pu, op=ALU.mult)
                ins.then_inc(semHEv.h, 1)
                eng.drain()
        semHEv.inc(NCC * FT)
    p.on("vector", v_hmul)

    def g_hout(eng):
        for c in range(NCC):
            for half in range(2):
                eng.wait_ge(semHEv.h, c * FT + half * HB + HB)
                eng.dma_start(
                    out=h_dram.ap().rearrange("(k p) t -> p k t", p=128)[:, half * HB:(half + 1) * HB, cc * c:cc * (c + 1)],
                    in_=hcc_stg[(2 * c + half) % 2][:].rearrange("p (k t) -> p k t", t=cc)
                    ).then_inc(sem_hd[(2 * c + half) % 2].h, 16)
    p.on("gpsimd", g_hout)
    for s in range(2):
        sem_hd[s].inc(16 * NCC)

    sem_dw = p.sem()
    sem_hc = [p.sem() for _ in range(2)]

    def g_dw(eng):
        eng.wait_ge(semH.h, NCC * FT)
        eng.dma_start(out=gw_sb[:].rearrange("p (c d) -> p c d", d=D),
                      in_=dw_in.ap().rearrange("(c p) d -> p c d", p=128)
                      ).then_inc(sem_dw.h, 16)
        for s in range(2):
            eng.wait_ge(sem_hd[s].h, 16 * NCC)
        for ct in range(CT):
            if ct >= 2:
                eng.wait_ge(semY.h, ct - 1)
            eng.dma_start(
                out=hcol_stg[ct % 2][:].rearrange("p (k t) -> p k t", t=128),
                in_=h_dram.ap().rearrange("(k p) t -> p k t", p=128)[:, :, 128 * ct:128 * (ct + 1)]
                ).then_inc(sem_hc[ct % 2].h, 16)
    p.on("gpsimd", g_dw)
    sem_dw.inc(16)
    for s in range(2):
        sem_hc[s].inc(16 * ((CT + 1 - s) // 2))

    def t_ffn2(eng):
        eng.wait_ge(sem_dw.h, 16)
        for ct in range(CT):
            eng.wait_ge(sem_hc[ct % 2].h, 16 * (ct // 2 + 1))
            if ct >= 1:
                eng.wait_ge(semYE.h, ct)
            for dh in range(2):
                for ft in range(FT):
                    ins = eng.matmul(
                        py[dh][:],
                        hcol_stg[ct % 2][:, 128 * ft:128 * (ft + 1)],
                        gw_sb[:, ft * D + 512 * dh:ft * D + 512 * (dh + 1)],
                        start=(ft == 0), stop=(ft == FT - 1))
            ins.then_inc(semY.h, 1)
            semY.inc()
    p.on("tensor", t_ffn2)

    def v_ye2(eng):
        for ct in range(CT):
            eng.wait_ge(semY.h, ct + 1)
            if ct >= 2:
                eng.wait_ge(sem_yd[ct % 2].h, 16 * (ct // 2))
            eng.tensor_copy(y_stg[ct % 2][:, 0:512], py[0][:])
            ins = eng.tensor_copy(y_stg[ct % 2][:, 512:1024], py[1][:])
            ins.then_inc(semYE.h, 1)
            eng.drain()
        semYE.inc(CT)
    p.on("vector", v_ye2)

    def g_yout2(eng):
        for ct in range(CT):
            eng.wait_ge(semYE.h, ct + 1)
            eng.dma_start(out=y_out[128 * ct:128 * (ct + 1), :],
                          in_=y_stg[ct % 2][:]).then_inc(sem_yd[ct % 2].h, 16)
        for s in range(2):
            eng.wait_ge(sem_yd[s].h, 16 * ((CT + 1 - s) // 2))
        eng.wait_ge(semVp.h, 2 * NT_FULL)
        eng.dma_start(out=fp1_out[:, :], in_=fp_sb[0][:]).then_inc(sem_st.h, 16)
        eng.dma_start(out=fp2_out[:, :], in_=fp_sb[1][:]).then_inc(sem_st.h, 16)
        eng.wait_ge(sem_st.h, 32)
    p.on("sync", g_yout2)
    for s in range(2):
        sem_yd[s].inc(16 * ((CT + 1 - s) // 2))
    sem_st.inc(32)

    p.finalize()
    return nc


def build_ssm_launch():
    p = Prog()
    nc = p.nc
    ln_full = p.dram("ln_full", [T_FULL, D], BF16, kind="ExternalInput")
    ln_own = p.dram("ln_own", [T_SH, D], BF16, kind="ExternalInput")
    x_in = p.dram("x_sh", [T_SH, D], F32, kind="ExternalInput")
    ar_in = p.dram("A_real", [128, 1], F32, kind="ExternalInput")
    ai_in = p.dram("A_imag", [128, 1], F32, kind="ExternalInput")
    b_in = p.dram("Bmat", [128, D], F32, kind="ExternalInput")
    c_in = p.dram("Cmat", [D, 128], F32, kind="ExternalInput")
    ooh_in = p.dram("own_oh", [128, NT_SH], F32, kind="ExternalInput")
    xo = p.dram("xo", [T_SH, D], F32, kind="ExternalOutput")
    sdram = p.dram("sdram", [128, 256], F32)

    cst = Consts(p)
    DT_ = 0.01
    ar = p.sb("ar", [128, 1], F32); ai = p.sb("ai", [128, 1], F32)
    adr = p.sb("adr", [128, 1], F32); adi = p.sb("adi", [128, 1], F32)
    tmp1 = p.sb("tmp1", [128, 1], F32); tmp2 = p.sb("tmp2", [128, 1], F32)
    tmp_w = p.sb("tmp_w", [128, 128], F32); tmp_w2 = p.sb("tmp_w2", [128, 128], F32)
    inv = p.sb("inv", [128, 1], F32)
    air = p.sb("air", [128, 1], F32); aii = p.sb("aii", [128, 1], F32)
    cr = p.sb("cr", [128, 1], F32); ci = p.sb("ci", [128, 1], F32)
    b_sb = p.sb("b_sb", [128, D], F32)
    br_bf = p.sb("br_bf", [128, D], BF16); bi_bf = p.sb("bi_bf", [128, D], BF16)
    btr = p.sb("btr", [128, D], BF16); bti = p.sb("bti", [128, D], BF16)
    c_sb = p.sb("c_sb", [128, D], F32)
    ct_bf = p.sb("ct_bf", [128, D], BF16)
    WA = [p.sb(f"WA{i}", [128, 128], F32) for i in range(2)]
    WB = [p.sb(f"WB{i}", [128, 128], F32) for i in range(2)]
    VA = [p.sb(f"VA{i}", [128, 128], F32) for i in range(2)]
    VB = [p.sb(f"VB{i}", [128, 128], F32) for i in range(2)]
    Pr = p.sb("Pr", [128, 128], F32); Pi = p.sb("Pi", [128, 128], F32)
    Q0r = p.sb("Q0r", [128, 128], F32); Q0i = p.sb("Q0i", [128, 128], F32)
    Q1r = p.sb("Q1r", [128, 128], F32); Q1i = p.sb("Q1i", [128, 128], F32)
    A127r = p.sb("A127r", [1, 128], F32); A127i = p.sb("A127i", [1, 128], F32)
    A2r = p.sb("A2r", [128, 7], F32); A2i = p.sb("A2i", [128, 7], F32)
    tri_i = p.sb("tri_i", [128, 128], BF16)
    ln_stg = [p.sb(f"lnst{i}", [128, D], BF16) for i in range(2)]
    lnT_stg = [p.sb(f"lnT{i}", [128, D], BF16) for i in range(2)]
    bqr = p.sb("bqr", [128, 128], F32); bqi = p.sb("bqi", [128, 128], F32)
    bqr_bf = p.sb("bqr_bf", [128, 128], BF16)
    bqi_bf = p.sb("bqi_bf", [128, 128], BF16)
    srow = p.sb("srow", [1, 2 * 128], F32)
    sumrow = p.sb("sumrow", [1, 256], F32)
    hsl_r = p.sb("hsl_r", [128, NT_SH * 128], F32)
    Lr_sb = p.sb("Lr_sb", [128, 128], F32); Li_sb = p.sb("Li_sb", [128, 128], F32)
    SS = [p.sb(f"SS{i}", [128, 256], F32) for i in range(2)]
    hbr = p.sb("hbr", [128, 128], F32); hbi = p.sb("hbi", [128, 128], F32)
    hsr_bf = p.sb("hsr_bf", [128, 128], BF16)
    hinr = p.sb("hinr", [128, NT_SH], F32); hini = p.sb("hini", [128, NT_SH], F32)
    ooh_sb = p.sb("ooh_sb", [128, NT_SH], F32)
    ooh_bf = p.sb("ooh_bf", [128, NT_SH], BF16)
    x_stg = [p.sb(f"xst{i}", [128, D], F32) for i in range(2)]
    pbu = [p.ps(f"pbu{i}", [128, 256], F32) for i in range(2)]
    pL = p.ps("pL", [128, 256], F32)
    ptp = p.ps("ptp", [128, 1024], BF16)
    ptf = p.ps("ptf", [128, 256], F32)
    phb = p.ps("phb", [128, 256], F32)
    py = [p.ps(f"py{i}", [128, 512], F32) for i in range(2)]

    sem_ld = p.sem("ld")
    semPrep = p.sem()
    semT = p.sem()
    semV = p.sem()
    semLn = [p.sem() for _ in range(2)]
    semTT = p.sem()
    semBu = p.sem()
    semVq = p.sem()
    semL = p.sem()
    semSum = p.sem()
    semSD = p.sem()
    semScan = p.sem()
    semHB = p.sem()
    semHS = p.sem()
    semY = p.sem()
    semYE = p.sem()
    semXL = [p.sem() for _ in range(2)]
    sem_xo = [p.sem() for _ in range(2)]
    sem_st = p.sem()

    t_ld = dma_in(p, "sync", [
        (ar[:], ar_in[:, :]), (ai[:], ai_in[:, :]),
        (b_sb[:], b_in[:, :]),
        (c_sb[:].rearrange("p (c n) -> p c n", n=128),
         c_in.ap().rearrange("(c p) n -> p c n", p=128)),
        (ooh_sb[:], ooh_in[:, :]),
    ], sem_ld)

    def g_cst2(eng):
        eng.memset(tri_i[:], 0.0)
        eng.drain()
        ins = eng.affine_select(out=tri_i[:], in_=tri_i[:], compare_op=ALU.is_ge,
                                fill=1.0, base=0, pattern=[[-1, 128]],
                                channel_multiplier=1)
        eng.drain()
        eng.memset(tri_i[:], 0.0)
        eng.drain()
        ins = eng.affine_select(out=tri_i[:], in_=tri_i[:], compare_op=ALU.is_ge,
                                fill=1.0, base=-1, pattern=[[-1, 128]],
                                channel_multiplier=1)
        ins.then_inc(semPrep.h, 1)
        eng.drain()
    p.on("gpsimd", g_cst2)
    tP0 = semPrep.inc()

    def v_prep(eng):
        eng.wait_ge(sem_ld.h, t_ld)
        eng.tensor_scalar(tmp1[:], ar[:], 0.005, 1.0, op0=ALU.mult, op1=ALU.add)
        eng.tensor_scalar(tmp2[:], ai[:], 0.005, None, op0=ALU.mult)
        eng.tensor_scalar(adr[:], ar[:], -0.005, 1.0, op0=ALU.mult, op1=ALU.add)
        eng.tensor_scalar(adi[:], ai[:], -0.005, None, op0=ALU.mult)
        eng.drain()
        eng.tensor_tensor(inv[:], adr[:], adr[:], op=ALU.mult)
        eng.tensor_tensor(air[:], adi[:], adi[:], op=ALU.mult)
        eng.drain()
        eng.tensor_tensor(inv[:], inv[:], air[:], op=ALU.add)
        eng.drain()
        eng.reciprocal(inv[:], inv[:])
        eng.drain()
        eng.tensor_tensor(cr[:], adr[:], inv[:], op=ALU.mult)
        eng.tensor_tensor(ci[:], adi[:], inv[:], op=ALU.mult)
        eng.drain()
        eng.tensor_scalar(cr[:], cr[:], 0.1, None, op0=ALU.mult)
        eng.tensor_scalar(ci[:], ci[:], -0.1, None, op0=ALU.mult)
        eng.drain()
        eng.tensor_tensor(air[:], tmp1[:], adr[:], op=ALU.mult)
        eng.tensor_tensor(aii[:], tmp2[:], adi[:], op=ALU.mult)
        eng.drain()
        eng.tensor_tensor(air[:], air[:], aii[:], op=ALU.add)
        eng.drain()
        eng.tensor_tensor(aii[:], tmp2[:], adr[:], op=ALU.mult)
        eng.tensor_tensor(tmp1[:], tmp1[:], adi[:], op=ALU.mult)
        eng.drain()
        eng.tensor_tensor(tmp2[:], aii[:], tmp1[:], op=ALU.subtract)
        eng.drain()
        eng.tensor_tensor(adr[:], air[:], inv[:], op=ALU.mult)
        eng.tensor_tensor(adi[:], tmp2[:], inv[:], op=ALU.mult)
        eng.drain()
        eng.tensor_tensor(tmp1[:], adr[:], adr[:], op=ALU.mult)
        eng.tensor_tensor(tmp2[:], adi[:], adi[:], op=ALU.mult)
        eng.drain()
        eng.tensor_tensor(tmp1[:], tmp1[:], tmp2[:], op=ALU.add)
        eng.drain()
        eng.reciprocal(tmp1[:], tmp1[:])
        eng.drain()
        eng.tensor_tensor(air[:], adr[:], tmp1[:], op=ALU.mult)
        eng.tensor_tensor(aii[:], adi[:], tmp1[:], op=ALU.mult)
        eng.drain()
        eng.tensor_scalar(aii[:], aii[:], -1.0, None, op0=ALU.mult)
        eng.drain()
        eng.tensor_scalar(br_bf[:], b_sb[:], cr[:], None, op0=ALU.mult)
        eng.tensor_scalar(bi_bf[:], b_sb[:], ci[:], None, op0=ALU.mult)
        eng.drain()
        for (Wr, Wi, xr, xi) in ((WA, WB, adr, adi), (VA, VB, air, aii)):
            eng.memset(Wr[0][:], 1.0)
            eng.memset(Wi[0][:], 0.0)
            eng.drain()
            eng.tensor_scalar(Wr[0][:, 1:128], Wr[0][:, 1:128], xr[:], None, op0=ALU.mult)
            eng.drain()
            eng.memset(Wi[0][:, 1:128], 1.0)
            eng.drain()
            eng.tensor_scalar(Wi[0][:, 1:128], Wi[0][:, 1:128], xi[:], None, op0=ALU.mult)
            eng.drain()
            cur = 0
            for s in (1, 2, 4, 8, 16, 32, 64):
                a0, b0 = Wr[cur], Wi[cur]
                a1, b1 = Wr[1 - cur], Wi[1 - cur]
                eng.tensor_copy(a1[:, 0:s], a0[:, 0:s])
                eng.tensor_copy(b1[:, 0:s], b0[:, 0:s])
                eng.drain()
                eng.tensor_tensor(a1[:, s:128], a0[:, s:128], a0[:, 0:128 - s], op=ALU.mult)
                eng.tensor_tensor(b1[:, s:128], b0[:, s:128], a0[:, 0:128 - s], op=ALU.mult)
                eng.drain()
                eng.tensor_tensor(tmp_w[:, s:128], b0[:, 0:128 - s], a0[:, s:128], op=ALU.mult)
                eng.tensor_tensor(tmp_w2[:, s:128], b0[:, s:128], b0[:, 0:128 - s], op=ALU.mult)
                eng.drain()
                eng.tensor_tensor(a1[:, s:128], a1[:, s:128], tmp_w2[:, s:128], op=ALU.subtract)
                eng.tensor_tensor(b1[:, s:128], b1[:, s:128], tmp_w[:, s:128], op=ALU.add)
                eng.drain()
                cur = 1 - cur
        Wr, Wi = WA[1], WB[1]
        eng.tensor_tensor(A2r[:, 0:1], Wr[:, 127:128], adr[:], op=ALU.mult)
        eng.tensor_tensor(tmp1[:], Wi[:, 127:128], adi[:], op=ALU.mult)
        eng.tensor_tensor(A2i[:, 0:1], Wr[:, 127:128], adi[:], op=ALU.mult)
        eng.tensor_tensor(tmp2[:], Wi[:, 127:128], adr[:], op=ALU.mult)
        eng.drain()
        eng.tensor_tensor(A2r[:, 0:1], A2r[:, 0:1], tmp1[:], op=ALU.subtract)
        eng.tensor_tensor(A2i[:, 0:1], A2i[:, 0:1], tmp2[:], op=ALU.add)
        eng.drain()
        for k in range(1, 7):
            eng.tensor_tensor(tmp1[:], A2r[:, k - 1:k], A2r[:, k - 1:k], op=ALU.mult)
            eng.tensor_tensor(tmp2[:], A2i[:, k - 1:k], A2i[:, k - 1:k], op=ALU.mult)
            eng.tensor_tensor(A2i[:, k:k + 1], A2r[:, k - 1:k], A2i[:, k - 1:k], op=ALU.mult)
            eng.drain()
            eng.tensor_tensor(A2r[:, k:k + 1], tmp1[:], tmp2[:], op=ALU.subtract)
            eng.tensor_scalar(A2i[:, k:k + 1], A2i[:, k:k + 1], 2.0, None, op0=ALU.mult)
            eng.drain()
        eng.tensor_scalar(VA[0][:], Wr[:], adr[:], None, op0=ALU.mult)
        eng.tensor_scalar(VB[0][:], Wi[:], adr[:], None, op0=ALU.mult)
        eng.drain()
        eng.tensor_scalar(tmp_w[:], Wi[:], adi[:], None, op0=ALU.mult)
        eng.tensor_scalar(tmp_w2[:], Wr[:], adi[:], None, op0=ALU.mult)
        eng.drain()
        eng.tensor_tensor(VA[0][:], VA[0][:], tmp_w[:], op=ALU.subtract)
        ins = eng.tensor_tensor(VB[0][:], VB[0][:], tmp_w2[:], op=ALU.add)
        ins.then_inc(semPrep.h, 1)
        eng.drain()
        eng.tensor_copy(ooh_bf[:], ooh_sb[:])
        eng.drain()
    p.on("vector", v_prep)
    tP1 = semPrep.inc()

    return _ssm_tail(p, nc, cst, locals())


def _ssm_tail(p, nc, cst, L):
    g = lambda n: L[n]
    (sem_ld, semPrep, semT, semV, semLn, semTT, semBu, semVq, semL, semSum,
     semSD, semScan, semHB, semHS, semY, semYE, semXL, sem_xo, sem_st) = (
        L[k] for k in ("sem_ld", "semPrep", "semT", "semV", "semLn", "semTT",
                       "semBu", "semVq", "semL", "semSum", "semSD", "semScan",
                       "semHB", "semHS", "semY", "semYE", "semXL", "sem_xo",
                       "sem_st"))
    tP0, tP1 = L["tP0"], L["tP1"]
    WA, WB, VA, VB = L["WA"], L["WB"], L["VA"], L["VB"]
    Pr, Pi, Q0r, Q0i, Q1r, Q1i = (L[k] for k in ("Pr", "Pi", "Q0r", "Q0i", "Q1r", "Q1i"))
    A127r, A127i, A2r, A2i = L["A127r"], L["A127i"], L["A2r"], L["A2i"]
    tri_i = L["tri_i"]
    br_bf, bi_bf, btr, bti = L["br_bf"], L["bi_bf"], L["btr"], L["bti"]
    c_sb, ct_bf = L["c_sb"], L["ct_bf"]
    ln_stg, lnT_stg = L["ln_stg"], L["lnT_stg"]
    bqr, bqi, bqr_bf, bqi_bf = L["bqr"], L["bqi"], L["bqr_bf"], L["bqi_bf"]
    srow, sumrow = L["srow"], L["sumrow"]
    Lr_sb, Li_sb = L["Lr_sb"], L["Li_sb"]
    SS, hbr, hbi, hsr_bf = L["SS"], L["hbr"], L["hbi"], L["hsr_bf"]
    hinr, hini, ooh_bf = L["hinr"], L["hini"], L["ooh_bf"]
    x_stg = L["x_stg"]
    pbu, pL, ptp, ptf, phb, py = (L[k] for k in ("pbu", "pL", "ptp", "ptf", "phb", "py"))
    ln_full, ln_own, x_in, xo, sdram = (L[k] for k in ("ln_full", "ln_own", "x_in", "xo", "sdram"))
    tmp_w, tmp_w2 = L["tmp_w"], L["tmp_w2"]
    t_ld = L["t_ld"]

    xp_f32 = [(VA[1][:], Pr[:]), (VB[1][:], Pi[:]),
              (WA[1][:], Q0r[:]), (WB[1][:], Q0i[:]),
              (VA[0][:], Q1r[:]), (VB[0][:], Q1i[:]),
              (WA[1][:, 127:128], A127r[:]), (WB[1][:, 127:128], A127i[:])]
    xp_f32 += [(c_sb[:, 128 * c:128 * (c + 1)], ct_bf[:, 128 * c:128 * (c + 1)])
               for c in range(8)]
    xp_bf = [(br_bf[:, 128 * c:128 * (c + 1)], btr[:, 128 * c:128 * (c + 1)])
             for c in range(8)]
    xp_bf += [(bi_bf[:, 128 * c:128 * (c + 1)], bti[:, 128 * c:128 * (c + 1)])
              for c in range(8)]

    def t_xp(eng):
        eng.wait_ge(cst.sem.h, cst.tgt)
        eng.wait_ge(semPrep.h, tP1)
        for n, (srcap, dstap) in enumerate(xp_f32):
            if n >= 1:
                eng.wait_ge(semV.h, n)
            npart = srcap.ap[1][1] if False else None
            eng.transpose(ptf[:, 0:srcap.ap[0][1]] if False else ptf[:, 0:128][0:srcap.ap[1][1], 0:srcap.ap[0][1]],
                          srcap, cst.ident_f32[:]).then_inc(semT.h, 1)
            semT.inc()
        for n, (srcap, dstap) in enumerate(xp_bf):
            eng.wait_ge(semV.h, len(xp_f32) + n)
            eng.transpose(ptp[0:128, 0:128], srcap, cst.ident_bf[:]).then_inc(semT.h, 1)
            semT.inc()
    p.on("tensor", t_xp)

    def v_xp(eng):
        for n, (srcap, dstap) in enumerate(xp_f32):
            eng.wait_ge(semT.h, n + 1)
            ins = eng.tensor_copy(dstap, ptf[0:128, 0:128][0:srcap.ap[1][1], 0:srcap.ap[0][1]])
            ins.then_inc(semV.h, 1)
            eng.drain()
        for n, (srcap, dstap) in enumerate(xp_bf):
            eng.wait_ge(semT.h, len(xp_f32) + n + 1)
            ins = eng.tensor_copy(dstap, ptp[0:128, 0:128])
            ins.then_inc(semV.h, 1)
            eng.drain()
        semV.inc(len(xp_f32) + len(xp_bf))
    p.on("vector", v_xp)
    NXP = len(xp_f32) + len(xp_bf)

    def s_lnl(eng):
        for c in range(NT_FULL):
            if c >= 2:
                eng.wait_ge(semTT.h, 8 * (c - 1))
            eng.dma_start(out=ln_stg[c % 2][:],
                          in_=ln_full[128 * c:128 * (c + 1), :]
                          ).then_inc(semLn[c % 2].h, 16)
    p.on("sync", s_lnl)
    for s in range(2):
        semLn[s].inc(16 * ((NT_FULL + 1 - s) // 2))

    def t_g(eng):
        eng.wait_ge(semV.h, NXP)
        for c in range(NT_FULL):
            eng.wait_ge(semLn[c % 2].h, 16 * (c // 2 + 1))
            if c >= 1:
                eng.wait_ge(semVq.h, 3 * (c - 1) + 1)
            for dc in range(DC):
                ins = eng.transpose(ptp[:, 128 * dc:128 * (dc + 1)],
                                    ln_stg[c % 2][:, 128 * dc:128 * (dc + 1)],
                                    cst.ident_bf[:])
            ins.then_inc(semTT.h, 8)
            semTT.inc(8)
            eng.wait_ge(semVq.h, 3 * c + 1)
            if c >= 2:
                eng.wait_ge(semVq.h, 3 * (c - 2) + 2)
            for dc in range(DC):
                eng.matmul(pbu[c % 2][:, 0:128],
                           lnT_stg[c % 2][:, 128 * dc:128 * (dc + 1)],
                           btr[:, 128 * dc:128 * (dc + 1)],
                           start=(dc == 0), stop=(dc == DC - 1))
            for dc in range(DC):
                ins = eng.matmul(pbu[c % 2][:, 128:256],
                                 lnT_stg[c % 2][:, 128 * dc:128 * (dc + 1)],
                                 bti[:, 128 * dc:128 * (dc + 1)],
                                 start=(dc == 0), stop=(dc == DC - 1))
            ins.then_inc(semBu.h, 1)
            semBu.inc()
            eng.wait_ge(semVq.h, 3 * c + 2)
            if c >= 1:
                eng.wait_ge(semVq.h, 3 * (c - 1) + 3)
            eng.matmul(pL[0:1, 0:128], cst.ones_bf[:], bqr_bf[:],
                       start=True, stop=True)
            ins = eng.matmul(pL[0:1, 128:256], cst.ones_bf[:], bqi_bf[:],
                             start=True, stop=True)
            ins.then_inc(semSum.h, 1)
            semSum.inc()
    p.on("tensor", t_g)

    def v_g(eng):
        for c in range(NT_FULL):
            eng.wait_ge(semTT.h, 8 * (c + 1))
            if c >= 2:
                eng.wait_ge(semBu.h, c - 1)
            ins = eng.tensor_copy(lnT_stg[c % 2][:], ptp[:])
            ins.then_inc(semVq.h, 1)
            eng.drain()
            eng.wait_ge(semBu.h, c + 1)
            eng.tensor_tensor(bqr[:], pbu[c % 2][:, 0:128], Pr[:], op=ALU.mult)
            eng.tensor_tensor(tmp_w[:], pbu[c % 2][:, 128:256], Pi[:], op=ALU.mult)
            eng.tensor_tensor(bqi[:], pbu[c % 2][:, 128:256], Pr[:], op=ALU.mult)
            eng.tensor_tensor(tmp_w2[:], pbu[c % 2][:, 0:128], Pi[:], op=ALU.mult)
            eng.drain()
            eng.tensor_tensor(bqr[:], bqr[:], tmp_w[:], op=ALU.subtract)
            eng.tensor_tensor(bqi[:], bqi[:], tmp_w2[:], op=ALU.add)
            eng.drain()
            eng.tensor_copy(bqr_bf[:], bqr[:])
            ins = eng.tensor_copy(bqi_bf[:], bqi[:])
            ins.then_inc(semVq.h, 1)
            eng.drain()
            semVq.inc(2)
            eng.wait_ge(semSum.h, c + 1)
            if c >= 1:
                eng.wait_ge(semSD.h, 16 * c)
            eng.tensor_tensor(srow[0:1, 0:128], pL[0:1, 0:128], A127r[:], op=ALU.mult)
            eng.tensor_tensor(tmp_w[0:1, 0:128], pL[0:1, 128:256], A127i[:], op=ALU.mult)
            eng.tensor_tensor(srow[0:1, 128:256], pL[0:1, 128:256], A127r[:], op=ALU.mult)
            eng.tensor_tensor(tmp_w2[0:1, 0:128], pL[0:1, 0:128], A127i[:], op=ALU.mult)
            eng.drain()
            eng.tensor_tensor(srow[0:1, 0:128], srow[0:1, 0:128], tmp_w[0:1, 0:128], op=ALU.subtract)
            ins = eng.tensor_tensor(srow[0:1, 128:256], srow[0:1, 128:256], tmp_w2[0:1, 0:128], op=ALU.add)
            ins.then_inc(semVq.h, 1)
            semVq.inc()
            eng.drain()
    p.on("vector", v_g)

    def g_srow(eng):
        for c in range(NT_FULL):
            eng.wait_ge(semVq.h, 3 * (c + 1))
            eng.dma_start(out=sdram[c:c + 1, :], in_=srow[:]
                          ).then_inc(semSD.h, 16)
    p.on("gpsimd", g_srow)
    semSD.inc(16 * NT_FULL)

    sem_sl = p.sem("sl")

    def g_sload(eng):
        eng.wait_ge(semSD.h, 16 * NT_FULL)
        eng.dma_start(out=SS[0][:], in_=sdram[:, :]).then_inc(sem_sl.h, 16)
    p.on("gpsimd", g_sload)
    sem_sl.inc(16)

    def t_scan(eng):
        eng.wait_ge(sem_sl.h, 16)
        eng.wait_ge(semV.h, NXP)
        eng.transpose(ptf[:, 0:128], SS[0][:, 0:128], cst.ident_f32[:])
        ins = eng.transpose(ptf[:, 128:256], SS[0][:, 128:256], cst.ident_f32[:])
        ins.then_inc(semScan.h, 1)
        semScan.inc()
    p.on("tensor", t_scan)

    def v_scan(eng):
        eng.wait_ge(semScan.h, 1)
        eng.tensor_copy(SS[1][:, 0:128], ptf[:, 0:128])
        eng.tensor_copy(SS[1][:, 128:256], ptf[:, 128:256])
        eng.drain()
        cur = 1
        for k in range(7):
            s = 1 << k
            a0 = SS[cur][:, 0:128]; b0 = SS[cur][:, 128:256]
            a1 = SS[1 - cur][:, 0:128]; b1 = SS[1 - cur][:, 128:256]
            eng.tensor_copy(a1[:, 0:s], a0[:, 0:s])
            eng.tensor_copy(b1[:, 0:s], b0[:, 0:s])
            eng.drain()
            eng.tensor_scalar(tmp_w[:, 0:128 - s], a0[:, 0:128 - s], A2r[:, k:k + 1], None, op0=ALU.mult)
            eng.tensor_scalar(tmp_w2[:, 0:128 - s], b0[:, 0:128 - s], A2i[:, k:k + 1], None, op0=ALU.mult)
            eng.drain()
            eng.tensor_tensor(tmp_w[:, 0:128 - s], tmp_w[:, 0:128 - s], tmp_w2[:, 0:128 - s], op=ALU.subtract)
            eng.drain()
            eng.tensor_tensor(a1[:, s:128], a0[:, s:128], tmp_w[:, 0:128 - s], op=ALU.add)
            eng.drain()
            eng.tensor_scalar(tmp_w[:, 0:128 - s], b0[:, 0:128 - s], A2r[:, k:k + 1], None, op0=ALU.mult)
            eng.tensor_scalar(tmp_w2[:, 0:128 - s], a0[:, 0:128 - s], A2i[:, k:k + 1], None, op0=ALU.mult)
            eng.drain()
            eng.tensor_tensor(tmp_w[:, 0:128 - s], tmp_w[:, 0:128 - s], tmp_w2[:, 0:128 - s], op=ALU.add)
            eng.drain()
            eng.tensor_tensor(b1[:, s:128], b0[:, s:128], tmp_w[:, 0:128 - s], op=ALU.add)
            eng.drain()
            cur = 1 - cur
        ins = eng.tensor_copy(hsr_bf[:], SS[cur][:, 0:128])
        eng.drain()
        ins2 = eng.tensor_copy(bqr_bf[:], SS[cur][:, 128:256])
        ins2.then_inc(semScan.h, 1)
        eng.drain()
    p.on("vector", v_scan)
    semScan.inc()

    def t_hsel(eng):
        eng.wait_ge(semScan.h, 2)
        eng.transpose(ptp[:, 0:128], hsr_bf[:], cst.ident_bf[:])
        ins = eng.transpose(ptp[:, 128:256], bqr_bf[:], cst.ident_bf[:])
        ins.then_inc(semScan.h, 1)
        semScan.inc()
    p.on("tensor", t_hsel)

    def v_hsel(eng):
        eng.wait_ge(semScan.h, 3)
        eng.tensor_copy(bqr_bf[:], ptp[:, 0:128])
        ins = eng.tensor_copy(bqi_bf[:], ptp[:, 128:256])
        ins.then_inc(semScan.h, 1)
        eng.drain()
    p.on("vector", v_hsel)
    semScan.inc()

    def t_hsel2(eng):
        eng.wait_ge(semScan.h, 4)
        eng.matmul(pL[:, 0:NT_SH], bqr_bf[:], ooh_bf[:], start=True, stop=True)
        ins = eng.matmul(pL[:, 64:64 + NT_SH], bqi_bf[:], ooh_bf[:],
                         start=True, stop=True)
        ins.then_inc(semScan.h, 1)
        semScan.inc()
    p.on("tensor", t_hsel2)

    def v_hsel2(eng):
        eng.wait_ge(semScan.h, 5)
        eng.tensor_copy(hinr[:], pL[:, 0:NT_SH])
        ins = eng.tensor_copy(hini[:], pL[:, 64:64 + NT_SH])
        ins.then_inc(semScan.h, 1)
        eng.drain()
    p.on("vector", v_hsel2)
    semScan.inc()

    hsT_sb = p.sb("hsT_sb", [128, 128], BF16)
    semOL = [p.sem() for _ in range(2)]
    semOX = [p.sem() for _ in range(2)]
    semOT = p.sem()
    semOV = p.sem()
    semOB = p.sem()
    semOY = p.sem()
    semOD = [p.sem() for _ in range(2)]

    def s_ol(eng):
        for s in range(2):
            eng.wait_ge(semLn[s].h, 16 * ((NT_FULL + 1 - s) // 2))
        eng.wait_ge(semTT.h, 8 * NT_FULL)
        for m in range(NT_SH):
            if m >= 2:
                eng.wait_ge(semOT.h, 9 * (m - 2) + 8)
            eng.dma_start(out=ln_stg[m % 2][:],
                          in_=ln_own[128 * m:128 * (m + 1), :]
                          ).then_inc(semOL[m % 2].h, 16)
            if m >= 2:
                eng.wait_ge(semOD[m % 2].h, 16 * (m // 2))
            eng.dma_start(out=x_stg[m % 2][:],
                          in_=x_in[128 * m:128 * (m + 1), :]
                          ).then_inc(semOX[m % 2].h, 16)
    p.on("sync", s_ol)
    for s in range(2):
        semOL[s].inc(16 * ((NT_SH + 1 - s) // 2))
        semOX[s].inc(16 * ((NT_SH + 1 - s) // 2))

    def t_o(eng):
        eng.wait_ge(semScan.h, 6)
        for m in range(NT_SH):
            eng.wait_ge(semOL[m % 2].h, 16 * (m // 2 + 1))
            if m >= 1:
                eng.wait_ge(semOV.h, 4 * (m - 1) + 4)
            for dc in range(DC):
                ins = eng.transpose(ptp[:, 128 * dc:128 * (dc + 1)],
                                    ln_stg[m % 2][:, 128 * dc:128 * (dc + 1)],
                                    cst.ident_bf[:])
            ins.then_inc(semOT.h, 8)
            semOT.inc(8)
            eng.wait_ge(semOV.h, 4 * m + 1)
            if m >= 2:
                eng.wait_ge(semOV.h, 4 * (m - 2) + 2)
            for dc in range(DC):
                eng.matmul(pbu[m % 2][:, 0:128],
                           lnT_stg[m % 2][:, 128 * dc:128 * (dc + 1)],
                           btr[:, 128 * dc:128 * (dc + 1)],
                           start=(dc == 0), stop=(dc == DC - 1))
            for dc in range(DC):
                ins = eng.matmul(pbu[m % 2][:, 128:256],
                                 lnT_stg[m % 2][:, 128 * dc:128 * (dc + 1)],
                                 bti[:, 128 * dc:128 * (dc + 1)],
                                 start=(dc == 0), stop=(dc == DC - 1))
            ins.then_inc(semOB.h, 1)
            eng.wait_ge(semOV.h, 4 * m + 2)
            if m >= 1:
                eng.wait_ge(semOV.h, 4 * (m - 1) + 3)
            eng.matmul(pL[:, 0:128], tri_i[:], bqr_bf[:], start=True, stop=True)
            eng.matmul(pL[:, 128:256], tri_i[:], bqi_bf[:], start=True, stop=True)
            col_r = hinr[:, m:m + 1]
            col_i = hini[:, m:m + 1]
            cb_r = bass.AP(col_r.tensor, col_r.offset, [[col_r.ap[0][0], 128], [0, 128]])
            cb_i = bass.AP(col_i.tensor, col_i.offset, [[col_i.ap[0][0], 128], [0, 128]])
            eng.matmul(phb[:, 0:128], cb_r, cst.ident_f32[:], start=True, stop=True)
            ins = eng.matmul(phb[:, 128:256], cb_i, cst.ident_f32[:], start=True, stop=True)
            ins.then_inc(semOB.h, 1)
            semOB.inc(2)
            eng.wait_ge(semOV.h, 4 * m + 3)
            eng.transpose(ptp[:, 0:128], hsr_bf[:], cst.ident_bf[:]).then_inc(semOT.h, 1)
            semOT.inc()
            eng.wait_ge(semOV.h, 4 * m + 4)
            if m >= 1:
                eng.wait_ge(semYE.h, m)
            eng.matmul(py[0][:], hsT_sb[:], ct_bf_w0 if False else ct_bf[:, 0:512],
                       start=True, stop=True)
            ins = eng.matmul(py[1][:], hsT_sb[:], ct_bf[:, 512:1024],
                             start=True, stop=True)
            ins.then_inc(semOY.h, 1)
            semOY.inc()
    p.on("tensor", t_o)

    def v_o(eng):
        for m in range(NT_SH):
            eng.wait_ge(semOT.h, 9 * m + 8)
            if m >= 2:
                eng.wait_ge(semOB.h, 2 * (m - 2) + 1)
            ins = eng.tensor_copy(lnT_stg[m % 2][:], ptp[:])
            ins.then_inc(semOV.h, 1)
            eng.drain()
            eng.wait_ge(semOB.h, 2 * m + 1)
            eng.tensor_tensor(bqr[:], pbu[m % 2][:, 0:128], Pr[:], op=ALU.mult)
            eng.tensor_tensor(tmp_w[:], pbu[m % 2][:, 128:256], Pi[:], op=ALU.mult)
            eng.tensor_tensor(bqi[:], pbu[m % 2][:, 128:256], Pr[:], op=ALU.mult)
            eng.tensor_tensor(tmp_w2[:], pbu[m % 2][:, 0:128], Pi[:], op=ALU.mult)
            eng.drain()
            eng.tensor_tensor(bqr[:], bqr[:], tmp_w[:], op=ALU.subtract)
            eng.tensor_tensor(bqi[:], bqi[:], tmp_w2[:], op=ALU.add)
            eng.drain()
            eng.tensor_copy(bqr_bf[:], bqr[:])
            ins = eng.tensor_copy(bqi_bf[:], bqi[:])
            ins.then_inc(semOV.h, 1)
            eng.drain()
            eng.wait_ge(semOB.h, 2 * m + 2)
            eng.tensor_tensor(tmp_w[:], pL[:, 0:128], Q0r[:], op=ALU.mult)
            eng.tensor_tensor(tmp_w2[:], pL[:, 128:256], Q0i[:], op=ALU.mult)
            eng.drain()
            eng.tensor_tensor(tmp_w[:], tmp_w[:], tmp_w2[:], op=ALU.subtract)
            eng.drain()
            eng.tensor_tensor(tmp_w2[:], phb[:, 0:128], Q1r[:], op=ALU.mult)
            eng.drain()
            eng.tensor_tensor(tmp_w[:], tmp_w[:], tmp_w2[:], op=ALU.add)
            eng.drain()
            eng.tensor_tensor(tmp_w2[:], phb[:, 128:256], Q1i[:], op=ALU.mult)
            eng.drain()
            eng.tensor_tensor(tmp_w[:], tmp_w[:], tmp_w2[:], op=ALU.subtract)
            eng.drain()
            ins = eng.tensor_copy(hsr_bf[:], tmp_w[:])
            ins.then_inc(semOV.h, 1)
            eng.drain()
            eng.wait_ge(semOT.h, 9 * m + 9)
            ins = eng.tensor_copy(hsT_sb[:], ptp[:, 0:128])
            ins.then_inc(semOV.h, 1)
            eng.drain()
            semOV.inc(4)
            eng.wait_ge(semOY.h, m + 1)
            eng.wait_ge(semOX[m % 2].h, 16 * (m // 2 + 1))
            if m >= 2:
                eng.wait_ge(semOD[m % 2].h, 16 * (m // 2))
            eng.scalar_tensor_tensor(x_stg[m % 2][:, 0:512], py[0][:], 1.0,
                                     x_stg[m % 2][:, 0:512], op0=ALU.mult, op1=ALU.add)
            eng.drain()
            ins = eng.scalar_tensor_tensor(x_stg[m % 2][:, 512:1024], py[1][:], 1.0,
                                           x_stg[m % 2][:, 512:1024], op0=ALU.mult, op1=ALU.add)
            ins.then_inc(semYE.h, 1)
            semYE.inc()
            eng.drain()
    p.on("vector", v_o)

    def g_xo(eng):
        for m in range(NT_SH):
            eng.wait_ge(semYE.h, m + 1)
            eng.dma_start(out=xo[128 * m:128 * (m + 1), :],
                          in_=x_stg[m % 2][:]).then_inc(semOD[m % 2].h, 16)
        for s in range(2):
            eng.wait_ge(semOD[s].h, 16 * ((NT_SH + 1 - s) // 2))
    p.on("gpsimd", g_xo)
    for s in range(2):
        semOD[s].inc(16 * ((NT_SH + 1 - s) // 2))

    p.finalize()
    return nc


def build_attn_launch(core_k):
    p = Prog()
    nc = p.nc

    def step(eng_name, waits, fn, inc=None, inc_k=1):
        tgt = inc.inc(inc_k) if inc is not None else None

        def closure(eng, waits=list(waits), fn=fn, inc=inc, inc_k=inc_k,
                    eng_name=eng_name):
            for (s, t) in waits:
                eng.wait_ge(s.h, t)
            ins = fn(eng)
            if inc is not None:
                assert ins is not None
                ins.then_inc(inc.h, inc_k)
            if eng_name in ("vector", "scalar", "gpsimd"):
                eng.drain()
        p.on(eng_name, closure)
        return (inc, tgt) if inc is not None else None

    ln_full = p.dram("ln_full", [T_FULL, D], BF16, kind="ExternalInput")
    ln_own = p.dram("ln_own", [T_SH, D], BF16, kind="ExternalInput")
    x_in = p.dram("x_sh", [T_SH, D], F32, kind="ExternalInput")
    w_in = {n: p.dram(n, [D, D], F32, kind="ExternalInput")
            for n in ("wq", "wk", "wv", "wo")}
    xo = p.dram("xo", [T_SH, D], F32, kind="ExternalOutput")

    cst = Consts(p)
    NSC = 2 * (core_k + 1)
    QT0 = 16 * core_k

    wkT = p.sb("wkT", [128, DC * D], BF16)
    wvT = p.sb("wvT", [128, DC * D], BF16)
    wqT = p.sb("wqT", [128, DC * D], BF16)
    QT = p.sb("QT", [128, DC * T_SH], BF16)
    ctx = p.sb("ctx", [128, NT_SH * D], F32)
    lnc = p.sb("lnc", [128, 8 * D], BF16)
    kT = p.sb("kT", [128, 8 * D], BF16)
    vC = p.sb("vC", [128, 8 * D], BF16)
    P_sb = p.sb("P_sb", [128, 1024], BF16)
    PT_sb = p.sb("PT_sb", [128, 1024], BF16)
    msk = p.sb("msk", [128, 1024], BF16)
    x_stg = [p.sb(f"xst{i}", [128, D], F32) for i in range(2)]
    m_run = p.sb("m_run", [128, NT_SH], F32)
    l_run = p.sb("l_run", [128, NT_SH], F32)
    mloc = p.sb("mloc", [128, 2], F32)
    mnew = p.sb("mnew", [128, 1], F32)
    negm32 = p.sb("negm32", [128, 1], F32)
    adel = p.sb("adel", [128, 1], F32)
    alpha = p.sb("alpha", [128, 1], F32)
    lpart = p.sb("lpart", [128, 2], F32)
    rinv = p.sb("rinv", [128, 1], F32)
    ctxb = P_sb
    ctxT = PT_sb

    psS = [p.ps(f"psS{i}", [128, 512], F32) for i in range(2)]
    ptp = p.ps("ptp", [128, 1024], BF16)
    psC = [p.ps(f"psC{i}", [128, 512], F32) for i in range(2)]

    cW = p.sem("cW")
    cLD = p.sem("cLD")
    cXL = [p.sem(f"cXL{i}") for i in range(2)]
    cOD = [p.sem(f"cOD{i}") for i in range(2)]
    cT = p.sem("cT")
    cV = p.sem("cV")
    cS = p.sem("cS")
    cG = p.sem("cG")

    wT_done = {}
    last_wcopy = None
    for i, (n, dst) in enumerate((("wk", wkT), ("wv", wvT), ("wq", wqT))):
        waits = [] if last_wcopy is None else [last_wcopy]
        step("gpsimd", waits,
             lambda eng, n=n: eng.dma_start(
                 out=lnc[:].rearrange("p (c i) -> p c i", i=D),
                 in_=w_in[n].ap().rearrange("(c p) i -> p c i", p=128)),
             inc=cW, inc_k=16)
        t_load = (cW, cW.n)
        for oc in range(DC):
            def t_fn(eng, oc=oc):
                for ic in range(DC):
                    ins = eng.transpose(
                        ptp[:, 128 * ic:128 * (ic + 1)],
                        lnc[:, oc * D + 128 * ic:oc * D + 128 * (ic + 1)],
                        cst.ident_bf[:])
                return ins
            w = [t_load, (cst.sem, cst.tgt)]
            if last_wcopy is not None:
                w.append(last_wcopy)
            step("tensor", w, t_fn, inc=cT)
            t_tr = (cT, cT.n)

            def v_fn(eng, dst=dst, oc=oc):
                return eng.tensor_copy(
                    dst[:].rearrange("p (c o) -> p c o", o=D)[:, :, 128 * oc:128 * (oc + 1)],
                    ptp[:].rearrange("p (c t) -> p c t", t=128))
            step("vector", [t_tr], v_fn, inc=cV)
            last_wcopy = (cV, cV.n)
        wT_done[n] = last_wcopy

    qt_ready = None
    last_lnT = None
    last_qcopy = None
    qmm = {}
    lnq_loaded = None
    for h in range(2):
        wl = [wT_done["wq"]]
        if h == 1:
            wl.append(last_lnT)
        step("sync", wl,
             lambda eng, h=h: eng.dma_start(
                 out=lnc[:].rearrange("p (j d) -> p j d", d=D),
                 in_=ln_own.ap().rearrange("(j p) d -> p j d", p=128)[:, 8 * h:8 * (h + 1), :]),
             inc=cLD, inc_k=16)
        lnq_loaded = (cLD, cLD.n)
        lnT_half = []
        for jj in range(8):
            j = jj
            w = [lnq_loaded, (cst.sem, cst.tgt)]
            if last_lnT is not None:
                w.append(last_lnT)

            def t_fn(eng, j=j):
                for dc in range(DC):
                    ins = eng.transpose(
                        ptp[:, 128 * dc:128 * (dc + 1)],
                        lnc[:, j * D + 128 * dc:j * D + 128 * (dc + 1)],
                        cst.ident_bf[:])
                return ins
            step("tensor", w, t_fn, inc=cT)
            t_tr = (cT, cT.n)
            wv = [t_tr]
            if h == 1 and jj == 0 and qmm:
                wv.append(qmm[(0, 15)])

            def v_fn(eng, jj=jj):
                return eng.tensor_copy(
                    vC[:].rearrange("p (c t) -> p c t", t=D)[:, :, 128 * jj:128 * (jj + 1)],
                    ptp[:].rearrange("p (c t) -> p c t", t=128))
            step("vector", wv, v_fn, inc=cV)
            last_lnT = (cV, cV.n)
        for u in range(16):
            dq, tc = u // 2, u % 2
            w = [last_lnT]
            if last_qcopy is not None and u >= 2:
                w.append(qprev[(h, u - 2)])
            elif h == 1 and u < 2:
                w.append(qprev[(0, 14 + u)])

            def t_fn(eng, dq=dq, tc=tc):
                for dc in range(DC):
                    ins = eng.matmul(
                        psS[tc % 2][:],
                        wqT[:, dc * D + 128 * dq:dc * D + 128 * (dq + 1)],
                        vC[:, dc * D + 512 * tc:dc * D + 512 * (tc + 1)],
                        start=(dc == 0), stop=(dc == DC - 1))
                return ins
            step("tensor", w, t_fn, inc=cT)
            qmm[(h, u)] = (cT, cT.n)

            def v_fn(eng, h=h, dq=dq, tc=tc):
                return eng.tensor_copy(
                    QT[:, dq * T_SH + 1024 * h + 512 * tc:
                       dq * T_SH + 1024 * h + 512 * (tc + 1)],
                    psS[tc % 2][:])
            step("vector", [qmm[(h, u)]], v_fn, inc=cV)
            if "qprev" not in dir():
                qprev = {}
            qprev[(h, u)] = (cV, cV.n)
            last_qcopy = qprev[(h, u)]
    qt_ready = last_qcopy

    def g_init(eng):
        eng.memset(m_run[:], -1e30)
        eng.drain()
        eng.memset(l_run[:], 0.0)
        eng.drain()
        ins = eng.memset(ctx[:], 0.0)
        eng.drain()
        return ins
    step("gpsimd", [], lambda eng: g_init(eng), inc=cG)
    init_done = (cG, cG.n)

    pairs = []
    for sc in range(NSC):
        for m in range(NT_SH):
            if 16 * core_k + m + 1 > 8 * sc:
                pairs.append((sc, m))

    lnT_free = qt_ready
    lnc_free = qt_ready
    ptp_free = qt_ready
    psS_free = [qmm[(1, 14)], qmm[(1, 15)]]
    psS_free = [qprev[(1, 14)], qprev[(1, 15)]]
    psC_free = [None, None]
    kT_free = None
    vC_free = qprev[(1, 15)]
    PT_free = None
    P_free = None
    msk_free = None
    ctx_free = {m: init_done for m in range(NT_SH)}
    mrun_state = {m: init_done for m in range(NT_SH)}

    for sc in range(NSC):
        w = [lnT_free] if lnc_free is None else [lnc_free]
        if sc == 0:
            w = [qt_ready, init_done]
        step("sync", w, lambda eng, sc=sc: eng.dma_start(
            out=lnc[:].rearrange("p (j d) -> p j d", d=D),
            in_=ln_full.ap().rearrange("(j p) d -> p j d", p=128)[:, 8 * sc:8 * (sc + 1), :]),
            inc=cLD, inc_k=16)
        lnc_loaded = (cLD, cLD.n)
        for jj in range(8):
            w = [lnc_loaded]
            if ptp_free is not None:
                w.append(ptp_free)
            if jj == 0 and lnT_free is not None:
                w.append(lnT_free)

            def t_fn(eng, jj=jj):
                for dc in range(DC):
                    ins = eng.transpose(
                        ptp[:, 128 * dc:128 * (dc + 1)],
                        lnc[:, jj * D + 128 * dc:jj * D + 128 * (dc + 1)],
                        cst.ident_bf[:])
                return ins
            step("tensor", w, t_fn, inc=cT)
            t_tr = (cT, cT.n)

            def v_fn(eng, jj=jj):
                return eng.tensor_copy(
                    wqT[:].rearrange("p (c t) -> p c t", t=D)[:, :, 128 * jj:128 * (jj + 1)],
                    ptp[:].rearrange("p (c t) -> p c t", t=128))
            step("vector", [t_tr], v_fn, inc=cV)
            ptp_free = (cV, cV.n)
        lnT_ready = (cV, cV.n)
        lnc_free = lnT_ready
        last_kcopy = None
        for u in range(16):
            dk, tc = u // 2, u % 2
            w = [lnT_ready]
            if psS_free[tc % 2] is not None:
                w.append(psS_free[tc % 2])
            if u < 2 and kT_free is not None:
                w.append(kT_free)

            def t_fn(eng, dk=dk, tc=tc):
                for dc in range(DC):
                    ins = eng.matmul(
                        psS[tc % 2][:],
                        wkT[:, dc * D + 128 * dk:dc * D + 128 * (dk + 1)],
                        wqT[:, dc * D + 512 * tc:dc * D + 512 * (tc + 1)],
                        start=(dc == 0), stop=(dc == DC - 1))
                return ins
            step("tensor", w, t_fn, inc=cT)
            kmm = (cT, cT.n)

            def v_fn(eng, dk=dk, tc=tc):
                return eng.tensor_copy(
                    kT[:, dk * D + 512 * tc:dk * D + 512 * (tc + 1)], psS[tc % 2][:])
            step("vector", [kmm], v_fn, inc=cV)
            psS_free[tc % 2] = (cV, cV.n)
            last_kcopy = (cV, cV.n)
        kT_ready = last_kcopy
        last_vcopy = None
        for u in range(16):
            tt, dh = u // 2, u % 2
            w = [lnT_ready]
            if psC_free[dh] is not None:
                w.append(psC_free[dh])
            if u < 2 and vC_free is not None:
                w.append(vC_free)

            def t_fn(eng, tt=tt, dh=dh):
                for dc in range(DC):
                    ins = eng.matmul(
                        psC[dh][:],
                        wqT[:, dc * D + 128 * tt:dc * D + 128 * (tt + 1)],
                        wvT[:, dc * D + 512 * dh:dc * D + 512 * (dh + 1)],
                        start=(dc == 0), stop=(dc == DC - 1))
                return ins
            step("tensor", w, t_fn, inc=cT)
            vmm = (cT, cT.n)

            def v_fn(eng, tt=tt, dh=dh):
                return eng.tensor_copy(
                    vC[:, tt * D + 512 * dh:tt * D + 512 * (dh + 1)], psC[dh][:])
            step("vector", [vmm], v_fn, inc=cV)
            psC_free[dh] = (cV, cV.n)
            last_vcopy = (cV, cV.n)
        vC_ready = last_vcopy
        kT_free = None
        sc_pairs = [(s2, m) for (s2, m) in pairs if s2 == sc]
        for (s2, m) in sc_pairs:
            gq = 16 * core_k + m
            klen = min(1024, 128 * (gq + 1) - 1024 * sc)
            nkc = (klen + 511) // 512
            nkt = (klen + 127) // 128
            diag = (128 * gq < 1024 * (sc + 1))
            smm = []
            for kc in range(nkc):
                w = [kT_ready, qt_ready]
                if psS_free[kc % 2] is not None:
                    w.append(psS_free[kc % 2])

                def t_fn(eng, m=m, kc=kc):
                    for dc in range(DC):
                        ins = eng.matmul(
                            psS[kc % 2][:],
                            QT[:, dc * T_SH + 128 * m:dc * T_SH + 128 * (m + 1)],
                            kT[:, dc * D + 512 * kc:dc * D + 512 * (kc + 1)],
                            start=(dc == 0), stop=(dc == DC - 1))
                    return ins
                step("tensor", w, t_fn, inc=cT)
                smm.append((cT, cT.n))
            def v_max(eng, m=m, nkc=nkc, klen=klen):
                for kc in range(nkc):
                    cl = min(512, klen - 512 * kc)
                    eng.reduce_max(mloc[:, kc:kc + 1], psS[kc % 2][:, 0:cl],
                                   axis=AX.X)
                    eng.drain()
                eng.tensor_tensor(mnew[:], mloc[:, 0:1],
                                  mloc[:, nkc - 1:nkc], op=ALU.max)
                eng.drain()
                eng.tensor_tensor(mnew[:], mnew[:], m_run[:, m:m + 1], op=ALU.max)
                eng.drain()
                eng.tensor_scalar(negm32[:], mnew[:], -0.03125, None, op0=ALU.mult)
                eng.tensor_tensor(adel[:], m_run[:, m:m + 1], mnew[:], op=ALU.subtract)
                eng.drain()
                ins = eng.tensor_copy(m_run[:, m:m + 1], mnew[:])
                eng.drain()
                return ins
            step("vector", smm + [mrun_state[m]], v_max, inc=cV)
            vmax_done = (cV, cV.n)
            mrun_state[m] = vmax_done
            def s_fn(eng, nkc=nkc, klen=klen):
                for kc in range(nkc):
                    cl = min(512, klen - 512 * kc)
                    eng.activation(P_sb[:, 512 * kc:512 * kc + cl],
                                   psS[kc % 2][:, 0:cl], AF.Exp,
                                   bias=negm32[:], scale=0.03125,
                                   accum_out=lpart[:, kc:kc + 1])
                    eng.drain()
                ins = eng.activation(alpha[:], adel[:], AF.Exp, scale=0.03125)
                eng.drain()
                return ins
            wS = [vmax_done]
            if P_free is not None:
                wS.append(P_free)
            step("scalar", wS, s_fn, inc=cS)
            exp_done = (cS, cS.n)
            psS_free[0] = exp_done
            psS_free[1] = exp_done
            if diag:
                wG = [exp_done] if msk_free is None else [msk_free, exp_done]

                def g_fn(eng, gq=gq, sc=sc):
                    eng.memset(msk[:], 1.0)
                    eng.drain()
                    ins = eng.affine_select(
                        out=msk[:], in_=msk[:], compare_op=ALU.is_ge, fill=0.0,
                        base=128 * gq - 1024 * sc,
                        pattern=[[-1, 1024]], channel_multiplier=1)
                    eng.drain()
                    return ins
                step("gpsimd", [vmax_done], g_fn, inc=cG)
                msk_ready = (cG, cG.n)
            def v_l(eng, m=m, nkc=nkc, klen=klen, diag=diag):
                if diag:
                    eng.tensor_tensor(P_sb[:, 0:klen], P_sb[:, 0:klen],
                                      msk[:, 0:klen], op=ALU.mult)
                    eng.drain()
                    for kc in range(nkc):
                        cl = min(512, klen - 512 * kc)
                        eng.reduce_sum(lpart[:, kc:kc + 1],
                                       P_sb[:, 512 * kc:512 * kc + cl], axis=AX.X)
                        eng.drain()
                eng.scalar_tensor_tensor(l_run[:, m:m + 1], l_run[:, m:m + 1],
                                         alpha[:], lpart[:, 0:1],
                                         op0=ALU.mult, op1=ALU.add)
                eng.drain()
                if nkc > 1:
                    eng.tensor_tensor(l_run[:, m:m + 1], l_run[:, m:m + 1],
                                      lpart[:, 1:2], op=ALU.add)
                    eng.drain()
                ins = eng.tensor_copy(mloc[:, 0:1], alpha[:])
                eng.drain()
                return ins
            wV = [exp_done]
            if diag:
                wV.append(msk_ready)
            step("vector", wV, v_l, inc=cV)
            vl_done = (cV, cV.n)
            if diag:
                msk_free = vl_done
            pt_marks = []
            for kt in range(nkt):
                w = [vl_done]
                if ptp_free is not None:
                    w.append(ptp_free)
                if kt == 0 and PT_free is not None:
                    w.append(PT_free)

                def t_fn(eng, kt=kt):
                    return eng.transpose(ptp[:, 128 * kt:128 * (kt + 1)],
                                         P_sb[:, 128 * kt:128 * (kt + 1)],
                                         cst.ident_bf[:])
                step("tensor", w, t_fn, inc=cT)
            t_ptr = (cT, cT.n)
            ptp_free = None

            def v_ptc(eng, nkt=nkt):
                return eng.tensor_copy(PT_sb[:, 0:128 * nkt], ptp[:, 0:128 * nkt])
            step("vector", [t_ptr], v_ptc, inc=cV)
            pt_done = (cV, cV.n)
            ptp_free = pt_done
            P_free = pt_done
            for dh in range(2):
                w = [pt_done, vC_ready]
                if psC_free[dh] is not None:
                    w.append(psC_free[dh])

                def t_fn(eng, dh=dh, nkt=nkt):
                    for kt in range(nkt):
                        ins = eng.matmul(
                            psC[dh][:],
                            PT_sb[:, 128 * kt:128 * (kt + 1)],
                            vC[:, kt * D + 512 * dh:kt * D + 512 * (dh + 1)],
                            start=(kt == 0), stop=(kt == nkt - 1))
                    return ins
                step("tensor", w, t_fn, inc=cT)
                cmm = (cT, cT.n)

                def v_fn(eng, m=m, dh=dh):
                    return eng.scalar_tensor_tensor(
                        ctx[:, m * D + 512 * dh:m * D + 512 * (dh + 1)],
                        ctx[:, m * D + 512 * dh:m * D + 512 * (dh + 1)],
                        mloc[:, 0:1], psC[dh][:], op0=ALU.mult, op1=ALU.add)
                step("vector", [cmm, vl_done, ctx_free[m]], v_fn, inc=cV)
                psC_free[dh] = (cV, cV.n)
                ctx_free[m] = (cV, cV.n)
            PT_free = ctx_free[m]
        kT_free = ctx_free.get(sc_pairs[-1][1]) if sc_pairs else kT_ready
        lnT_free = kT_free if sc_pairs else lnT_ready
        vC_free = kT_free

    step("gpsimd", [lnT_free if lnT_free else (cV, cV.n)],
         lambda eng: eng.dma_start(
             out=lnc[:].rearrange("p (c i) -> p c i", i=D),
             in_=w_in["wo"].ap().rearrange("(c p) i -> p c i", p=128)),
         inc=cW, inc_k=16)
    wo_loaded = (cW, cW.n)
    last = None
    for oc in range(DC):
        w = [wo_loaded]
        if ptp_free is not None and oc == 0:
            w.append(ptp_free)
        if last is not None:
            w.append(last)

        def t_fn(eng, oc=oc):
            for ic in range(DC):
                ins = eng.transpose(
                    ptp[:, 128 * ic:128 * (ic + 1)],
                    lnc[:, oc * D + 128 * ic:oc * D + 128 * (ic + 1)],
                    cst.ident_bf[:])
            return ins
        step("tensor", w + [kT_free] if oc == 0 else w, t_fn, inc=cT)
        t_tr = (cT, cT.n)

        def v_fn(eng, oc=oc):
            return eng.tensor_copy(
                kT[:].rearrange("p (c o) -> p c o", o=D)[:, :, 128 * oc:128 * (oc + 1)],
                ptp[:].rearrange("p (c t) -> p c t", t=128))
        step("vector", [t_tr], v_fn, inc=cV)
        last = (cV, cV.n)
    woT_ready = last

    for m in range(NT_SH):
        step("sync", [(cOD[m % 2], 16 * (m // 2)) if m >= 2 else (cst.sem, 0)],
             lambda eng, m=m: eng.dma_start(
                 out=x_stg[m % 2][:], in_=x_in[128 * m:128 * (m + 1), :]),
             inc=cXL[m % 2], inc_k=16)

        def v_norm(eng, m=m):
            eng.reciprocal(rinv[:], l_run[:, m:m + 1])
            eng.drain()
            ins = eng.tensor_scalar(ctxb[:], ctx[:, m * D:(m + 1) * D],
                                    rinv[:], None, op0=ALU.mult)
            eng.drain()
            return ins
        wv = [ctx_free[m], mrun_state[m]]
        if m >= 1:
            wv.append(ctxT_used)
        step("vector", wv, v_norm, inc=cV)
        nrm_done = (cV, cV.n)

        def t_ctr(eng, m=m):
            for dc in range(DC):
                ins = eng.transpose(ptp[:, 128 * dc:128 * (dc + 1)],
                                    ctxb[:, 128 * dc:128 * (dc + 1)],
                                    cst.ident_bf[:])
            return ins
        w = [nrm_done, woT_ready]
        if ptp_free is not None:
            w.append(ptp_free)
        step("tensor", w, t_ctr, inc=cT)
        t_tr = (cT, cT.n)

        def v_ctc(eng, m=m):
            return eng.tensor_copy(
                ctxT[:].rearrange("p (c t) -> p c t", t=128),
                ptp[:].rearrange("p (c t) -> p c t", t=128))
        step("vector", [t_tr], v_ctc, inc=cV)
        ctT_done = (cV, cV.n)
        ptp_free = ctT_done
        for dh in range(2):
            w = [ctT_done]
            if psC_free[dh] is not None:
                w.append(psC_free[dh])

            def t_fn(eng, dh=dh):
                for dc in range(DC):
                    ins = eng.matmul(
                        psC[dh][:],
                        ctxT[:, 128 * dc:128 * (dc + 1)],
                        kT[:, dc * D + 512 * dh:dc * D + 512 * (dh + 1)],
                        start=(dc == 0), stop=(dc == DC - 1))
                return ins
            step("tensor", w, t_fn, inc=cT)
            omm = (cT, cT.n)

            def v_fn(eng, m=m, dh=dh):
                return eng.tensor_tensor(
                    x_stg[m % 2][:, 512 * dh:512 * (dh + 1)],
                    x_stg[m % 2][:, 512 * dh:512 * (dh + 1)] if False else x_stg[m % 2][:, 512 * dh:512 * (dh + 1)],
                    psC[dh][:], op=ALU.add)
            step("vector", [omm, (cXL[m % 2], 16 * (m // 2 + 1))], v_fn, inc=cV)
            psC_free[dh] = (cV, cV.n)
        ctxT_used = (cV, cV.n)
        step("gpsimd", [ctxT_used],
             lambda eng, m=m: eng.dma_start(
                 out=xo[128 * m:128 * (m + 1), :], in_=x_stg[m % 2][:]),
             inc=cOD[m % 2], inc_k=16)

    def g_fin(eng):
        for s in range(2):
            eng.wait_ge(cOD[s].h, 16 * ((NT_SH + 1 - s) // 2))
    p.on("gpsimd", g_fin)

    p.finalize()
    return nc




def dma_x_chunks(p, x_sb, x_src, sem_ld, nck=4):
    per = NT_SH // nck

    def f(eng):
        for c in range(nck):
            eng.dma_start(
                out=x_sb[:].rearrange("p (j d) -> p j d", d=D)[:, per * c:per * (c + 1), :],
                in_=x_src.ap().rearrange("(j p) d -> p j d", p=128)[:, per * c:per * (c + 1), :]
            ).then_inc(sem_ld.h, 16)
    p.on("sync", f)
    base = sem_ld.n
    sem_ld.inc(16 * nck)
    return [(sem_ld, base + 16 * (j // per + 1)) for j in range(NT_SH)]

def emit_ln_route0(p, cst, x_src, rw_in, ln_piece, cnt_piece, meta_piece,
                   cc_ln, cc_cnt, cc_meta, gcar, pfx="a"):
    x_sb = p.sb(pfx + "_x", [128, NT_SH * D], F32)
    ln_sb = p.sb(pfx + "_ln", [128, NT_SH * D], F32)
    rw_sb = p.sb(pfx + "_rw", [128, DC * E], F32)
    bn6 = p.sb(pfx + "_bn6", [128, 12], F32)
    stats = p.sb(pfx + "_st", [128, 2 * NT_SH], F32)
    rstd = p.sb(pfx + "_rs", [128, NT_SH], F32)

    sem_ld = p.dsem("sync"); sem_v2s = p.sem(); sem_s2v = p.sem()
    sem_lnd = p.sem(); sem_st = p.dsem("gpsimd"); semC = p.sem()

    t_ld = dma_in(p, "sync", [
        (rw_sb[:].rearrange("p (c e) -> p c e", e=E),
         rw_in.ap().rearrange("(c p) e -> p c e", p=128))], sem_ld)
    wtile = dma_x_chunks(p, x_sb, x_src, sem_ld)

    def xv(j):
        return x_sb[:, D * j:D * (j + 1)]

    def lv(j):
        return ln_sb[:, D * j:D * (j + 1)]

    t_ln = emit_ln_sweep(p, NT_SH, xv, lv, (sem_v2s, sem_s2v, sem_lnd),
                         (bn6, stats, rstd), wait_v=(sem_ld, t_ld),
                         wait_tile=wtile)

    def tw(eng):
        eng.wait_ge(sem_lnd.h, t_ln)
    p.on("tensor", tw)

    r = emit_route_l0(p, cst, lv, rw_sb, n_tiles=NT_SH, psb=p.psb,
                      rpfx=pfx + "r")

    def v_g(eng):
        eng.wait_ge(r["sem_done"].h, r["tgt_done"])
        ins = eng.tensor_copy(gcar[:], r["gate_all"][:])
        ins.then_inc(semC.h, 1)
        eng.drain()
    p.on("vector", v_g)
    tC = semC.inc()

    def g_out(eng):
        eng.wait_ge(sem_lnd.h, t_ln)
        eng.dma_start(out=ln_piece.ap().rearrange("(j p) d -> p j d", p=128),
                      in_=ln_sb[:].rearrange("p (j d) -> p j d", d=D)
                      ).then_inc(sem_st.h, 16)
        eng.wait_ge(sem_st.h, 16)
        eng.collective_compute(
            "AllGather", mybir.AluOpType.bypass,
            replica_groups=[list(range(N_CORES))],
            ins=[ln_piece.ap().opt()], outs=[cc_ln[1].ap().opt()],
        ).then_inc(cc_ln[0].h, 1)
        eng.wait_ge(r["sem_done"].h, r["tgt_done"])
        eng.dma_start(out=cnt_piece.ap().rearrange("(o j) e -> o (j e)", o=1),
                      in_=r["counts_sb"][:]).then_inc(sem_st.h, 16)
        eng.dma_start(out=meta_piece[:, 0:16], in_=r["idxf_all"][:]).then_inc(sem_st.h, 16)
        eng.dma_start(out=meta_piece[:, 16:32], in_=r["gate_all"][:]).then_inc(sem_st.h, 16)
        eng.dma_start(out=meta_piece[:, 32:48], in_=r["wcum_all"][:]).then_inc(sem_st.h, 16)
        eng.wait_ge(sem_st.h, 16 * 5)
        eng.wait_ge(semC.h, tC)
        eng.collective_compute(
            "AllGather", mybir.AluOpType.bypass,
            replica_groups=[list(range(N_CORES))],
            ins=[cnt_piece.ap().opt()], outs=[cc_cnt[1].ap().opt()],
        ).then_inc(cc_cnt[0].h, 1)
        eng.collective_compute(
            "AllGather", mybir.AluOpType.bypass,
            replica_groups=[list(range(N_CORES))],
            ins=[meta_piece.ap().opt()], outs=[cc_meta[1].ap().opt()],
        ).then_inc(cc_meta[0].h, 1)
    p.on("gpsimd", g_out)
    sem_st.inc(16 * 5)
    for cc in (cc_ln, cc_cnt, cc_meta):
        cc[0].inc()


def emit_expert_l0(p, cst, w1_in, b1_in, w2_in, b2_in, ecol_in, ohsel_in,
                   cc_ln, cc_cnt, cc_meta, cc_ye, ye_piece, slot_tab, fpcar,
                   cc=256, pfx="e0"):
    ln_full = cc_ln[1]
    cnt_all = cc_cnt[1]
    meta_all = cc_meta[1]
    w1_sb = p.sb(pfx + "_w1_sb", [128, DC * FF], BF16)
    w2_sb = p.sb(pfx + "_w2_sb", [128, FT * D], BF16)
    xeT = p.sb(pfx + "_xeT", [128, DC * CAP0], BF16)
    h_sb = p.sb(pfx + "_h_sb", [128, FT * cc], BF16)
    xe_stg = [p.sb(pfx + f"_xe{i}", [128, D], BF16) for i in range(2)]
    y_stg = [p.sb(pfx + f"_ystg{i}", [128, D], F32) for i in range(2)]
    offs_row = y_stg[0][0:1, 0:NT_FULL * 8]
    pre = {}
    pre["cnt_sb"] = p.sb(pfx + "_cnt_sb", [128, E], F32)
    pre["cnt_bf"] = p.sb(pfx + "_cnt_bf", [128, E], BF16)
    pre["offsT_bf"] = p.sb(pfx + "_offsT_bf", [8, NT_FULL], F32)
    pre["idx_sb"] = p.sb(pfx + "_idx_sb", [128, NT_FULL], F32)
    pre["gate_sb"] = p.sb(pfx + "_gate_sb", [128, NT_FULL], F32)
    pre["wcum_sb"] = p.sb(pfx + "_wcum_sb", [128, NT_FULL], F32)
    pre["ecol"] = p.sb(pfx + "_ecol", [128, 1], F32)
    pre["fp_sb"] = p.sb(pfx + "_fp_sb", [128, NT_FULL], F32)
    pre["zero_cap"] = p.sb(pfx + "_zero_cap", [128, 32], F32)
    tmul = p.sb(pfx + "_tmul", [128, NT_FULL * 8], F32)
    emask_b = p.sb(pfx + "_emb", [128, NT_FULL * 8], BF16)
    tokid = p.sb(pfx + "_tokid", [128, NT_FULL], F32)
    oob128 = p.sb(pfx + "_oob", [128, NT_FULL], F32)
    pos_cS = p.sb(pfx + "_poscS", [128, NT_FULL], F32)
    keep_cS = p.sb(pfx + "_keepcS", [128, NT_FULL], F32)
    selS = p.sb(pfx + "_selS", [128, NT_FULL], F32)
    selU = p.sb(pfx + "_selU", [128, NT_FULL], U32)
    slotS = p.sb(pfx + "_slotS", [128, NT_FULL], F32)
    slot_uA = p.sb(pfx + "_slotuA", [128, NT_FULL], U32)
    tokslot = p.sb(pfx + "_tokslot", [128, 2 * NT_FULL], F32)
    offsJ = p.sb(pfx + "_offsJ", [128, 8], F32)
    offs_rt = p.dram(pfx + "_offsrt", [1, NT_FULL * 8], F32)
    ohsel_sb = p.sb(pfx + "_ohsel_sb", [128, 16], F32)
    fpT_sb = p.sb(pfx + "_fpT_sb", [128, 128], F32)
    ofpT_sb = p.sb(pfx + "_ofpT_sb", [16, 128], F32)
    gidx_u = p.sb(pfx + "_gidx_u", [128, NT_SH], U32)
    sl_sb = p.sb(pfx + "_sl_sb", [128, NT_SH * 2], F32)
    b1col = p.sb(pfx + "_b1col", [128, FT], F32)
    b1row = p.sb(pfx + "_b1row", [32, 128], F32)
    b2_bf = p.sb(pfx + "_b2_bf", [1, D], BF16)
    ones_row_bf = p.sb(pfx + "_ones_row_bf", [1, 128], BF16)
    ones_row_f = p.sb(pfx + "_ones_row_f", [1, 128], F32)

    psb = p.psb
    psm = psb["F0"][:, 0:256]
    ptr = [psb["B0"][:, 0:D], psb["B1"][:, 0:D]]
    ph = [psb["F1"][:, 0:cc], psb["F2"][:, 0:cc]]
    py = [psb["F1024"][:, 0:512], psb["F1024"][:, 512:1024]]
    pre["ppos"] = [ph[0][:, 0:8], ph[1][:, 0:8]]

    sem_ld = p.dsem("sync")
    sem_w = p.dsem("gpsimd")
    semV = p.sem()
    semP = p.sem()
    sem_offs = p.sem()
    sem_zero = p.sem()
    semSL = p.dsem("gpsimd")
    semGs = [p.dsem("gpsimd") for _ in range(2)]
    semTT = p.sem()
    semXT = p.sem()
    semH = p.sem()
    semHS = p.sem()
    semY = p.sem()
    semYE = p.sem()
    sem_yd = [p.dsem("gpsimd") for _ in range(2)]
    semFP = p.sem()

    def d_in(eng):
        eng.wait_ge(cc_cnt[0].h, 1)
        eng.dma_start(out=pre["cnt_sb"][:], in_=cnt_all[:, :]).then_inc(sem_ld.h, 16)
        eng.wait_ge(cc_meta[0].h, 1)
        ma = meta_all.ap().rearrange("(c p) x -> p c x", p=128)
        idx_d = pre["idx_sb"][:].rearrange("p (c m) -> p c m", m=16)
        gate_d = pre["gate_sb"][:].rearrange("p (c m) -> p c m", m=16)
        wcum_d = pre["wcum_sb"][:].rearrange("p (c m) -> p c m", m=16)
        eng.dma_start(out=idx_d, in_=ma[:, :, 0:16]).then_inc(sem_ld.h, 16)
        eng.dma_start(out=gate_d, in_=ma[:, :, 16:32]).then_inc(sem_ld.h, 16)
        eng.dma_start(out=wcum_d, in_=ma[:, :, 32:48]).then_inc(sem_ld.h, 16)
        eng.dma_start(out=pre["ecol"][:], in_=ecol_in[:, :]).then_inc(sem_ld.h, 16)
        eng.dma_start(out=ohsel_sb[:], in_=ohsel_in[:, :]).then_inc(sem_ld.h, 16)
        eng.dma_start(out=b1row[:], in_=b1_in.ap().rearrange("x (a b) -> (x a) b", b=128)).then_inc(sem_ld.h, 16)
    p.on("sync", d_in)
    t_ld = sem_ld.inc(16 * 7)

    def d_w(eng):
        eng.dma_start(out=w1_sb[:].rearrange("p (c f) -> p c f", f=FF),
                      in_=w1_in.ap().rearrange("(c p) f -> p c f", p=128)
                      ).then_inc(sem_w.h, 16)
        eng.dma_start(out=w2_sb[:].rearrange("p (c d) -> p c d", d=D),
                      in_=w2_in.ap().rearrange("(c p) d -> p c d", p=128)
                      ).then_inc(sem_w.h, 16)
        eng.dma_start(out=b2_bf[:], in_=b2_in[:, :]).then_inc(sem_w.h, 16)
    p.on("gpsimd", d_w)
    t_w = sem_w.inc(16 * 3)

    def g_prep(eng):
        eng.memset(ones_row_bf[:], 1.0)
        eng.memset(ones_row_f[:], 1.0)
        eng.drain()
        eng.iota(tokid[:], pattern=[[128, NT_FULL]], base=0,
                 channel_multiplier=1, allow_small_or_imprecise_dtypes=True)
        eng.drain()
        eng.memset(oob128[:], float(OOBV))
        eng.drain()
        eng.memset(pre["zero_cap"][:], 0.0).then_inc(sem_zero.h, 1)
        eng.drain()
    p.on("gpsimd", g_prep)
    pre["sem_zero"] = sem_zero
    pre["tgt_zero"] = sem_zero.inc()

    def v_prep(eng):
        eng.wait_ge(sem_ld.h, t_ld)
        eng.wait_ge(sem_zero.h, pre["tgt_zero"])
        ins = eng.tensor_copy(pre["cnt_bf"][:], pre["cnt_sb"][:])
        ins.then_inc(semV.h, 1)
        eng.drain()
    p.on("vector", v_prep)
    tVP = semV.inc()

    def t_prep(eng):
        eng.wait_ge(cst.sem.h, cst.tgt)
        eng.wait_ge(semV.h, tVP)
        eng.matmul(psm[0:8, 0:NT_FULL], pre["cnt_bf"][:], cst.tri_bf[:],
                   start=True, stop=True)
        ins = eng.transpose(psm[:, 128:160], b1row[:], cst.ident_f32[0:32, 0:32])
        ins.then_inc(semP.h, 1)
    p.on("tensor", t_prep)
    tP = semP.inc()

    def v_prep2(eng):
        eng.wait_ge(semP.h, tP)
        eng.tensor_copy(pre["offsT_bf"][:], psm[0:8, 0:NT_FULL])
        ins = eng.tensor_copy(b1col[:], psm[:, 128:160])
        ins.then_inc(sem_offs.h, 1)
        eng.drain()
    p.on("vector", v_prep2)
    pre["sem_offs"] = sem_offs
    pre["tgt_offs"] = sem_offs.inc()

    semSC = p.dsem("gpsimd")
    semZF2 = p.dsem("gpsimd")
    semPB = p.sem()
    semPD = p.dsem("gpsimd")
    pw1 = p.psb["F1"][:, 0:512]
    pw2 = p.psb["F2"][:, 0:512]

    def t_posb(eng):
        eng.wait_ge(sem_offs.h, pre["tgt_offs"])
        ins = eng.transpose(psm[:, 0:8], pre["offsT_bf"][:],
                            cst.ident_f32[0:8, 0:8])
        ins.then_inc(semPB.h, 1)
    p.on("tensor", t_posb)
    tPB1 = semPB.inc()

    def v_posb1(eng):
        eng.wait_ge(semPB.h, tPB1)
        ins = eng.tensor_copy(offsJ[:], psm[:, 0:8])
        ins.then_inc(semPB.h, 1)
        eng.drain()
    p.on("vector", v_posb1)
    tPB2 = semPB.inc()

    def g_offs(eng):
        eng.wait_ge(semPB.h, tPB2)
        eng.dma_start(out=offs_rt.ap().rearrange("o (j e) -> (o j) e", e=8),
                      in_=offsJ[:]).then_inc(semPD.h, 16)
        eng.wait_ge(semPD.h, 16)
        eng.dma_start(out=offs_row[:], in_=offs_rt[0:1, :]).then_inc(semPD.h, 16)
    p.on("gpsimd", g_offs)
    tPD = semPD.inc(32)

    def t_posb2(eng):
        eng.wait_ge(semPD.h, tPD)
        eng.matmul(pw1[:], ones_row_f[0:1, 0:128], offs_row[0:1, 0:512],
                   start=True, stop=True)
        ins = eng.matmul(pw2[:], ones_row_f[0:1, 0:128], offs_row[0:1, 512:1024],
                         start=True, stop=True)
        ins.then_inc(semPB.h, 1)
    p.on("tensor", t_posb2)
    tPB3 = semPB.inc()

    def v_posb2(eng):
        eng.wait_ge(semPB.h, tPB3)
        idx = pre["idx_sb"]
        ib3 = bass.AP(idx[:].tensor, idx[:].offset,
                      [[idx[:].ap[0][0], 128], [1, NT_FULL], [0, 8]])
        io8 = cst.iota8[:]
        io3 = bass.AP(io8.tensor, io8.offset,
                      [[io8.ap[0][0], 128], [0, NT_FULL], [1, 8]])
        eng.tensor_tensor(emask_b[:].rearrange("p (j e) -> p j e", e=8),
                          io3, ib3, op=ALU.is_equal)
        eng.drain()
        eng.tensor_tensor(tmul[:, 0:512], pw1[:], emask_b[:, 0:512], op=ALU.mult)
        eng.drain()
        eng.tensor_tensor(tmul[:, 512:1024], pw2[:], emask_b[:, 512:1024],
                          op=ALU.mult)
        eng.drain()
        tv = tmul[:].rearrange("p (j e) -> p j e", e=8)
        eng.tensor_tensor(pos_cS[:], tv[:, :, 0], tv[:, :, 1], op=ALU.add)
        eng.drain()
        for e2 in range(2, 8):
            eng.tensor_tensor(pos_cS[:], pos_cS[:], tv[:, :, e2], op=ALU.add)
            eng.drain()
        eng.tensor_tensor(pos_cS[:], pos_cS[:], pre["wcum_sb"][:], op=ALU.add)
        eng.drain()
        eng.tensor_scalar(keep_cS[:], pos_cS[:], float(CAP0), None, op0=ALU.is_lt)
        eng.drain()
        eng.tensor_scalar(selS[:], idx[:], pre["ecol"][:], None, op0=ALU.is_equal)
        eng.drain()
        eng.tensor_tensor(selS[:], selS[:], keep_cS[:], op=ALU.mult)
        eng.drain()
        eng.tensor_copy(selU[:], selS[:])
        eng.drain()
        eng.select(slotS[:], selU[:], pos_cS[:], oob128[:], add_drain=True)
        eng.drain()
        eng.tensor_copy(slot_uA[:], slotS[:])
        eng.drain()
        ts2 = tokslot[:].rearrange("p (j x) -> p j x", x=2)
        eng.tensor_copy(ts2[:, :, 0], tokid[:])
        eng.tensor_copy(ts2[:, :, 1], pre["gate_sb"][:])
        eng.drain()
        eng.tensor_scalar(selS[:], idx[:], float(CAP0), None, op0=ALU.mult)
        eng.drain()
        eng.tensor_tensor(selS[:], selS[:], pos_cS[:], op=ALU.add)
        eng.tensor_copy(selU[:], keep_cS[:])
        eng.drain()
        ins = eng.select(pre["fp_sb"][:], selU[:], selS[:], oob128[:],
                         add_drain=True)
        ins.then_inc(semPB.h, 1)
        eng.drain()
    p.on("vector", v_posb2)
    tPB4 = semPB.inc()

    def g_scat0(eng):
        eng.wait_ge(sem_zero.h, pre["tgt_zero"])
        eng.dma_start(out=slot_tab.ap().rearrange("(j p) x -> p j x", p=128),
                      in_=pre["zero_cap"][:].rearrange("p (j x) -> p j x", x=2)
                      ).then_inc(semZF2.h, 16)
        eng.wait_ge(semZF2.h, 16)
        eng.wait_ge(semPB.h, tPB4)
        with eng.register(pfx + "_bcs") as bc:
            eng.reg_mov(bc, CAP0 - 1)
            for u in range(NT_FULL):
                if u >= 12:
                    eng.wait_ge(semSC.h, IDMA_INC * (u - 11))
                eng.indirect_dma_start(
                    out=slot_tab[:, :],
                    out_offset=IndirectOffsetOnAxis(slot_uA[:, u:u + 1], 0),
                    in_=tokslot[:, 2 * u:2 * u + 2], in_offset=None,
                    bounds_check=bc, oob_is_err=False,
                ).then_inc(semSC.h, 16)
    p.on("gpsimd", g_scat0)
    semZF2.inc(16)
    t_sc = semSC.inc(IDMA_INC * NT_FULL)

    def g_gather(eng):
        eng.wait_ge(cc_ln[0].h, 1)
        eng.wait_ge(semSC.h, t_sc)
        eng.dma_start(out=sl_sb[:].rearrange("p (j x) -> p j x", x=2),
                      in_=slot_tab.ap().rearrange("(j p) x -> p j x", p=128)
                      ).then_inc(semSL.h, 16)
        eng.wait_ge(semSL.h, 16)
        eng.wait_ge(semV.h, tVP + 1)
        for j in range(NT_SH):
            if j >= 2:
                eng.wait_ge(semTT.h, j - 1)
            eng.indirect_dma_start(
                out=xe_stg[j % 2][:], out_offset=None,
                in_=ln_full[:, :],
                in_offset=IndirectOffsetOnAxis(gidx_u[:, j:j + 1], 0),
            ).then_inc(semGs[j % 2].h, 16)
    p.on("gpsimd", g_gather)
    semSL.inc(16)
    t_gs = [semGs[s].inc(IDMA_INC * (NT_SH // 2)) for s in range(2)]

    def v_gidx(eng):
        eng.wait_ge(semSL.h, 16)
        ins = eng.tensor_copy(gidx_u[:], sl_sb[:].rearrange("p (j x) -> p j x", x=2)[:, :, 0])
        ins.then_inc(semV.h, 1)
        eng.drain()
    p.on("vector", v_gidx)
    semV.inc()

    def t_tr(eng):
        for j in range(NT_SH):
            eng.wait_ge(semGs[j % 2].h, IDMA_INC * (j // 2 + 1))
            if j >= 2:
                eng.wait_ge(semXT.h, j - 1)
            for dc in range(DC):
                ins = eng.transpose(ptr[j % 2][:, 128 * dc:128 * (dc + 1)],
                                    xe_stg[j % 2][:, 128 * dc:128 * (dc + 1)],
                                    cst.ident_bf[:])
            ins.then_inc(semTT.h, 1)
            semTT.inc()
    p.on("tensor", t_tr)

    def v_xt(eng):
        for j in range(NT_SH):
            eng.wait_ge(semTT.h, j + 1)
            ins = eng.tensor_copy(
                xeT[:].rearrange("p (c t) -> p c t", t=CAP0)[:, :, 128 * j:128 * (j + 1)],
                ptr[j % 2][:].rearrange("p (c t) -> p c t", t=128))
            ins.then_inc(semXT.h, 1)
            eng.drain()
        semXT.inc(NT_SH)
    p.on("vector", v_xt)

    NCC = CAP0 // cc
    CT = cc // 128

    def t_ffn(eng):
        eng.wait_ge(semXT.h, NT_SH)
        eng.wait_ge(sem_w.h, t_w)
        eng.wait_ge(sem_offs.h, pre["tgt_offs"])
        for c in range(NCC):
            for ft in range(FT):
                if c * FT + ft >= 2:
                    eng.wait_ge(semHS.h, c * FT + ft - 1)
                for dc in range(DC):
                    ins = eng.matmul(
                        ph[ft % 2][:, 0:cc],
                        w1_sb[:, dc * FF + 128 * ft:dc * FF + 128 * (ft + 1)],
                        xeT[:, dc * CAP0 + c * cc:dc * CAP0 + (c + 1) * cc],
                        start=(dc == 0), stop=(dc == DC - 1))
                ins.then_inc(semH.h, 1)
                semH.inc()
            eng.wait_ge(semHS.h, (c + 1) * FT)
            for ct in range(CT):
                if c * CT + ct >= 1:
                    eng.wait_ge(semYE.h, c * CT + ct)
                for dh in range(2):
                    for ft in range(FT):
                        eng.matmul(
                            py[dh][:],
                            h_sb[:, ft * cc + 128 * ct:ft * cc + 128 * (ct + 1)],
                            w2_sb[:, ft * D + 512 * dh:ft * D + 512 * (dh + 1)],
                            start=(ft == 0), stop=False)
                    ins = eng.matmul(py[dh][:], ones_row_bf[0:1, 0:128],
                                     b2_bf[0:1, 512 * dh:512 * (dh + 1)],
                                     start=False, stop=True)
                ins.then_inc(semY.h, 1)
                semY.inc()
        eng.wait_ge(semPB.h, tPB4)
        eng.transpose(psm[:, 0:128], pre["fp_sb"][:], cst.ident_f32[:]).then_inc(semFP.h, 1)
        eng.wait_ge(semFP.h, 2)
        eng.matmul(psm[0:16, 128:256], ohsel_sb[:], fpT_sb[:],
                   start=True, stop=True).then_inc(semFP.h, 1)
        eng.wait_ge(semFP.h, 4)
        eng.transpose(psm[:, 128:144], ofpT_sb[:], cst.ident_f32[0:16, 0:16]).then_inc(semFP.h, 1)
    p.on("tensor", t_ffn)

    def s_relu(eng):
        for c in range(NCC):
            for ft in range(FT):
                eng.wait_ge(semH.h, c * FT + ft + 1)
                ins = eng.activation(h_sb[:, ft * cc + 0:ft * cc + cc],
                                     ph[ft % 2][:, 0:cc], AF.Relu,
                                     bias=b1col[:, ft:ft + 1], scale=1.0)
                ins.then_inc(semHS.h, 1)
                eng.drain()
        semHS.inc(NCC * FT)
    p.on("scalar", s_relu)

    def v_ye(eng):
        for c in range(NCC):
            for ct in range(CT):
                i = c * CT + ct
                eng.wait_ge(semY.h, i + 1)
                if i >= 2:
                    eng.wait_ge(sem_yd[i % 2].h, 16 * (i // 2))
                eng.tensor_copy(y_stg[i % 2][:, 0:512], py[0][:])
                ins = eng.tensor_copy(y_stg[i % 2][:, 512:1024], py[1][:])
                ins.then_inc(semYE.h, 1)
                eng.drain()
        semYE.inc(NCC * CT)
    p.on("vector", v_ye)

    def v_fp(eng):
        eng.wait_ge(semFP.h, 1)
        ins = eng.tensor_copy(fpT_sb[:], psm[:, 0:128])
        ins.then_inc(semFP.h, 1)
        eng.drain()
        eng.wait_ge(semFP.h, 3)
        ins = eng.tensor_copy(ofpT_sb[:], psm[0:16, 128:256])
        ins.then_inc(semFP.h, 1)
        eng.drain()
        eng.wait_ge(semFP.h, 5)
        ins = eng.tensor_copy(fpcar[:], psm[:, 128:144])
        ins.then_inc(semFP.h, 1)
        eng.drain()
    p.on("vector", v_fp)
    t_fp = semFP.inc(6)


    def g_yout(eng):
        for c in range(NCC):
            for ct in range(CT):
                i = c * CT + ct
                eng.wait_ge(semYE.h, i + 1)
                eng.dma_start(out=ye_piece[128 * i:128 * (i + 1), :],
                              in_=y_stg[i % 2][:]).then_inc(sem_yd[i % 2].h, 16)
        for s in range(2):
            eng.wait_ge(sem_yd[s].h, 16 * (NCC * CT // 2))
        eng.collective_compute(
            "AllGather", mybir.AluOpType.bypass,
            replica_groups=[list(range(N_CORES))],
            ins=[ye_piece.ap().opt()], outs=[cc_ye[1].ap().opt()],
        ).then_inc(cc_ye[0].h, 1)
        eng.wait_ge(semFP.h, t_fp)
    p.on("gpsimd", g_yout)
    for s in range(2):
        sem_yd[s].inc(16 * (NCC * CT // 2))
    cc_ye[0].inc()


def emit_combine(p, cst, x_src, x_dst, cc_ye, fpcars, gcars, scale=0.5,
                 pfx="c0", ln_out=None, ln_piece=None, cc_ln=None):
    nadd = len(fpcars)
    ye_all = cc_ye[1]
    nrows = ye_all.shape[0]
    x_sb = p.sb(pfx + "_c_x", [128, NT_SH * D], F32)
    yg = [p.sb(pfx + f"_c_yg{i}", [128, D], BF16) for i in range(4)]
    ygf = [p.sb(pfx + f"_c_ygf{i}", [128, D], F32) for i in range(4)]
    fp_u = [p.sb(pfx + f"_c_fpu{i}", [128, NT_SH], U32) for i in range(nadd)]
    hg = [p.sb(pfx + f"_c_hg{i}", [128, NT_SH], F32) for i in range(nadd)]

    sem_ld = p.dsem("sync")
    semVr = p.sem()
    semGd = [p.dsem("gpsimd") for _ in range(4)]
    semAcc = p.sem()
    sem_st = p.dsem("gpsimd")

    t_ld = dma_in(p, "sync", [
        (x_sb[:].rearrange("p (j d) -> p j d", d=D),
         x_src.ap().rearrange("(j p) d -> p j d", p=128))], sem_ld)

    def v_prep(eng):
        eng.wait_ge(sem_ld.h, t_ld)
        for i in range(nadd):
            eng.tensor_copy(fp_u[i][:], fpcars[i][:])
            eng.tensor_scalar(hg[i][:], gcars[i][:], scale, None, op0=ALU.mult)
        eng.drain()
        ins = eng.tensor_copy(ygf[0][:, 0:1], x_sb[:, 0:1])
        ins.then_inc(semVr.h, 1)
        eng.drain()
    p.on("vector", v_prep)
    tVr = semVr.inc()

    NG = NT_SH * nadd

    def g_go(eng):
        ccs = cc_ye[0] if isinstance(cc_ye[0], list) else [cc_ye[0]]
        for s4 in ccs:
            eng.wait_ge(s4.h, 1)
        eng.wait_ge(semVr.h, tVr)
        with eng.register(pfx + "_bc") as bc:
            eng.reg_mov(bc, nrows - 1)
            for u in range(NG):
                j, i = u // nadd, u % nadd
                if u >= 4:
                    eng.wait_ge(semAcc.h, u - 3)
                eng.memset(yg[u % 4][:], 0.0)
                eng.drain()
                ins = eng.indirect_dma_start(
                    out=yg[u % 4][:], out_offset=None,
                    in_=ye_all[:, :],
                    in_offset=IndirectOffsetOnAxis(fp_u[i][:, j:j + 1], 0),
                    bounds_check=bc, oob_is_err=False)
                ins.then_inc(semGd[u % 4].h, 16)
    p.on("gpsimd", g_go)
    for s in range(4):
        semGd[s].inc(IDMA_INC * ((NG + 3 - s) // 4))

    def v_acc(eng):
        for u in range(NG):
            j, i = u // nadd, u % nadd
            eng.wait_ge(semGd[u % 4].h, IDMA_INC * (u // 4 + 1))
            eng.tensor_copy(ygf[u % 4][:], yg[u % 4][:])
            eng.drain()
            xt = x_sb[:, D * j:D * (j + 1)]
            ins = eng.scalar_tensor_tensor(
                xt, ygf[u % 4][:], hg[i][:, j:j + 1], xt,
                op0=ALU.mult, op1=ALU.add)
            ins.then_inc(semAcc.h, 1)
            eng.drain()
        semAcc.inc(NG)
    p.on("vector", v_acc)

    def g_out(eng):
        eng.wait_ge(semAcc.h, NG)
        eng.dma_start(out=x_dst.ap().rearrange("(j p) d -> p j d", p=128),
                      in_=x_sb[:].rearrange("p (j d) -> p j d", d=D)
                      ).then_inc(sem_st.h, 16)
        eng.wait_ge(sem_st.h, 16)
    p.on("gpsimd", g_out)
    sem_st.inc(16)

    if ln_out is not None:
        bn6 = p.sb(pfx + "_lbn6", [128, 12], F32)
        stats = p.sb(pfx + "_lst", [128, 2 * NT_SH], F32)
        rstd = p.sb(pfx + "_lrs", [128, NT_SH], F32)
        sem_v2s = p.sem(); sem_s2v = p.sem(); sem_lnd = p.sem()
        wtile = [(semAcc, nadd * (j + 1)) for j in range(NT_SH)]

        def xv(j):
            return x_sb[:, D * j:D * (j + 1)]

        def lv(j):
            return ln_out[:, D * j:D * (j + 1)]

        t_ln = emit_ln_sweep(p, NT_SH, xv, lv, (sem_v2s, sem_s2v, sem_lnd),
                             (bn6, stats, rstd), wait_tile=wtile)
        if ln_piece is not None:
            sem_lp = p.dsem("gpsimd")

            def g_lp(eng):
                eng.wait_ge(sem_lnd.h, t_ln)
                eng.dma_start(
                    out=ln_piece.ap().rearrange("(j p) d -> p j d", p=128),
                    in_=ln_out[:].rearrange("p (j d) -> p j d", d=D)
                    ).then_inc(sem_lp.h, 16)
                eng.wait_ge(sem_lp.h, 16)
                eng.collective_compute(
                    "AllGather", mybir.AluOpType.bypass,
                    replica_groups=[list(range(N_CORES))],
                    ins=[ln_piece.ap().opt()], outs=[cc_ln[1].ap().opt()],
                ).then_inc(cc_ln[0].h, 1)
            p.on("gpsimd", g_lp)
            sem_lp.inc(16)
            cc_ln[0].inc()



def emit_ln_bf(p, cst, x_src, ln_bf, pfx="lb"):
    x_sb = p.sb(pfx + "_x", [128, NT_SH * D], F32)
    bn6 = p.sb(pfx + "_bn6", [128, 12], F32)
    stats = p.sb(pfx + "_st", [128, 2 * NT_SH], F32)
    rstd = p.sb(pfx + "_rs", [128, NT_SH], F32)
    sem_ld = p.dsem("sync")
    sem_v2s = p.sem(); sem_s2v = p.sem(); sem_lnd = p.sem()
    wtile = dma_x_chunks(p, x_sb, x_src, sem_ld)

    def xv(j):
        return x_sb[:, D * j:D * (j + 1)]

    def lv(j):
        return ln_bf[:, D * j:D * (j + 1)]

    t_ln = emit_ln_sweep(p, NT_SH, xv, lv, (sem_v2s, sem_s2v, sem_lnd),
                         (bn6, stats, rstd), wait_tile=wtile)
    return sem_lnd, t_ln


def _mk_step(p):
    def step(eng_name, waits, fn, inc=None, inc_k=1):
        tgt = inc.inc(inc_k) if inc is not None else None

        def closure(eng, waits=list(waits), fn=fn, inc=inc, inc_k=inc_k,
                    eng_name=eng_name):
            for (s, t) in waits:
                eng.wait_ge(s.h, t)
            ins = fn(eng)
            if inc is not None:
                assert ins is not None
                ins.then_inc(inc.h, inc_k)
            if eng_name in ("vector", "scalar", "gpsimd"):
                eng.drain()
        p.on(eng_name, closure)
        return (inc, tgt) if inc is not None else None
    return step


def emit_attn_qkv(p, cst, ln_bf, ln_done, w_ins, q_piece, kt_piece, v_piece,
                  cc_q, cc_kt, cc_v):
    wq_in, wk_in, wv_in = w_ins
    wstage = p.sb("aq_wst", [128, DC * D], BF16)
    wT = p.sb("aq_wT", [128, DC * D], BF16)
    lnT = p.sb("aq_lnT", [128, DC * D], BF16)
    kst = [p.sb(f"aq_kst{i}", [128, 512], BF16) for i in range(2)]
    vst = [p.sb(f"aq_vst{i}", [128, 512], BF16) for i in range(2)]
    psS = [p.psb["F0"][:, 0:512], p.psb["F1"][:, 0:512]]
    psC = [p.psb["F2"][:, 0:512], p.psb["F3"][:, 0:512]]
    ptp = p.psb["B0"][:, 0:1024]

    sW = p.dsem("gpsimd")
    sKD = p.dsem("gpsimd")
    cT = p.sem()
    cV = p.sem()
    step = _mk_step(p)

    n_out = [0]

    def w_build(name, win, prev_done):
        step("gpsimd", [prev_done] if prev_done else [],
             lambda eng, win=win: eng.dma_start(
                 out=wstage[:].rearrange("p (c i) -> p c i", i=D),
                 in_=win.ap().rearrange("(c p) i -> p c i", p=128)),
             inc=sW, inc_k=16)
        t_load = (sW, sW.n)
        last = None
        for oc in range(DC):
            w = [t_load, (cst.sem, cst.tgt)]
            if last is not None:
                w.append(last)

            def t_fn(eng, oc=oc):
                for ic in range(DC):
                    ins = eng.transpose(
                        ptp[:, 128 * ic:128 * (ic + 1)],
                        wstage[:, oc * D + 128 * ic:oc * D + 128 * (ic + 1)],
                        cst.ident_bf[:])
                return ins
            step("tensor", w, t_fn, inc=cT)
            t_tr = (cT, cT.n)

            def v_fn(eng, oc=oc):
                return eng.tensor_copy(
                    wT[:].rearrange("p (c o) -> p c o", o=D)[:, :, 128 * oc:128 * (oc + 1)],
                    ptp[:].rearrange("p (c t) -> p c t", t=128))
            step("vector", [t_tr], v_fn, inc=cV)
            last = (cV, cV.n)
        return last

    prev = None
    for h in range(2):
        lnT_done = None
        for jj in range(8):
            j = 8 * h + jj
            w = [(ln_done[0], ln_done[1])]
            if prev is not None:
                w.append(prev)

            def t_fn(eng, j=j):
                for dc in range(DC):
                    ins = eng.transpose(
                        ptp[:, 128 * dc:128 * (dc + 1)],
                        ln_bf[:, j * D + 128 * dc:j * D + 128 * (dc + 1)],
                        cst.ident_bf[:])
                return ins
            step("tensor", w, t_fn, inc=cT)
            t_tr = (cT, cT.n)

            def v_fn(eng, jj=jj):
                return eng.tensor_copy(
                    lnT[:].rearrange("p (c t) -> p c t", t=D)[:, :, 128 * jj:128 * (jj + 1)],
                    ptp[:].rearrange("p (c t) -> p c t", t=128))
            step("vector", [t_tr], v_fn, inc=cV)
            prev = (cV, cV.n)
            lnT_done = prev
        wq_done = w_build("wq", wq_in, prev)
        last_q = None
        qfree = {}
        for u in range(16):
            tt, dh = u // 2, u % 2
            w = [wq_done, lnT_done]
            if u >= 2:
                w.append(qfree[dh])

            def t_fn(eng, tt=tt, dh=dh):
                for dc in range(DC):
                    ins = eng.matmul(
                        psC[dh][:],
                        lnT[:, dc * D + 128 * tt:dc * D + 128 * (tt + 1)],
                        wT[:, dc * D + 512 * dh:dc * D + 512 * (dh + 1)],
                        start=(dc == 0), stop=(dc == DC - 1))
                return ins
            step("tensor", w, t_fn, inc=cT)
            qmm = (cT, cT.n)
            wv_ = [qmm]
            if n_out[0] >= 1:
                wv_.append((sKD, 16 * (n_out[0] - 1)))

            def v_fn(eng, dh=dh):
                return eng.tensor_copy(vst[dh][:], psC[dh][:])
            step("vector", wv_, v_fn, inc=cV)
            qcp = (cV, cV.n)
            qfree[dh] = qcp
            n_out[0] += 1
            step("gpsimd", [qcp],
                 lambda eng, tt=tt, dh=dh, h=h:
                 eng.dma_start(
                     out=q_piece[1024 * h + 128 * tt:1024 * h + 128 * (tt + 1),
                                 512 * dh:512 * (dh + 1)],
                     in_=vst[dh][:]),
                 inc=sKD, inc_k=16)
            last_q = (cV, cV.n)
        if h == 1:
            def g_qcc(eng):
                eng.wait_ge(sKD.h, sKD.n)
                eng.collective_compute(
                    "AllGather", mybir.AluOpType.bypass,
                    replica_groups=[list(range(N_CORES))],
                    ins=[q_piece.ap().opt()], outs=[cc_q[1].ap().opt()],
                ).then_inc(cc_q[0].h, 1)
            p.on("gpsimd", g_qcc)
            cc_q[0].inc()
        wk_done = w_build("wk", wk_in, last_q)
        last_k = None
        kfree = {}
        for u in range(16):
            dk, tc = u // 2, u % 2
            w = [wk_done, lnT_done]
            if u >= 2:
                w.append(kfree[tc])

            def t_fn(eng, dk=dk, tc=tc):
                for dc in range(DC):
                    ins = eng.matmul(
                        psS[tc][:],
                        wT[:, dc * D + 128 * dk:dc * D + 128 * (dk + 1)],
                        lnT[:, dc * D + 512 * tc:dc * D + 512 * (tc + 1)],
                        start=(dc == 0), stop=(dc == DC - 1))
                return ins
            step("tensor", w, t_fn, inc=cT)
            kmm = (cT, cT.n)
            wv_ = [kmm]
            if n_out[0] >= 1:
                wv_.append((sKD, 16 * (n_out[0] - 1)))

            def v_fn(eng, tc=tc):
                return eng.tensor_copy(kst[tc][:], psS[tc][:])
            step("vector", wv_, v_fn, inc=cV)
            kcp = (cV, cV.n)
            kfree[tc] = kcp
            n_out[0] += 1
            step("gpsimd", [kcp],
                 lambda eng, dk=dk, tc=tc, h=h:
                 eng.dma_start(
                     out=kt_piece[128 * dk:128 * (dk + 1),
                                  1024 * h + 512 * tc:1024 * h + 512 * (tc + 1)],
                     in_=kst[tc][:]),
                 inc=sKD, inc_k=16)
            last_k = (sKD, sKD.n)
        if h == 1:
            def g_kcc(eng, tgt=sKD.n):
                eng.wait_ge(sKD.h, tgt)
                eng.collective_compute(
                    "AllGather", mybir.AluOpType.bypass,
                    replica_groups=[list(range(N_CORES))],
                    ins=[kt_piece.ap().opt()], outs=[cc_kt[1].ap().opt()],
                ).then_inc(cc_kt[0].h, 1)
            p.on("gpsimd", g_kcc)
            cc_kt[0].inc()
        wv_done = w_build("wv", wv_in, (cV, cV.n))
        vfree = {}
        for u in range(16):
            tt, dh = u // 2, u % 2
            w = [wv_done, lnT_done]
            if u >= 2:
                w.append(vfree[dh])

            def t_fn(eng, tt=tt, dh=dh):
                for dc in range(DC):
                    ins = eng.matmul(
                        psC[dh][:],
                        lnT[:, dc * D + 128 * tt:dc * D + 128 * (tt + 1)],
                        wT[:, dc * D + 512 * dh:dc * D + 512 * (dh + 1)],
                        start=(dc == 0), stop=(dc == DC - 1))
                return ins
            step("tensor", w, t_fn, inc=cT)
            vmm = (cT, cT.n)
            wv_ = [vmm]
            if n_out[0] >= 1:
                wv_.append((sKD, 16 * (n_out[0] - 1)))

            def v_fn(eng, dh=dh):
                return eng.tensor_copy(vst[dh][:], psC[dh][:])
            step("vector", wv_, v_fn, inc=cV)
            vcp = (cV, cV.n)
            vfree[dh] = vcp
            n_out[0] += 1
            step("gpsimd", [vcp],
                 lambda eng, tt=tt, dh=dh, h=h:
                 eng.dma_start(
                     out=v_piece[1024 * h + 128 * tt:1024 * h + 128 * (tt + 1),
                                 512 * dh:512 * (dh + 1)],
                     in_=vst[dh][:]),
                 inc=sKD, inc_k=16)
        prev = (cV, cV.n)

    def g_cc(eng):
        eng.wait_ge(sKD.h, sKD.n)
        eng.collective_compute(
            "AllGather", mybir.AluOpType.bypass,
            replica_groups=[list(range(N_CORES))],
            ins=[v_piece.ap().opt()], outs=[cc_v[1].ap().opt()],
        ).then_inc(cc_v[0].h, 1)
    p.on("gpsimd", g_cc)
    cc_v[0].inc()


def emit_attn_flash(p, cst, cq_items, cc_q, cc_kt, cc_v, cc_ao, ao_piece,
                    wo_in, x_src, x_dst):
    keyrow, sKR, cqsc, sCQ, cq_sb, aidx_sb, sIDX = cq_items
    Q_all = cc_q[1]
    KT_all = cc_kt[1]
    V_all = cc_v[1]
    AO_all = cc_ao[1]
    QT = p.sb("af_QT", [128, DC * T_SH], BF16)
    kT = [p.sb(f"af_kT{i}", [128, DC * 1024], BF16) for i in range(2)]
    vC = [p.sb(f"af_vC{i}", [128, 8 * 1024], BF16) for i in range(2)]
    ctx4 = p.sb("af_ctx", [128, 4 * D], F32)
    sc_sb = p.sb("af_sc", [128, 8 * 1024], BF16)
    P8 = p.sb("af_P8", [128, 8 * 1024], BF16)
    PT8 = p.sb("af_PT8", [128, 8 * 1024], BF16)
    msk = p.sb("af_msk", [128, 1024], BF16)
    m_run = p.sb("af_m", [128, 16], F32)
    l_run = p.sb("af_l", [128, 16], F32)
    mloc8 = p.sb("af_mloc", [128, 8], F32)
    mnew8 = p.sb("af_mnew", [128, 4], F32)
    negm8 = p.sb("af_negm", [128, 4], F32)
    adel8 = p.sb("af_adel", [128, 4], F32)
    alpha8 = p.sb("af_alpha", [128, 4], F32)
    lp8 = p.sb("af_lp", [128, 4], F32)
    rinv = p.sb("af_rinv", [128, 1], F32)
    woT = p.sb("af_woT", [128, DC * D], BF16)
    qg_stg = [p.sb(f"af_qg{i}", [128, D], BF16) for i in range(2)]
    ao_stg = [p.sb(f"af_ao{i}", [128, D], BF16) for i in range(2)] + qg_stg[0:1]
    x_stg = [p.sb(f"af_xst{i}", [128, D], F32) for i in range(3)]
    qidx_u = p.sb("af_qidx", [128, 16], U32)
    aidx_u = p.sb("af_aidx", [128, 16], U32)
    psSp = [[p.psb["F0"][:, 0:512], p.psb["F1"][:, 0:512]],
            [p.psb["F2"][:, 0:512], p.psb["F3"][:, 0:512]]]
    psC = [p.psb["F1024"][:, 0:512], p.psb["F1024"][:, 512:1024]]
    ptpp = [p.psb["B0"][:, 0:1024], p.psb["B1"][:, 0:1024]]
    ptp = ptpp[0]

    cLD = p.dsem("sync")
    cXL = p.dsem("sync")
    cOD = p.dsem("gpsimd")
    cQG = p.dsem("gpsimd")
    sW = p.dsem("gpsimd")
    cT = p.sem()
    cV = p.sem()
    cS = p.sem()
    cG = p.sem()
    step = _mk_step(p)

    def g_init(eng):
        eng.memset(m_run[:], -1e5)
        eng.drain()
        eng.memset(l_run[:], 0.0)
        eng.drain()
        ins = eng.memset(ctx4[:], 0.0)
        eng.drain()
        return ins
    step("gpsimd", [], g_init, inc=cG)
    init_ml = (cG, cG.n)

    def v_idx(eng):
        eng.wait_ge(sIDX.h, 1)
        eng.tensor_copy(qidx_u[:], cq_sb[:])
        ins = eng.tensor_copy(aidx_u[:], aidx_sb[:])
        eng.drain()
        return ins
    step("vector", [], v_idx, inc=cV)
    idx_done = (cV, cV.n)

    step("gpsimd", [], lambda eng: eng.dma_start(
        out=PT8[:].rearrange("p (c i) -> p c i", i=D),
        in_=wo_in.ap().rearrange("(c p) i -> p c i", p=128)),
        inc=sW, inc_k=16)
    wo_loaded = (sW, sW.n)
    wo_done = None
    for oc in range(DC):
        w = [wo_loaded, (cst.sem, cst.tgt)]
        if wo_done is not None:
            w.append(wo_done)

        def t_fn(eng, oc=oc):
            for ic in range(DC):
                ins = eng.transpose(
                    ptp[:, 128 * ic:128 * (ic + 1)],
                    PT8[:, oc * D + 128 * ic:oc * D + 128 * (ic + 1)],
                    cst.ident_bf[:])
            return ins
        step("tensor", w, t_fn, inc=cT)
        t_tr = (cT, cT.n)

        def v_fn(eng, oc=oc):
            return eng.tensor_copy(
                woT[:].rearrange("p (c o) -> p c o", o=D)[:, :, 128 * oc:128 * (oc + 1)],
                ptp[:].rearrange("p (c t) -> p c t", t=128))
        step("vector", [t_tr], v_fn, inc=cV)
        wo_done = (cV, cV.n)

    qg_done = {}
    last_tr = wo_done
    for j in range(16):
        w = [(cc_q[0], 1), idx_done]
        if j >= 2:
            w.append(qg_done[j - 2])
        step("gpsimd", w, lambda eng, j=j: eng.indirect_dma_start(
            out=qg_stg[j % 2][:], out_offset=None,
            in_=Q_all[:, :],
            in_offset=IndirectOffsetOnAxis(qidx_u[:, j:j + 1], 0)),
            inc=cQG, inc_k=16)
        q_ld = (cQG, cQG.n)
        w = [q_ld]
        if last_tr is not None:
            w.append(last_tr)

        def t_fn(eng, j=j):
            for dc in range(DC):
                ins = eng.transpose(ptp[:, 128 * dc:128 * (dc + 1)],
                                    qg_stg[j % 2][:, 128 * dc:128 * (dc + 1)],
                                    cst.ident_bf[:])
            return ins
        step("tensor", w, t_fn, inc=cT)
        t_tr = (cT, cT.n)

        def v_fn(eng, j=j):
            return eng.tensor_copy(
                QT[:].rearrange("p (c t) -> p c t", t=T_SH)[:, :, 128 * j:128 * (j + 1)],
                ptp[:].rearrange("p (c t) -> p c t", t=128))
        step("vector", [t_tr], v_fn, inc=cV)
        last_tr = (cV, cV.n)
        qg_done[j] = last_tr
    qt_ready = last_tr

    kv_free = {0: None, 1: None}
    evac_free = [None, None]
    P8_free = None
    ptp_free = [qt_ready, qt_ready]
    PT_free = {}
    psC_free = [None, None]
    sc_free = None
    ao_free = {}
    iu = 0
    ld_u = 0

    for grp in range(4):
        tiles = list(range(4 * grp, 4 * grp + 4))
        wz = [init_ml]
        if grp > 0:
            wz.append(epi_done)
        step("gpsimd", wz, lambda eng: eng.memset(ctx4[:], 0.0), inc=cG)
        ctx_memset = (cG, cG.n)
        for sc in range(4 * grp + 4):
            part = [j for j in tiles if j >= sc]
            slot = ld_u % 2
            ld_u += 1
            wl = [(cc_kt[0], 1)]
            if kv_free[slot] is not None:
                wl.append(kv_free[slot])
            c0 = sc // 2
            tcol = 1024 * (sc % 2)
            step("sync", wl, lambda eng, c0=c0, tcol=tcol, slot=slot:
                 eng.dma_start(
                     out=kT[slot][:].rearrange("p (c t) -> p c t", t=1024),
                     in_=KT_all.ap()[1024 * c0:1024 * (c0 + 1),
                                     tcol:tcol + 1024].rearrange(
                         "(c p) t -> p c t", p=128)),
                 inc=cLD, inc_k=16)
            step("sync", [(cc_v[0], 1)], lambda eng, sc=sc, slot=slot:
                 eng.dma_start(
                     out=vC[slot][:].rearrange("p (j d) -> p j d", d=D),
                     in_=V_all.ap().rearrange("(j p) d -> p j d", p=128)[:, 8 * sc:8 * (sc + 1), :]),
                 inc=cLD, inc_k=16)
            kv_loaded = (cLD, cLD.n)

            for n_, j in enumerate(part):
                par = iu % 2
                iu += 1
                psS = psSp[par]
                w = [kv_loaded, qt_ready]
                if sc_free is not None:
                    w.append(sc_free)
                if evac_free[par] is not None:
                    w.append(evac_free[par])

                def t_fn(eng, j=j, slot=slot, psS=psS):
                    for kc in range(2):
                        for dc in range(DC):
                            ins = eng.matmul(
                                psS[kc][:],
                                QT[:, dc * T_SH + 128 * j:dc * T_SH + 128 * (j + 1)],
                                kT[slot][:, dc * 1024 + 512 * kc:dc * 1024 + 512 * (kc + 1)],
                                start=(dc == 0), stop=(dc == DC - 1))
                    return ins
                step("tensor", w, t_fn, inc=cT)
                smm = (cT, cT.n)

                def v_ev(eng, n_=n_, psS=psS):
                    eng.tensor_copy(sc_sb[:, n_ * 1024:n_ * 1024 + 512], psS[0][:])
                    ins = eng.tensor_copy(sc_sb[:, n_ * 1024 + 512:(n_ + 1) * 1024], psS[1][:])
                    eng.drain()
                    return ins
                step("vector", [smm], v_ev, inc=cV)
                evac_free[par] = (cV, cV.n)
            evacs_done = (cV, cV.n)

            def v_msk(eng, sc=sc, part=tuple(part)):
                for n_, j in enumerate(part):
                    eng.tensor_scalar(msk[:], keyrow[:],
                                      cqsc[:, 16 * sc + j:16 * sc + j + 1],
                                      None, op0=ALU.is_ge)
                    eng.drain()
                    ins = eng.scalar_tensor_tensor(
                        sc_sb[:, n_ * 1024:(n_ + 1) * 1024], msk[:], -1e30,
                        sc_sb[:, n_ * 1024:(n_ + 1) * 1024],
                        op0=ALU.mult, op1=ALU.add)
                    eng.drain()
                return ins
            step("vector", [evacs_done, (sCQ, 1), (sKR, 1)], v_msk, inc=cV)
            msk_done = (cV, cV.n)

            def v_max(eng, part=tuple(part)):
                np_ = len(part)
                for n_ in range(np_):
                    eng.reduce_max(mloc8[:, 2 * n_:2 * n_ + 1],
                                   sc_sb[:, n_ * 1024:n_ * 1024 + 512], axis=AX.X)
                    eng.reduce_max(mloc8[:, 2 * n_ + 1:2 * n_ + 2],
                                   sc_sb[:, n_ * 1024 + 512:(n_ + 1) * 1024], axis=AX.X)
                eng.drain()
                eng.tensor_tensor(mnew8[:, 0:np_], mloc8[:, 0:2 * np_:2],
                                  mloc8[:, 1:2 * np_:2], op=ALU.max)
                eng.drain()
                eng.tensor_tensor(mnew8[:, 0:np_], mnew8[:, 0:np_],
                                  m_run[:, part[0]:part[0] + np_], op=ALU.max)
                eng.drain()
                eng.tensor_scalar(mnew8[:, 0:np_], mnew8[:, 0:np_], -1e5, None,
                                  op0=ALU.max)
                eng.drain()
                eng.tensor_scalar(negm8[:, 0:np_], mnew8[:, 0:np_], -0.03125,
                                  None, op0=ALU.mult)
                eng.tensor_tensor(adel8[:, 0:np_],
                                  m_run[:, part[0]:part[0] + np_],
                                  mnew8[:, 0:np_], op=ALU.subtract)
                eng.drain()
                ins = eng.tensor_copy(m_run[:, part[0]:part[0] + np_],
                                      mnew8[:, 0:np_])
                eng.drain()
                return ins
            step("vector", [msk_done, init_ml], v_max, inc=cV)
            vmax_done = (cV, cV.n)

            wS = [vmax_done]
            if P8_free is not None:
                wS.append(P8_free)

            def s_fn(eng, part=tuple(part)):
                for n_ in range(len(part)):
                    eng.activation(P8[:, n_ * 1024:(n_ + 1) * 1024],
                                   sc_sb[:, n_ * 1024:(n_ + 1) * 1024], AF.Exp,
                                   bias=negm8[:, n_:n_ + 1], scale=0.03125,
                                   accum_out=lp8[:, n_:n_ + 1])
                    eng.drain()
                ins = eng.activation(alpha8[:, 0:len(part)],
                                     adel8[:, 0:len(part)], AF.Exp, scale=0.03125)
                eng.drain()
                return ins
            step("scalar", wS, s_fn, inc=cS)
            exp_done = (cS, cS.n)
            sc_free = exp_done

            def v_l(eng, part=tuple(part)):
                np_ = len(part)
                eng.tensor_tensor(l_run[:, part[0]:part[0] + np_],
                                  l_run[:, part[0]:part[0] + np_],
                                  alpha8[:, 0:np_], op=ALU.mult)
                eng.drain()
                ins = eng.tensor_tensor(l_run[:, part[0]:part[0] + np_],
                                        l_run[:, part[0]:part[0] + np_],
                                        lp8[:, 0:np_], op=ALU.add)
                eng.drain()
                return ins
            step("vector", [exp_done], v_l, inc=cV)
            vl_done = (cV, cV.n)

            for n_, j in enumerate(part):
                par = n_ % 2
                w = [exp_done]
                if ptp_free[par] is not None:
                    w.append(ptp_free[par])

                def t_fn(eng, n_=n_, par=par):
                    for kt_ in range(8):
                        ins = eng.transpose(
                            ptpp[par][:, 128 * kt_:128 * (kt_ + 1)],
                            P8[:, n_ * 1024 + 128 * kt_:n_ * 1024 + 128 * (kt_ + 1)],
                            cst.ident_bf[:])
                    return ins
                step("tensor", w, t_fn, inc=cT)
                t_ptr = (cT, cT.n)
                wv_ = [t_ptr]
                if PT_free.get(n_) is not None:
                    wv_.append(PT_free[n_])

                def v_ptc(eng, n_=n_, par=par):
                    return eng.tensor_copy(
                        PT8[:, n_ * 1024:(n_ + 1) * 1024], ptpp[par][:])
                step("vector", wv_, v_ptc, inc=cV)
                ptp_free[par] = (cV, cV.n)
            P8_free = (cV, cV.n)
            pt_all = (cV, cV.n)

            for n_, j in enumerate(part):
                ml = j - 4 * grp
                for dh in range(2):
                    w = [pt_all, kv_loaded]
                    if psC_free[dh] is not None:
                        w.append(psC_free[dh])

                    def t_fn(eng, n_=n_, dh=dh, slot=slot):
                        for kt_ in range(8):
                            ins = eng.matmul(
                                psC[dh][:],
                                PT8[:, n_ * 1024 + 128 * kt_:n_ * 1024 + 128 * (kt_ + 1)],
                                vC[slot][:, kt_ * D + 512 * dh:kt_ * D + 512 * (dh + 1)],
                                start=(kt_ == 0), stop=(kt_ == 7))
                        return ins
                    step("tensor", w, t_fn, inc=cT)
                    cmm = (cT, cT.n)

                    def v_fn(eng, ml=ml, n_=n_, dh=dh):
                        return eng.scalar_tensor_tensor(
                            ctx4[:, ml * D + 512 * dh:ml * D + 512 * (dh + 1)],
                            ctx4[:, ml * D + 512 * dh:ml * D + 512 * (dh + 1)],
                            alpha8[:, n_:n_ + 1], psC[dh][:],
                            op0=ALU.mult, op1=ALU.add)
                    step("vector", [cmm, vl_done, ctx_memset], v_fn, inc=cV)
                    psC_free[dh] = (cV, cV.n)
                PT_free[n_] = (cV, cV.n)
            kv_free[slot] = (cV, cV.n)

        for ml in range(4):
            j = 4 * grp + ml
            def v_norm(eng, ml=ml, j=j):
                eng.reciprocal(rinv[:], l_run[:, j:j + 1])
                eng.drain()
                ins = eng.tensor_scalar(P8[:, 0:1024], ctx4[:, ml * D:(ml + 1) * D],
                                        rinv[:], None, op0=ALU.mult)
                eng.drain()
                return ins
            wv_ = [P8_free] if P8_free is not None else []
            step("vector", wv_, v_norm, inc=cV)
            nrm_done = (cV, cV.n)

            w = [nrm_done, wo_done]
            if ptp_free[0] is not None:
                w.append(ptp_free[0])

            def t_ctr(eng):
                for dc in range(DC):
                    ins = eng.transpose(ptp[:, 128 * dc:128 * (dc + 1)],
                                        P8[:, 128 * dc:128 * (dc + 1)],
                                        cst.ident_bf[:])
                return ins
            step("tensor", w, t_ctr, inc=cT)
            t_tr = (cT, cT.n)
            wv_ = [t_tr]
            if PT_free.get(0) is not None:
                wv_.append(PT_free[0])

            def v_ctc(eng):
                return eng.tensor_copy(PT8[:, 0:1024], ptp[:])
            step("vector", wv_, v_ctc, inc=cV)
            ctT_done = (cV, cV.n)
            ptp_free[0] = ctT_done
            P8_free = ctT_done
            for dh in range(2):
                w = [ctT_done]
                if psC_free[dh] is not None:
                    w.append(psC_free[dh])

                def t_fn(eng, dh=dh):
                    for dc in range(DC):
                        ins = eng.matmul(
                            psC[dh][:],
                            PT8[:, 128 * dc:128 * (dc + 1)],
                            woT[:, dc * D + 512 * dh:dc * D + 512 * (dh + 1)],
                            start=(dc == 0), stop=(dc == DC - 1))
                    return ins
                step("tensor", w, t_fn, inc=cT)
                omm = (cT, cT.n)
                wv_ = [omm]
                if ao_free.get(j - 2) is not None and dh == 0:
                    wv_.append(ao_free[j - 2])

                def v_fn(eng, j=j, dh=dh):
                    return eng.tensor_copy(
                        ao_stg[j % 2][:, 512 * dh:512 * (dh + 1)], psC[dh][:])
                step("vector", wv_, v_fn, inc=cV)
                psC_free[dh] = (cV, cV.n)
            PT_free[0] = (cV, cV.n)
            step("gpsimd", [(cV, cV.n)],
                 lambda eng, j=j: eng.dma_start(
                     out=ao_piece.ap()[128 * j:128 * (j + 1), :],
                     in_=ao_stg[j % 2][:]),
                 inc=cOD, inc_k=16)
            ao_free[j] = (cOD, cOD.n)
        epi_done = (cV, cV.n)

    def g_aocc(eng):
        eng.wait_ge(cOD.h, cOD.n)
        eng.collective_compute(
            "AllGather", mybir.AluOpType.bypass,
            replica_groups=[list(range(N_CORES))],
            ins=[ao_piece.ap().opt()], outs=[cc_ao[1].ap().opt()],
        ).then_inc(cc_ao[0].h, 1)
    p.on("gpsimd", g_aocc)
    cc_ao[0].inc()

    ag_done = {}
    xo_done = {}
    for m in range(16):
        w = [(cc_ao[0], 1), idx_done]
        if m >= 3:
            w.append(ag_done[m - 3])
        step("gpsimd", w, lambda eng, m=m: eng.indirect_dma_start(
            out=ao_stg[m % 3][:], out_offset=None,
            in_=AO_all[:, :],
            in_offset=IndirectOffsetOnAxis(aidx_u[:, m:m + 1], 0)),
            inc=cQG, inc_k=16)
        a_ld = (cQG, cQG.n)
        step("sync", [xo_done[m - 3]] if m >= 3 else [],
             lambda eng, m=m: eng.dma_start(
                 out=x_stg[m % 3][:], in_=x_src.ap()[128 * m:128 * (m + 1), :]),
             inc=cXL, inc_k=16)
        x_ld = (cXL, cXL.n)

        def v_add(eng, m=m):
            ins = eng.tensor_tensor(x_stg[m % 3][:], x_stg[m % 3][:],
                                    ao_stg[m % 3][:], op=ALU.add)
            eng.drain()
            return ins
        step("vector", [a_ld, x_ld], v_add, inc=cV)
        ag_done[m] = (cV, cV.n)
        step("gpsimd", [ag_done[m]],
             lambda eng, m=m: eng.dma_start(
                 out=x_dst.ap()[128 * m:128 * (m + 1), :],
                 in_=x_stg[m % 3][:]),
             inc=cOD, inc_k=16)
        xo_done[m] = (cOD, cOD.n)

    def g_fin(eng):
        eng.wait_ge(cOD.h, cOD.n)
    p.on("gpsimd", g_fin)

def emit_ln_route1(p, cst, x_src, rw_in, ln_piece, meta_piece, cc_ln, cc_meta,
                   g1car, g2car, pfx="d1"):
    x_sb = p.sb(pfx + "_x", [128, NT_SH * D], F32)
    ln_sb = p.sb(pfx + "_ln", [128, NT_SH * D], F32)
    rw_sb = p.sb(pfx + "_rw", [128, DC * E], F32)
    bn6 = p.sb(pfx + "_bn6", [128, 12], F32)
    stats = p.sb(pfx + "_st", [128, 2 * NT_SH], F32)
    rstd = p.sb(pfx + "_rs", [128, NT_SH], F32)

    sem_ld = p.dsem("sync"); sem_v2s = p.sem(); sem_s2v = p.sem()
    sem_lnd = p.sem(); sem_st = p.dsem("gpsimd"); semC = p.sem()

    t_ld = dma_in(p, "sync", [
        (rw_sb[:].rearrange("p (c e) -> p c e", e=E),
         rw_in.ap().rearrange("(c p) e -> p c e", p=128))], sem_ld)
    wtile = dma_x_chunks(p, x_sb, x_src, sem_ld)

    def xv(j):
        return x_sb[:, D * j:D * (j + 1)]

    def lv(j):
        return ln_sb[:, D * j:D * (j + 1)]

    t_ln = emit_ln_sweep(p, NT_SH, xv, lv, (sem_v2s, sem_s2v, sem_lnd),
                         (bn6, stats, rstd), wait_v=(sem_ld, t_ld),
                         wait_tile=wtile)

    def tw(eng):
        eng.wait_ge(sem_lnd.h, t_ln)
    p.on("tensor", tw)

    r = emit_route_l1(p, cst, lv, rw_sb, n_tiles=NT_SH, psb=p.psb,
                      rpfx=pfx + "q")

    def v_g(eng):
        eng.wait_ge(r["sem_done"].h, r["tgt_done"])
        eng.tensor_copy(g1car[:], r["g1"][:])
        ins = eng.tensor_copy(g2car[:], r["g2"][:])
        ins.then_inc(semC.h, 1)
        eng.drain()
    p.on("vector", v_g)
    tC = semC.inc()

    def g_out(eng):
        eng.wait_ge(sem_lnd.h, t_ln)
        eng.dma_start(out=ln_piece.ap().rearrange("(j p) d -> p j d", p=128),
                      in_=ln_sb[:].rearrange("p (j d) -> p j d", d=D)
                      ).then_inc(sem_st.h, 16)
        eng.wait_ge(sem_st.h, 16)
        eng.collective_compute(
            "AllGather", mybir.AluOpType.bypass,
            replica_groups=[list(range(N_CORES))],
            ins=[ln_piece.ap().opt()], outs=[cc_ln[1].ap().opt()],
        ).then_inc(cc_ln[0].h, 1)
        eng.wait_ge(r["sem_done"].h, r["tgt_done"])
        eng.dma_start(out=meta_piece[:, 0:128], in_=r["gates"][:]).then_inc(sem_st.h, 16)
        eng.dma_start(out=meta_piece[:, 128:144], in_=r["i1f"][:]).then_inc(sem_st.h, 16)
        eng.dma_start(out=meta_piece[:, 144:160], in_=r["i2f"][:]).then_inc(sem_st.h, 16)
        eng.wait_ge(sem_st.h, 16 * 4)
        eng.wait_ge(semC.h, tC)
        eng.collective_compute(
            "AllGather", mybir.AluOpType.bypass,
            replica_groups=[list(range(N_CORES))],
            ins=[meta_piece.ap().opt()], outs=[cc_meta[1].ap().opt()],
        ).then_inc(cc_meta[0].h, 1)
    p.on("gpsimd", g_out)
    sem_st.inc(16 * 4)
    cc_ln[0].inc()
    cc_meta[0].inc()


def emit_route1_only(p, cst, ln_sb, rw_in, meta_piece, cc_meta,
                     g1car, g2car, pfx="d1"):
    rw_sb = p.sb(pfx + "_rw", [128, DC * E], F32)
    sem_ld = p.dsem("sync"); sem_st = p.dsem("gpsimd"); semC = p.sem()
    t_ld = dma_in(p, "sync", [
        (rw_sb[:].rearrange("p (c e) -> p c e", e=E),
         rw_in.ap().rearrange("(c p) e -> p c e", p=128))], sem_ld)

    def lv(j):
        return ln_sb[:, D * j:D * (j + 1)]

    def tw(eng):
        eng.wait_ge(sem_ld.h, t_ld)
    p.on("tensor", tw)

    r = emit_route_l1(p, cst, lv, rw_sb, n_tiles=NT_SH, psb=p.psb,
                      rpfx=pfx + "q")

    def v_g(eng):
        eng.wait_ge(r["sem_done"].h, r["tgt_done"])
        eng.tensor_copy(g1car[:], r["g1"][:])
        ins = eng.tensor_copy(g2car[:], r["g2"][:])
        ins.then_inc(semC.h, 1)
        eng.drain()
    p.on("vector", v_g)
    tC = semC.inc()

    def g_out(eng):
        eng.wait_ge(r["sem_done"].h, r["tgt_done"])
        eng.dma_start(out=meta_piece[:, 0:128], in_=r["gates"][:]).then_inc(sem_st.h, 16)
        eng.dma_start(out=meta_piece[:, 128:144], in_=r["i1f"][:]).then_inc(sem_st.h, 16)
        eng.dma_start(out=meta_piece[:, 144:160], in_=r["i2f"][:]).then_inc(sem_st.h, 16)
        eng.wait_ge(sem_st.h, 16 * 3)
        eng.wait_ge(semC.h, tC)
        eng.collective_compute(
            "AllGather", mybir.AluOpType.bypass,
            replica_groups=[list(range(N_CORES))],
            ins=[meta_piece.ap().opt()], outs=[cc_meta[1].ap().opt()],
        ).then_inc(cc_meta[0].h, 1)
    p.on("gpsimd", g_out)
    sem_st.inc(16 * 3)
    cc_meta[0].inc()


def emit_expert_l1(p, cst, gw_in, uw_in, dw_in, ecol_in, ohsel_in,
                   cc_ln, cc_meta, cc_ye, ye_piece, slot_tab, cnt_rt,
                   xeT_dram, h_dram, fpcar1, fpcar2, cc=256, niter=24,
                   pfx="e1"):
    ln_full = cc_ln[1]
    meta_all = cc_meta[1]
    NCC = CAP1 // cc
    CT = CAP1 // 128
    psb = p.psb
    gw_sb = p.sb(pfx + "_gw", [128, DC * FF], BF16)
    uw_sb = p.sb(pfx + "_uw", [128, DC * FF], BF16)
    gates_sb = p.sb(pfx + "_gates", [128, NT_FULL * 8], F32)
    i1_sb = p.sb(pfx + "_i1", [128, NT_FULL], F32)
    i2_sb = p.sb(pfx + "_i2", [128, NT_FULL], F32)
    ecol = p.sb(pfx + "_ecol", [128, 1], F32)
    ones_row = p.sb(pfx + "_or", [1, 128], BF16)
    ones_row_f = p.sb(pfx + "_orf", [1, 128], F32)
    lohi = p.sb(pfx + "_lohi", [1, 24], F32)
    bdbg = p.sb(pfx + "_bdbg", [1, 3 * 28], F32)
    lohi2 = p.sb(pfx + "_lohi2", [1, 16], F32)
    thbc = p.sb(pfx + "_thbc", [128, 8], F32)
    cmp_s = p.sb(pfx + "_cmps", [128, 128], F32)
    cntp = p.sb(pfx + "_cntp", [128, 8], F32)
    cnt_row = p.sb(pfx + "_cntr", [1, 8], F32)
    over_u = p.sb(pfx + "_over", [1, 8], U32)
    keep_t = p.sb(pfx + "_keept", [128, 8], F32)
    cnt_tile = p.sb(pfx + "_cntt", [1, NT_FULL * 8], F32)
    cnt_pm = p.sb(pfx + "_cntpm", [128, 8], F32)
    cnt_pm_bf = p.sb(pfx + "_cntpmb", [128, 8], BF16)
    offsT = p.sb(pfx + "_offsT", [8, NT_FULL], F32)
    fp_sb = [p.sb(pfx + f"_fp{i}", [128, NT_FULL], F32) for i in range(2)]
    zero_cap = p.sb(pfx + "_zc", [128, (CAP1 * 2 + 127) // 128], F32)
    gidx_u = p.sb(pfx + "_gidx", [128, CT], U32)
    sl_sb = p.sb(pfx + "_sl", [128, CT * 2], F32)
    xc_stg = [p.sb(pfx + f"_xc{i}", [128, DC * cc], BF16) for i in range(2)]
    s1_stg = p.sb(pfx + "_s1", [128, 2 * cc], BF16)
    hcc_stg = [p.sb(pfx + f"_hcc{i}", [128, (FT // 2) * cc], BF16) for i in range(2)]
    hcol_stg = [p.sb(pfx + f"_hc{i}", [128, FT * 128], BF16) for i in range(2)]
    xe_stg = [hcol_stg[i][:, 0:D] for i in range(2)]
    xt_stg = [hcol_stg[i][:, D:2 * D] for i in range(2)]
    y_stg = [p.sb(pfx + f"_y{i}", [128, D], BF16) for i in range(2)]
    ohsel_sb = p.sb(pfx + "_ohsel", [128, 16], F32)
    fpT_sb = p.sb(pfx + "_fpT", [128, 128], F32)
    ofpT_sb = p.sb(pfx + "_ofpT", [16, 128], F32)
    psm = psb["F0"][:, 0:256]
    ptr = [psb["B0"][:, 0:D], psb["B1"][:, 0:D]]
    ph1 = psb["F1"][:, 0:cc]
    ph2 = psb["F2"][:, 0:cc]
    py = [psb["F3"][:, 0:512], psb["F1024"][:, 0:512]]
    ppos = [ph1[:, 0:8], ph2[:, 0:8]]

    sem_ld = p.dsem("sync")
    sem_w = p.dsem("gpsimd")
    semV = p.sem()
    semT = p.sem()
    semB = p.sem()
    semVp = p.sem()
    semZP = p.sem()
    semZF = p.dsem("gpsimd")
    semSC = p.dsem("gpsimd")
    semSL = p.dsem("gpsimd")
    semGs = [p.dsem("gpsimd") for _ in range(2)]
    semTT = p.sem()
    semXTd = [p.dsem("gpsimd") for _ in range(2)]
    semXC = [p.sem() for _ in range(2)]
    semH = p.sem()
    semHEs = p.sem()
    semHEv = p.sem()
    semHC = [p.sem() for _ in range(2)]
    semY = p.sem()
    semYE = p.sem()
    sem_yd = [p.dsem("sync") for _ in range(2)]
    semFP = p.sem()

    def d_in(eng):
        eng.wait_ge(cc_meta[0].h, 1)
        ma = meta_all.ap().rearrange("(c p) x -> p c x", p=128)
        eng.dma_start(out=gates_sb[:].rearrange("p (c m) -> p c m", m=128),
                      in_=ma[:, :, 0:128]).then_inc(sem_ld.h, 16)
        eng.dma_start(out=i1_sb[:].rearrange("p (c m) -> p c m", m=16),
                      in_=ma[:, :, 128:144]).then_inc(sem_ld.h, 16)
        eng.dma_start(out=i2_sb[:].rearrange("p (c m) -> p c m", m=16),
                      in_=ma[:, :, 144:160]).then_inc(sem_ld.h, 16)
        eng.dma_start(out=ecol[:], in_=ecol_in[:, :]).then_inc(sem_ld.h, 16)
        eng.dma_start(out=ohsel_sb[:], in_=ohsel_in[:, :]).then_inc(sem_ld.h, 16)
    p.on("sync", d_in)
    t_ld = sem_ld.inc(16 * 5)

    def d_w(eng):
        eng.dma_start(out=gw_sb[:].rearrange("p (c f) -> p c f", f=FF),
                      in_=gw_in.ap().rearrange("(c p) f -> p c f", p=128)
                      ).then_inc(sem_w.h, 16)
        eng.dma_start(out=uw_sb[:].rearrange("p (c f) -> p c f", f=FF),
                      in_=uw_in.ap().rearrange("(c p) f -> p c f", p=128)
                      ).then_inc(sem_w.h, 16)
    p.on("gpsimd", d_w)
    t_w = sem_w.inc(32)

    def g_prep(eng):
        eng.memset(ones_row[:], 1.0)
        eng.memset(ones_row_f[:], 1.0)
        eng.drain()
        eng.memset(zero_cap[:], 0.0)
        eng.drain()
        eng.memset(lohi[:, 0:8], 0.0)
        eng.drain()
        ins = eng.memset(lohi[:, 8:16], 1.0)
        ins.then_inc(semZP.h, 1)
        eng.drain()
    p.on("gpsimd", g_prep)
    tZP = semZP.inc()

    def v_bis(eng):
        eng.wait_ge(sem_ld.h, t_ld)
        eng.wait_ge(semZP.h, tZP)
        for it in range(niter):
            eng.tensor_tensor(lohi[:, 16:24], lohi[:, 0:8], lohi[:, 8:16],
                              op=ALU.add)
            eng.drain()
            eng.tensor_scalar(lohi[:, 16:24], lohi[:, 16:24], 0.5, None,
                              op0=ALU.mult).then_inc(semV.h, 1)
            eng.drain()
            eng.wait_ge(semT.h, 2 * it + 1)
            eng.tensor_copy(thbc[:], psm[:, 0:8])
            eng.drain()
            for e in range(E):
                ins = eng.tensor_scalar(
                    cmp_s[:], gates_sb[:].rearrange("p (j e) -> p e j", e=8)[:, e],
                    thbc[:, e:e + 1], None, op0=ALU.is_gt, op1=ALU.add,
                    accum_out=cntp[:, e:e + 1])
                eng.drain()
            ins.then_inc(semV.h, 1)
            eng.wait_ge(semT.h, 2 * it + 2)
            eng.tensor_copy(cnt_row[:], psm[0:1, 8:16])
            eng.drain()
            eng.tensor_scalar(over_u[:], cnt_row[:], float(CAP1), None,
                              op0=ALU.is_gt)
            eng.drain()
            eng.tensor_copy(bdbg[:, 3 * it:3 * it + 1], lohi[:, 16:17])
            eng.tensor_copy(bdbg[:, 3 * it + 1:3 * it + 2], cnt_row[:, 0:1])
            eng.drain()
            eng.select(lohi2[:, 0:8], over_u[:], lohi[:, 16:24], lohi[:, 0:8],
                       add_drain=True)
            eng.drain()
            eng.select(lohi2[:, 8:16], over_u[:], lohi[:, 8:16], lohi[:, 16:24],
                       add_drain=True)
            eng.drain()
            eng.tensor_copy(lohi[:, 0:16], lohi2[:])
            eng.drain()
        eng.tensor_copy(lohi[:, 16:24], lohi[:, 8:16]).then_inc(semV.h, 1)
        eng.drain()
        eng.wait_ge(semT.h, 2 * niter + 1)
        ins = eng.tensor_copy(thbc[:], psm[:, 0:8])
        ins.then_inc(semV.h, 1)
        eng.drain()
    p.on("vector", v_bis)
    semV.inc(2 * niter + 2)

    def t_bis(eng):
        eng.wait_ge(cst.sem.h, cst.tgt)
        for it in range(niter):
            eng.wait_ge(semV.h, 2 * it + 1)
            eng.matmul(psm[:, 0:8], ones_row_f[0:1, 0:128], lohi[0:1, 16:24],
                       start=True, stop=True).then_inc(semT.h, 1)
            eng.wait_ge(semV.h, 2 * it + 2)
            eng.matmul(psm[0:1, 8:16], cst.ones_f32[:], cntp[:],
                       start=True, stop=True).then_inc(semT.h, 1)
        eng.wait_ge(semV.h, 2 * niter + 1)
        eng.matmul(psm[:, 0:8], ones_row_f[0:1, 0:128], lohi[0:1, 16:24],
                   start=True, stop=True).then_inc(semT.h, 1)
        semT.inc(2 * niter + 1)
    p.on("tensor", t_bis)

    semKT = p.sem()
    semKC = p.sem()
    semKD = p.dsem("gpsimd")
    keep_all = p.sb(pfx + "_keepall", [128, NT_FULL * 8], BF16)

    def v_keep(eng):
        eng.wait_ge(semV.h, 2 * niter + 2)
        for j in range(NT_FULL):
            eng.tensor_tensor(keep_t[:], gates_sb[:, 8 * j:8 * (j + 1)],
                              thbc[:], op=ALU.is_gt)
            eng.drain()
            ins = eng.tensor_copy(keep_all[:, 8 * j:8 * (j + 1)], keep_t[:])
            ins.then_inc(semKT.h, 1)
            eng.drain()
        semKT.inc(NT_FULL)
    p.on("vector", v_keep)

    def t_keep(eng):
        for g in range(NT_FULL // 8):
            if g >= 1:
                eng.wait_ge(semKC.h, 2 * g)
            for jj in range(8):
                j = 8 * g + jj
                eng.wait_ge(semKT.h, j + 1)
                ins = eng.matmul(psm[0:1, 32 + 8 * jj:40 + 8 * jj],
                                 cst.ones_bf[:], keep_all[:, 8 * j:8 * (j + 1)],
                                 start=True, stop=True)
            ins.then_inc(semKC.h, 1)
            semKC.inc()
    p.on("tensor", t_keep)

    def v_keep2(eng):
        for g in range(NT_FULL // 8):
            eng.wait_ge(semKC.h, 2 * g + 1)
            ins = eng.tensor_copy(cnt_tile[0:1, 64 * g:64 * (g + 1)],
                                  psm[0:1, 32:96])
            ins.then_inc(semKC.h, 1)
            semKC.inc()
            eng.drain()
    p.on("vector", v_keep2)

    def g_keep(eng):
        eng.wait_ge(semKC.h, 2 * (NT_FULL // 8))
        eng.dma_start(out=cnt_rt.ap().rearrange("(o j) e -> o (j e)", o=1),
                      in_=cnt_tile[:]).then_inc(semKD.h, 16)
        eng.wait_ge(semKD.h, 16)
        eng.dma_start(out=cnt_pm[:], in_=cnt_rt[:, :]).then_inc(semKD.h, 16)
        eng.wait_ge(semKD.h, 32)
    p.on("gpsimd", g_keep)
    semKD.inc(32)

    semOF = p.sem()

    def v_off(eng):
        eng.wait_ge(semKD.h, 32)
        ins = eng.tensor_copy(cnt_pm_bf[:], cnt_pm[:])
        ins.then_inc(semOF.h, 1)
        eng.drain()
    p.on("vector", v_off)
    tOF1 = semOF.inc()

    def t_off(eng):
        eng.wait_ge(semOF.h, tOF1)
        eng.matmul(psm[0:8, 128:256], cnt_pm_bf[:], cst.tri_bf[:],
                   start=True, stop=True).then_inc(semOF.h, 1)
    p.on("tensor", t_off)
    tOF2 = semOF.inc()

    def v_off2(eng):
        eng.wait_ge(semOF.h, tOF2)
        ins = eng.tensor_copy(offsT[:], psm[0:8, 128:256])
        ins.then_inc(semOF.h, 1)
        eng.drain()
    p.on("vector", v_off2)
    tOF3 = semOF.inc()

    wcum_f = p.sb(pfx + "_wcumf", [128, NT_FULL * 8], F32)
    posall = p.sb(pfx + "_posall", [128, NT_FULL * 8], F32)
    emask_b = p.sb(pfx + "_emb", [128, NT_FULL * 8], BF16)
    tmul = wcum_f
    tokid = p.sb(pfx + "_tokid", [128, NT_FULL], F32)
    oob128 = p.sb(pfx + "_oob", [128, NT_FULL], F32)
    pos_cS = p.sb(pfx + "_poscS", [128, NT_FULL], F32)
    keep_cS = p.sb(pfx + "_keepcS", [128, NT_FULL], F32)
    selS = p.sb(pfx + "_selS", [128, NT_FULL], F32)
    selU = p.sb(pfx + "_selU", [128, NT_FULL], U32)
    t3S = selS
    kuS = selU
    remS = None
    slotS = p.sb(pfx + "_slotS", [128, NT_FULL], F32)
    remS = slotS
    slot_uA = p.sb(pfx + "_slotuA", [128, 2 * NT_FULL], U32)
    tokslot = p.sb(pfx + "_tokslot", [128, 4 * NT_FULL], F32)
    offsJ = p.sb(pfx + "_offsJ", [128, 8], F32)
    offs_row = cnt_tile
    offs_rt = p.dram(pfx + "_offsrt", [1, NT_FULL * 8], F32)
    pw1 = psb["F1"][:, 0:512]
    pw2 = psb["F2"][:, 0:512]
    pb1 = psb["F3"][:, 0:512]

    semPB = p.sem()
    semPD = p.dsem("gpsimd")

    def g_pprep(eng):
        eng.iota(tokid[:], pattern=[[128, NT_FULL]], base=0,
                 channel_multiplier=1, allow_small_or_imprecise_dtypes=True)
        eng.drain()
        ins = eng.memset(oob128[:], float(OOBV))
        ins.then_inc(semPB.h, 1)
        eng.drain()
    p.on("gpsimd", g_pprep)
    tPBprep = semPB.inc()

    def t_posb(eng):
        eng.wait_ge(semOF.h, tOF3)
        eng.wait_ge(semKT.h, NT_FULL)
        eng.matmul(pw1[:], cst.tri_bf[:], keep_all[:, 0:512],
                   start=True, stop=True)
        eng.matmul(pw2[:], cst.tri_bf[:], keep_all[:, 512:1024],
                   start=True, stop=True)
        ins = eng.transpose(psm[:, 0:8], offsT[:], cst.ident_f32[0:8, 0:8])
        ins.then_inc(semPB.h, 1)
    p.on("tensor", t_posb)
    tPB1 = semPB.inc()

    def v_posb1(eng):
        eng.wait_ge(semPB.h, tPB1)
        eng.tensor_copy(wcum_f[:, 0:512], pw1[:])
        eng.tensor_copy(wcum_f[:, 512:1024], pw2[:])
        ins = eng.tensor_copy(offsJ[:], psm[:, 0:8])
        ins.then_inc(semPB.h, 1)
        eng.drain()
    p.on("vector", v_posb1)
    tPB2 = semPB.inc()

    def g_offs(eng):
        eng.wait_ge(semPB.h, tPB2)
        eng.dma_start(out=offs_rt.ap().rearrange("o (j e) -> (o j) e", e=8),
                      in_=offsJ[:]).then_inc(semPD.h, 16)
        eng.wait_ge(semPD.h, 16)
        eng.dma_start(out=offs_row[:], in_=offs_rt[0:1, :]).then_inc(semPD.h, 16)
    p.on("gpsimd", g_offs)
    tPD = semPD.inc(32)

    def t_posb2(eng):
        eng.wait_ge(semPD.h, tPD)
        eng.matmul(pb1[:], ones_row_f[0:1, 0:128], offs_row[0:1, 0:512],
                   start=True, stop=True)
        ins = eng.matmul(pw1[:], ones_row_f[0:1, 0:128], offs_row[0:1, 512:1024],
                         start=True, stop=True)
        ins.then_inc(semPB.h, 1)
    p.on("tensor", t_posb2)
    tPB3 = semPB.inc()

    def v_posb2(eng):
        eng.wait_ge(semPB.h, tPB3)
        eng.wait_ge(semPB.h, tPBprep)
        eng.tensor_tensor(posall[:, 0:512], wcum_f[:, 0:512], pb1[:], op=ALU.add)
        eng.drain()
        eng.tensor_tensor(posall[:, 512:1024], wcum_f[:, 512:1024], pw1[:],
                          op=ALU.add)
        eng.drain()
        for s in range(2):
            ib = i1_sb if s == 0 else i2_sb
            ib3 = bass.AP(ib[:].tensor, ib[:].offset,
                          [[ib[:].ap[0][0], 128], [1, NT_FULL], [0, 8]])
            io8 = cst.iota8[:]
            io3 = bass.AP(io8.tensor, io8.offset,
                          [[io8.ap[0][0], 128], [0, NT_FULL], [1, 8]])
            eng.tensor_tensor(emask_b[:].rearrange("p (j e) -> p j e", e=8),
                              io3, ib3, op=ALU.is_equal)
            eng.drain()
            eng.tensor_tensor(tmul[:], posall[:], emask_b[:], op=ALU.mult)
            eng.drain()
            tv = tmul[:].rearrange("p (j e) -> p j e", e=8)
            eng.tensor_tensor(pos_cS[:], tv[:, :, 0], tv[:, :, 1], op=ALU.add)
            eng.drain()
            for e2 in range(2, 8):
                eng.tensor_tensor(pos_cS[:], pos_cS[:], tv[:, :, e2], op=ALU.add)
                eng.drain()
            eng.tensor_tensor(tmul[:], emask_b[:], keep_all[:], op=ALU.mult)
            eng.drain()
            eng.tensor_tensor(keep_cS[:], tv[:, :, 0], tv[:, :, 1], op=ALU.add)
            eng.drain()
            for e2 in range(2, 8):
                eng.tensor_tensor(keep_cS[:], keep_cS[:], tv[:, :, e2], op=ALU.add)
                eng.drain()
            eng.tensor_scalar(selS[:], ib[:], ecol[:], None, op0=ALU.is_equal)
            eng.drain()
            eng.tensor_tensor(selS[:], selS[:], keep_cS[:], op=ALU.mult)
            eng.drain()
            eng.tensor_copy(selU[:], selS[:])
            eng.drain()
            eng.select(slotS[:], selU[:], pos_cS[:], oob128[:], add_drain=True)
            eng.drain()
            ts4 = tokslot[:].rearrange("p (j x) -> p j x", x=4)
            suA = slot_uA[:].rearrange("p (j s) -> p j s", s=2)
            eng.tensor_copy(ts4[:, :, 2 * s + 1], slotS[:])
            eng.tensor_copy(suA[:, :, s], slotS[:])
            eng.drain()
            eng.tensor_scalar(t3S[:], pos_cS[:], 2560.0, None, op0=ALU.is_ge)
            eng.drain()
            eng.tensor_scalar(t3S[:], t3S[:], 17920.0, None, op0=ALU.mult)
            eng.drain()
            eng.tensor_tensor(t3S[:], t3S[:], pos_cS[:], op=ALU.add)
            eng.drain()
            eng.tensor_scalar(remS[:], ib[:], 2560.0, None, op0=ALU.mult)
            eng.drain()
            eng.tensor_tensor(t3S[:], t3S[:], remS[:], op=ALU.add)
            eng.tensor_copy(kuS[:], keep_cS[:])
            eng.drain()
            eng.select(fp_sb[s][:], kuS[:], t3S[:], oob128[:], add_drain=True)
            eng.drain()
        ts4 = tokslot[:].rearrange("p (j x) -> p j x", x=4)
        eng.tensor_copy(ts4[:, :, 0], tokid[:])
        ins = eng.tensor_copy(ts4[:, :, 2], tokid[:])
        ins.then_inc(semVp.h, 1)
        eng.drain()
    p.on("vector", v_posb2)
    tVp = semVp.inc()

    def g_scat(eng):
        eng.wait_ge(semZP.h, tZP)
        eng.dma_start(out=slot_tab.ap().rearrange("(j p) x -> p j x", p=128),
                      in_=zero_cap[:].rearrange("p (j x) -> p j x", x=2)[:, 0:CAP1 // 128, :]
                      ).then_inc(semZF.h, 16)
        eng.wait_ge(semZF.h, 16)
        eng.wait_ge(semVp.h, tVp)
        with eng.register(pfx + "_bcs") as bc:
            eng.reg_mov(bc, CAP1 - 1)
            for u in range(2 * NT_FULL):
                if u >= 12:
                    eng.wait_ge(semSC.h, IDMA_INC * (u - 11))
                eng.indirect_dma_start(
                    out=slot_tab[:, :],
                    out_offset=IndirectOffsetOnAxis(slot_uA[:, u:u + 1], 0),
                    in_=tokslot[:, 2 * u:2 * u + 2], in_offset=None,
                    bounds_check=bc, oob_is_err=False,
                ).then_inc(semSC.h, 16)
    p.on("gpsimd", g_scat)
    semZF.inc(16)
    t_sc = semSC.inc(IDMA_INC * 2 * NT_FULL)

    def g_gath(eng):
        eng.wait_ge(cc_ln[0].h, 1)
        eng.wait_ge(semSC.h, t_sc)
        eng.dma_start(out=sl_sb[:].rearrange("p (j x) -> p j x", x=2),
                      in_=slot_tab.ap().rearrange("(j p) x -> p j x", p=128)
                      ).then_inc(semSL.h, 16)
        eng.wait_ge(semSL.h, 16)
        eng.wait_ge(semOF.h, tOF3 + 1)
        for j in range(CT + 2):
            if j < CT:
                if j >= 2:
                    eng.wait_ge(semTT.h, j - 1)
                eng.indirect_dma_start(
                    out=xe_stg[j % 2], out_offset=None,
                    in_=ln_full[:, :],
                    in_offset=IndirectOffsetOnAxis(gidx_u[:, j:j + 1], 0),
                ).then_inc(semGs[j % 2].h, 16)
            if j >= 2:
                jo = j - 2
                eng.wait_ge(semXC[jo % 2].h, jo // 2 + 1)
                eng.dma_start(
                    out=xeT_dram.ap().rearrange("(c p) t -> p c t", p=128)[:, :, 128 * jo:128 * (jo + 1)],
                    in_=xt_stg[jo % 2].rearrange("p (c t) -> p c t", t=128)
                    ).then_inc(semXTd[jo % 2].h, 16)
    p.on("gpsimd", g_gath)
    semSL.inc(16)
    for s in range(2):
        semGs[s].inc(IDMA_INC * ((CT + 1 - s) // 2))
        semXTd[s].inc(16 * ((CT + 1 - s) // 2))

    def v_gidx2(eng):
        eng.wait_ge(semSL.h, 16)
        ins = eng.tensor_copy(gidx_u[:], sl_sb[:].rearrange("p (j x) -> p j x", x=2)[:, :, 0])
        ins.then_inc(semOF.h, 1)
        eng.drain()
    p.on("vector", v_gidx2)
    semOF.inc()

    def t_tr2(eng):
        for j in range(CT):
            eng.wait_ge(semGs[j % 2].h, IDMA_INC * (j // 2 + 1))
            if j >= 2:
                eng.wait_ge(semXC[j % 2].h, j // 2)
            for dc in range(DC):
                ins = eng.transpose(ptr[j % 2][:, 128 * dc:128 * (dc + 1)],
                                    xe_stg[j % 2][:, 128 * dc:128 * (dc + 1)],
                                    cst.ident_bf[:])
            ins.then_inc(semTT.h, 1)
            semTT.inc()
    p.on("tensor", t_tr2)

    def v_xt2(eng):
        for j in range(CT):
            eng.wait_ge(semTT.h, j + 1)
            if j >= 2:
                eng.wait_ge(semXTd[j % 2].h, 16 * (j // 2))
            ins = eng.tensor_copy(xt_stg[j % 2], ptr[j % 2][:])
            ins.then_inc(semXC[j % 2].h, 1)
            eng.drain()
    p.on("vector", v_xt2)
    for s in range(2):
        semXC[s].inc((CT + 1 - s) // 2)

    HB = FT // 2
    sem_xc = [p.dsem("sync") for _ in range(2)]
    sem_hd = [p.dsem("gpsimd") for _ in range(2)]

    def s_xcl(eng):
        for c in range(NCC):
            jo_last = min(2 * c + 1, CT - 1)
            for s in range(2):
                need = (jo_last + 2 - s) // 2
                eng.wait_ge(semXTd[s].h, 16 * min(need, (CT + 1 - s) // 2))
            if c >= 2:
                eng.wait_ge(semH.h, (c - 1) * FT)
            eng.dma_start(
                out=xc_stg[c % 2][:].rearrange("p (k t) -> p k t", t=cc),
                in_=xeT_dram.ap().rearrange("(k p) t -> p k t", p=128)[:, :, cc * c:cc * (c + 1)]
                ).then_inc(sem_xc[c % 2].h, 16)
    p.on("sync", s_xcl)
    for s in range(2):
        sem_xc[s].inc(16 * ((NCC + 1 - s) // 2))

    def t_ffn1(eng):
        eng.wait_ge(sem_w.h, t_w)
        for c in range(NCC):
            eng.wait_ge(sem_xc[c % 2].h, 16 * (c // 2 + 1))
            for ft in range(FT):
                pg = ph1 if ft % 2 == 0 else py[0][:, 0:cc]
                pu = ph2 if ft % 2 == 0 else py[1][:, 0:cc]
                u = c * FT + ft
                if u >= 2:
                    eng.wait_ge(semHEs.h, u - 1)
                    eng.wait_ge(semHEv.h, u - 1)
                for dc in range(DC):
                    eng.matmul(pg[:, 0:cc] if ft % 2 == 0 else pg,
                               gw_sb[:, dc * FF + 128 * ft:dc * FF + 128 * (ft + 1)],
                               xc_stg[c % 2][:, dc * cc:(dc + 1) * cc],
                               start=(dc == 0), stop=(dc == DC - 1))
                for dc in range(DC):
                    ins = eng.matmul(pu[:, 0:cc] if ft % 2 == 0 else pu,
                                     uw_sb[:, dc * FF + 128 * ft:dc * FF + 128 * (ft + 1)],
                                     xc_stg[c % 2][:, dc * cc:(dc + 1) * cc],
                                     start=(dc == 0), stop=(dc == DC - 1))
                ins.then_inc(semH.h, 1)
                semH.inc()
    p.on("tensor", t_ffn1)

    def s_silu(eng):
        for c in range(NCC):
            for ft in range(FT):
                u = c * FT + ft
                pg = ph1 if ft % 2 == 0 else py[0][:, 0:cc]
                eng.wait_ge(semH.h, u + 1)
                if u >= 2:
                    eng.wait_ge(semHEv.h, u - 1)
                ins = eng.activation(s1_stg[:, cc * (ft % 2):cc * (ft % 2 + 1)],
                                     pg[:, 0:cc] if ft % 2 == 0 else pg,
                                     AF.Silu)
                ins.then_inc(semHEs.h, 1)
                eng.drain()
        semHEs.inc(NCC * FT)
    p.on("scalar", s_silu)

    def v_hmul(eng):
        for c in range(NCC):
            for ft in range(FT):
                u = c * FT + ft
                pu = ph2 if ft % 2 == 0 else py[1][:, 0:cc]
                eng.wait_ge(semHEs.h, u + 1)
                half = ft // HB
                if ft % HB == 0 and (2 * c + half) >= 2:
                    eng.wait_ge(sem_hd[(2 * c + half) % 2].h,
                                16 * ((2 * c + half) // 2))
                ins = eng.tensor_tensor(
                    hcc_stg[(2 * c + half) % 2][:, cc * (ft % HB):cc * (ft % HB + 1)],
                    s1_stg[:, cc * (ft % 2):cc * (ft % 2 + 1)],
                    pu[:, 0:cc] if ft % 2 == 0 else pu, op=ALU.mult)
                ins.then_inc(semHEv.h, 1)
                eng.drain()
        semHEv.inc(NCC * FT)
    p.on("vector", v_hmul)

    def g_hout(eng):
        for c in range(NCC):
            for half in range(2):
                eng.wait_ge(semHEv.h, c * FT + half * HB + HB)
                eng.dma_start(
                    out=h_dram.ap().rearrange("(k p) t -> p k t", p=128)[:, half * HB:(half + 1) * HB, cc * c:cc * (c + 1)],
                    in_=hcc_stg[(2 * c + half) % 2][:].rearrange("p (k t) -> p k t", t=cc)
                    ).then_inc(sem_hd[(2 * c + half) % 2].h, 16)
    p.on("gpsimd", g_hout)
    for s in range(2):
        sem_hd[s].inc(16 * NCC)

    sem_dw = p.dsem("gpsimd")
    sem_hc = [p.dsem("gpsimd") for _ in range(2)]

    def g_dw(eng):
        eng.wait_ge(semH.h, NCC * FT)
        eng.dma_start(out=gw_sb[:].rearrange("p (c d) -> p c d", d=D),
                      in_=dw_in.ap().rearrange("(c p) d -> p c d", p=128)
                      ).then_inc(sem_dw.h, 16)
        for s in range(2):
            eng.wait_ge(sem_hd[s].h, 16 * NCC)
        for ct in range(CT):
            if ct >= 2:
                eng.wait_ge(semY.h, ct - 1)
            eng.dma_start(
                out=hcol_stg[ct % 2][:].rearrange("p (k t) -> p k t", t=128),
                in_=h_dram.ap().rearrange("(k p) t -> p k t", p=128)[:, :, 128 * ct:128 * (ct + 1)]
                ).then_inc(sem_hc[ct % 2].h, 16)
    p.on("gpsimd", g_dw)
    sem_dw.inc(16)
    for s in range(2):
        sem_hc[s].inc(16 * ((CT + 1 - s) // 2))

    def t_ffn2(eng):
        eng.wait_ge(sem_dw.h, 16)
        for ct in range(CT):
            eng.wait_ge(sem_hc[ct % 2].h, 16 * (ct // 2 + 1))
            if ct >= 1:
                eng.wait_ge(semYE.h, ct)
            for dh in range(2):
                for ft in range(FT):
                    ins = eng.matmul(
                        py[dh][:],
                        hcol_stg[ct % 2][:, 128 * ft:128 * (ft + 1)],
                        gw_sb[:, ft * D + 512 * dh:ft * D + 512 * (dh + 1)],
                        start=(ft == 0), stop=(ft == FT - 1))
            ins.then_inc(semY.h, 1)
            semY.inc()
        eng.wait_ge(semVp.h, 1)
        for s in range(2):
            eng.wait_ge(semFP.h, 6 * s)
            eng.transpose(psm[:, 0:128], fp_sb[s][:], cst.ident_f32[:]).then_inc(semFP.h, 1)
            eng.wait_ge(semFP.h, 6 * s + 2)
            eng.matmul(psm[0:16, 128:256], ohsel_sb[:], fpT_sb[:],
                       start=True, stop=True).then_inc(semFP.h, 1)
            eng.wait_ge(semFP.h, 6 * s + 4)
            eng.transpose(psm[:, 128:144], ofpT_sb[:], cst.ident_f32[0:16, 0:16]).then_inc(semFP.h, 1)
    p.on("tensor", t_ffn2)

    def v_ye2(eng):
        for ct in range(CT):
            eng.wait_ge(semY.h, ct + 1)
            if ct >= 2:
                eng.wait_ge(sem_yd[ct % 2].h, 16 * (ct // 2))
            eng.tensor_copy(y_stg[ct % 2][:, 0:512], py[0][:])
            ins = eng.tensor_copy(y_stg[ct % 2][:, 512:1024], py[1][:])
            ins.then_inc(semYE.h, 1)
            eng.drain()
        semYE.inc(CT)
    p.on("vector", v_ye2)

    def v_fp2(eng):
        for s in range(2):
            fpc = fpcar1 if s == 0 else fpcar2
            eng.wait_ge(semFP.h, 6 * s + 1)
            ins = eng.tensor_copy(fpT_sb[:], psm[:, 0:128])
            ins.then_inc(semFP.h, 1)
            eng.drain()
            eng.wait_ge(semFP.h, 6 * s + 3)
            ins = eng.tensor_copy(ofpT_sb[:], psm[0:16, 128:256])
            ins.then_inc(semFP.h, 1)
            eng.drain()
            eng.wait_ge(semFP.h, 6 * s + 5)
            ins = eng.tensor_copy(fpc[:], psm[:, 128:144])
            ins.then_inc(semFP.h, 1)
            eng.drain()
    p.on("vector", v_fp2)
    t_fp = semFP.inc(12)

    def g_yout2(eng):
        for ct in range(CT):
            eng.wait_ge(semYE.h, ct + 1)
            eng.dma_start(out=ye_piece[128 * ct:128 * (ct + 1), :],
                          in_=y_stg[ct % 2][:]).then_inc(sem_yd[ct % 2].h, 16)
        for s in range(2):
            eng.wait_ge(sem_yd[s].h, 16 * ((CT + 1 - s) // 2))
    p.on("sync", g_yout2)
    for s in range(2):
        sem_yd[s].inc(16 * ((CT + 1 - s) // 2))

    def g_cc(eng):
        NQ = CAP1 // 2
        for q in range(2):
            last_tile = (NQ * (q + 1)) // 128 - 1
            for s in range(2):
                eng.wait_ge(sem_yd[s].h, 16 * ((last_tile + 2 - s) // 2))
            eng.collective_compute(
                "AllGather", mybir.AluOpType.bypass,
                replica_groups=[list(range(N_CORES))],
                ins=[ye_piece.ap()[NQ * q:NQ * (q + 1), :].opt()],
                outs=[cc_ye[1].ap()[N_CORES * NQ * q:
                                    N_CORES * NQ * (q + 1), :].opt()],
            ).then_inc(cc_ye[0][q].h, 1)
        eng.wait_ge(semFP.h, t_fp)
    p.on("gpsimd", g_cc)
    for q in range(2):
        cc_ye[0][q].inc()


def emit_ssm(p, cst, x_src, x_dst, ar_in, ai_in, b_in, c_in, ooh_in,
             sum_piece, cc_sum, pfx="s", ln_bf_ext=None):
    psb = p.psb
    if ln_bf_ext is None:
        ln_bf = p.sb(pfx + "_lnbf", [128, NT_SH * D], BF16)
        ln_done = emit_ln_bf(p, cst, x_src, ln_bf, pfx=pfx + "l")
    else:
        ln_bf = ln_bf_ext
        ln_done = (cst.sem, 0)
    sdram = cc_sum[1]

    DT_ = 0.01
    ar = p.sb(pfx + "_ar", [128, 1], F32); ai = p.sb(pfx + "_ai", [128, 1], F32)
    adr = p.sb(pfx + "_adr", [128, 1], F32); adi = p.sb(pfx + "_adi", [128, 1], F32)
    tmp1 = p.sb(pfx + "_t1", [128, 1], F32); tmp2 = p.sb(pfx + "_t2", [128, 1], F32)
    tmp_w = p.sb(pfx + "_tw", [128, 128], F32); tmp_w2 = p.sb(pfx + "_tw2", [128, 128], F32)
    inv = p.sb(pfx + "_inv", [128, 1], F32)
    air = p.sb(pfx + "_air", [128, 1], F32); aii = p.sb(pfx + "_aii", [128, 1], F32)
    cr = p.sb(pfx + "_cr", [128, 1], F32); ci = p.sb(pfx + "_ci", [128, 1], F32)
    b_sb = p.sb(pfx + "_b", [128, D], F32)
    br_bf = p.sb(pfx + "_brb", [128, D], BF16); bi_bf = p.sb(pfx + "_bib", [128, D], BF16)
    btr = p.sb(pfx + "_btr", [128, D], BF16); bti = p.sb(pfx + "_bti", [128, D], BF16)
    c_sb = p.sb(pfx + "_c", [128, D], F32)
    ct_bf = p.sb(pfx + "_ctb", [128, D], BF16)
    WA = [p.sb(pfx + f"_WA{i}", [128, 128], F32) for i in range(2)]
    WB = [p.sb(pfx + f"_WB{i}", [128, 128], F32) for i in range(2)]
    VA = [p.sb(pfx + f"_VA{i}", [128, 128], F32) for i in range(2)]
    VB = [p.sb(pfx + f"_VB{i}", [128, 128], F32) for i in range(2)]
    Pr = p.sb(pfx + "_Pr", [128, 128], F32); Pi = p.sb(pfx + "_Pi", [128, 128], F32)
    Q0r = p.sb(pfx + "_Q0r", [128, 128], F32); Q0i = p.sb(pfx + "_Q0i", [128, 128], F32)
    Q1r = p.sb(pfx + "_Q1r", [128, 128], F32); Q1i = p.sb(pfx + "_Q1i", [128, 128], F32)
    A127r = p.sb(pfx + "_A7r", [1, 128], F32); A127i = p.sb(pfx + "_A7i", [1, 128], F32)
    A2r = p.sb(pfx + "_A2r", [128, 7], F32); A2i = p.sb(pfx + "_A2i", [128, 7], F32)
    tri_i = p.sb(pfx + "_tri", [128, 128], BF16)
    lnT_stg = [p.sb(pfx + f"_lnT{i}", [128, D], BF16) for i in range(2)]
    bqr = p.sb(pfx + "_bqr", [128, 128], F32); bqi = p.sb(pfx + "_bqi", [128, 128], F32)
    bqr_bf = p.sb(pfx + "_bqrb", [128, 128], BF16)
    bqi_bf = p.sb(pfx + "_bqib", [128, 128], BF16)
    srow = p.sb(pfx + "_srow", [1, 2 * 128], F32)
    Lr_sb = p.sb(pfx + "_Lr", [128, 128], F32); Li_sb = p.sb(pfx + "_Li", [128, 128], F32)
    SS = [p.sb(pfx + f"_SS{i}", [128, 256], F32) for i in range(2)]
    hsr_bf = p.sb(pfx + "_hsrb", [128, 128], BF16)
    hinr = p.sb(pfx + "_hinr", [128, NT_SH], F32); hini = p.sb(pfx + "_hini", [128, NT_SH], F32)
    ooh_sb = p.sb(pfx + "_ooh", [128, NT_SH], F32)
    ooh_bf = p.sb(pfx + "_oohb", [128, NT_SH], BF16)
    x_stg = [p.sb(pfx + f"_xst{i}", [128, D], F32) for i in range(2)]
    hsT_sb = p.sb(pfx + "_hsT", [128, 128], BF16)
    pbu = [psb["F1"][:, 0:256], psb["F2"][:, 0:256]]
    pL = psb["F3"][:, 0:256]
    ptp = psb["B0"][:, 0:1024]
    ptf = psb["F3"][:, 256:512]
    phb = psb["F0"][:, 0:256]
    py = [psb["F1024"][:, 0:512], psb["F1024"][:, 512:1024]]

    sem_ld = p.dsem("sync")
    semPrep = p.sem()
    semT = p.sem()
    semV = p.sem()
    semTT = p.sem()
    semBu = p.sem()
    semVq = p.sem()
    semSum = p.sem()
    semSD = p.dsem("gpsimd")
    semScan = p.sem()
    semY = p.sem()
    semYE = p.sem()
    semOT = p.sem()
    semOV = p.sem()
    semOB = p.sem()
    semOY = p.sem()
    semOX = [p.dsem("sync") for _ in range(2)]
    semOD = [p.dsem("gpsimd") for _ in range(2)]
    sem_sl = p.dsem("gpsimd")

    t_ld = dma_in(p, "sync", [
        (ar[:], ar_in[:, :]), (ai[:], ai_in[:, :]),
        (b_sb[:], b_in[:, :]),
        (c_sb[:].rearrange("p (c n) -> p c n", n=128),
         c_in.ap().rearrange("(c p) n -> p c n", p=128)),
        (ooh_sb[:], ooh_in[:, :]),
    ], sem_ld)

    def g_cst2(eng):
        eng.memset(tri_i[:], 0.0)
        eng.drain()
        ins = eng.affine_select(out=tri_i[:], in_=tri_i[:], compare_op=ALU.is_ge,
                                fill=1.0, base=-1, pattern=[[-1, 128]],
                                channel_multiplier=1)
        ins.then_inc(semPrep.h, 1)
        eng.drain()
    p.on("gpsimd", g_cst2)
    tP0 = semPrep.inc()

    def v_prep(eng):
        eng.wait_ge(sem_ld.h, t_ld)
        eng.tensor_scalar(tmp1[:], ar[:], 0.005, 1.0, op0=ALU.mult, op1=ALU.add)
        eng.tensor_scalar(tmp2[:], ai[:], 0.005, None, op0=ALU.mult)
        eng.tensor_scalar(adr[:], ar[:], -0.005, 1.0, op0=ALU.mult, op1=ALU.add)
        eng.tensor_scalar(adi[:], ai[:], -0.005, None, op0=ALU.mult)
        eng.drain()
        eng.tensor_tensor(inv[:], adr[:], adr[:], op=ALU.mult)
        eng.tensor_tensor(air[:], adi[:], adi[:], op=ALU.mult)
        eng.drain()
        eng.tensor_tensor(inv[:], inv[:], air[:], op=ALU.add)
        eng.drain()
        eng.reciprocal(inv[:], inv[:])
        eng.drain()
        eng.tensor_tensor(cr[:], adr[:], inv[:], op=ALU.mult)
        eng.tensor_tensor(ci[:], adi[:], inv[:], op=ALU.mult)
        eng.drain()
        eng.tensor_scalar(cr[:], cr[:], 0.1, None, op0=ALU.mult)
        eng.tensor_scalar(ci[:], ci[:], -0.1, None, op0=ALU.mult)
        eng.drain()
        eng.tensor_tensor(air[:], tmp1[:], adr[:], op=ALU.mult)
        eng.tensor_tensor(aii[:], tmp2[:], adi[:], op=ALU.mult)
        eng.drain()
        eng.tensor_tensor(air[:], air[:], aii[:], op=ALU.add)
        eng.drain()
        eng.tensor_tensor(aii[:], tmp2[:], adr[:], op=ALU.mult)
        eng.tensor_tensor(tmp1[:], tmp1[:], adi[:], op=ALU.mult)
        eng.drain()
        eng.tensor_tensor(tmp2[:], aii[:], tmp1[:], op=ALU.subtract)
        eng.drain()
        eng.tensor_tensor(adr[:], air[:], inv[:], op=ALU.mult)
        eng.tensor_tensor(adi[:], tmp2[:], inv[:], op=ALU.mult)
        eng.drain()
        eng.tensor_tensor(tmp1[:], adr[:], adr[:], op=ALU.mult)
        eng.tensor_tensor(tmp2[:], adi[:], adi[:], op=ALU.mult)
        eng.drain()
        eng.tensor_tensor(tmp1[:], tmp1[:], tmp2[:], op=ALU.add)
        eng.drain()
        eng.reciprocal(tmp1[:], tmp1[:])
        eng.drain()
        eng.tensor_tensor(air[:], adr[:], tmp1[:], op=ALU.mult)
        eng.tensor_tensor(aii[:], adi[:], tmp1[:], op=ALU.mult)
        eng.drain()
        eng.tensor_scalar(aii[:], aii[:], -1.0, None, op0=ALU.mult)
        eng.drain()
        eng.tensor_scalar(br_bf[:], b_sb[:], cr[:], None, op0=ALU.mult)
        eng.tensor_scalar(bi_bf[:], b_sb[:], ci[:], None, op0=ALU.mult)
        eng.drain()
        for (Wr, Wi, xr, xi) in ((WA, WB, adr, adi), (VA, VB, air, aii)):
            eng.memset(Wr[0][:], 1.0)
            eng.memset(Wi[0][:], 0.0)
            eng.drain()
            eng.tensor_scalar(Wr[0][:, 1:128], Wr[0][:, 1:128], xr[:], None, op0=ALU.mult)
            eng.drain()
            eng.memset(Wi[0][:, 1:128], 1.0)
            eng.drain()
            eng.tensor_scalar(Wi[0][:, 1:128], Wi[0][:, 1:128], xi[:], None, op0=ALU.mult)
            eng.drain()
            cur = 0
            for s in (1, 2, 4, 8, 16, 32, 64):
                a0, b0 = Wr[cur], Wi[cur]
                a1, b1 = Wr[1 - cur], Wi[1 - cur]
                eng.tensor_copy(a1[:, 0:s], a0[:, 0:s])
                eng.tensor_copy(b1[:, 0:s], b0[:, 0:s])
                eng.drain()
                eng.tensor_tensor(a1[:, s:128], a0[:, s:128], a0[:, 0:128 - s], op=ALU.mult)
                eng.tensor_tensor(b1[:, s:128], b0[:, s:128], a0[:, 0:128 - s], op=ALU.mult)
                eng.drain()
                eng.tensor_tensor(tmp_w[:, s:128], b0[:, 0:128 - s], a0[:, s:128], op=ALU.mult)
                eng.tensor_tensor(tmp_w2[:, s:128], b0[:, s:128], b0[:, 0:128 - s], op=ALU.mult)
                eng.drain()
                eng.tensor_tensor(a1[:, s:128], a1[:, s:128], tmp_w2[:, s:128], op=ALU.subtract)
                eng.tensor_tensor(b1[:, s:128], b1[:, s:128], tmp_w[:, s:128], op=ALU.add)
                eng.drain()
                cur = 1 - cur
        Wr, Wi = WA[1], WB[1]
        eng.tensor_tensor(A2r[:, 0:1], Wr[:, 127:128], adr[:], op=ALU.mult)
        eng.tensor_tensor(tmp1[:], Wi[:, 127:128], adi[:], op=ALU.mult)
        eng.tensor_tensor(A2i[:, 0:1], Wr[:, 127:128], adi[:], op=ALU.mult)
        eng.tensor_tensor(tmp2[:], Wi[:, 127:128], adr[:], op=ALU.mult)
        eng.drain()
        eng.tensor_tensor(A2r[:, 0:1], A2r[:, 0:1], tmp1[:], op=ALU.subtract)
        eng.tensor_tensor(A2i[:, 0:1], A2i[:, 0:1], tmp2[:], op=ALU.add)
        eng.drain()
        for k in range(1, 7):
            eng.tensor_tensor(tmp1[:], A2r[:, k - 1:k], A2r[:, k - 1:k], op=ALU.mult)
            eng.tensor_tensor(tmp2[:], A2i[:, k - 1:k], A2i[:, k - 1:k], op=ALU.mult)
            eng.tensor_tensor(A2i[:, k:k + 1], A2r[:, k - 1:k], A2i[:, k - 1:k], op=ALU.mult)
            eng.drain()
            eng.tensor_tensor(A2r[:, k:k + 1], tmp1[:], tmp2[:], op=ALU.subtract)
            eng.tensor_scalar(A2i[:, k:k + 1], A2i[:, k:k + 1], 2.0, None, op0=ALU.mult)
            eng.drain()
        eng.tensor_scalar(VA[0][:], Wr[:], adr[:], None, op0=ALU.mult)
        eng.tensor_scalar(VB[0][:], Wi[:], adr[:], None, op0=ALU.mult)
        eng.drain()
        eng.tensor_scalar(tmp_w[:], Wi[:], adi[:], None, op0=ALU.mult)
        eng.tensor_scalar(tmp_w2[:], Wr[:], adi[:], None, op0=ALU.mult)
        eng.drain()
        eng.tensor_tensor(VA[0][:], VA[0][:], tmp_w[:], op=ALU.subtract)
        ins = eng.tensor_tensor(VB[0][:], VB[0][:], tmp_w2[:], op=ALU.add)
        ins.then_inc(semPrep.h, 1)
        eng.drain()
        eng.tensor_copy(ooh_bf[:], ooh_sb[:])
        eng.drain()
    p.on("vector", v_prep)
    tP1 = semPrep.inc()

    xp_f32 = [(VA[1][:], Pr[:]), (VB[1][:], Pi[:]),
              (WA[1][:], Q0r[:]), (WB[1][:], Q0i[:]),
              (VA[0][:], Q1r[:]), (VB[0][:], Q1i[:]),
              (WA[1][:, 127:128], A127r[:]), (WB[1][:, 127:128], A127i[:])]
    xp_f32 += [(c_sb[:, 128 * c:128 * (c + 1)], ct_bf[:, 128 * c:128 * (c + 1)])
               for c in range(8)]
    xp_bf = [(br_bf[:, 128 * c:128 * (c + 1)], btr[:, 128 * c:128 * (c + 1)])
             for c in range(8)]
    xp_bf += [(bi_bf[:, 128 * c:128 * (c + 1)], bti[:, 128 * c:128 * (c + 1)])
              for c in range(8)]

    def t_xp(eng):
        eng.wait_ge(cst.sem.h, cst.tgt)
        eng.wait_ge(semPrep.h, tP1)
        for n, (srcap, dstap) in enumerate(xp_f32):
            if n >= 1:
                eng.wait_ge(semV.h, n)
            eng.transpose(ptf[:, 0:128][0:srcap.ap[1][1], 0:srcap.ap[0][1]],
                          srcap, cst.ident_f32[:]).then_inc(semT.h, 1)
            semT.inc()
        for n, (srcap, dstap) in enumerate(xp_bf):
            eng.wait_ge(semV.h, len(xp_f32) + n)
            eng.transpose(ptp[0:128, 0:128], srcap, cst.ident_bf[:]).then_inc(semT.h, 1)
            semT.inc()
    p.on("tensor", t_xp)

    def v_xp(eng):
        for n, (srcap, dstap) in enumerate(xp_f32):
            eng.wait_ge(semT.h, n + 1)
            ins = eng.tensor_copy(dstap, ptf[0:128, 0:128][0:srcap.ap[1][1], 0:srcap.ap[0][1]])
            ins.then_inc(semV.h, 1)
            eng.drain()
        for n, (srcap, dstap) in enumerate(xp_bf):
            eng.wait_ge(semT.h, len(xp_f32) + n + 1)
            ins = eng.tensor_copy(dstap, ptp[0:128, 0:128])
            ins.then_inc(semV.h, 1)
            eng.drain()
        semV.inc(len(xp_f32) + len(xp_bf))
    p.on("vector", v_xp)
    NXP = len(xp_f32) + len(xp_bf)

    def t_g(eng):
        eng.wait_ge(semV.h, NXP)
        eng.wait_ge(ln_done[0].h, ln_done[1])
        for c in range(NT_SH):
            if c >= 1:
                eng.wait_ge(semVq.h, 3 * (c - 1) + 1)
            for dc in range(DC):
                ins = eng.transpose(ptp[:, 128 * dc:128 * (dc + 1)],
                                    ln_bf[:, c * D + 128 * dc:c * D + 128 * (dc + 1)],
                                    cst.ident_bf[:])
            ins.then_inc(semTT.h, 8)
            semTT.inc(8)
            eng.wait_ge(semVq.h, 3 * c + 1)
            if c >= 2:
                eng.wait_ge(semVq.h, 3 * (c - 2) + 2)
            for dc in range(DC):
                eng.matmul(pbu[c % 2][:, 0:128],
                           lnT_stg[c % 2][:, 128 * dc:128 * (dc + 1)],
                           btr[:, 128 * dc:128 * (dc + 1)],
                           start=(dc == 0), stop=(dc == DC - 1))
            for dc in range(DC):
                ins = eng.matmul(pbu[c % 2][:, 128:256],
                                 lnT_stg[c % 2][:, 128 * dc:128 * (dc + 1)],
                                 bti[:, 128 * dc:128 * (dc + 1)],
                                 start=(dc == 0), stop=(dc == DC - 1))
            ins.then_inc(semBu.h, 1)
            semBu.inc()
            eng.wait_ge(semVq.h, 3 * c + 2)
            if c >= 1:
                eng.wait_ge(semVq.h, 3 * (c - 1) + 3)
            eng.matmul(pL[0:1, 0:128], cst.ones_bf[:], bqr_bf[:],
                       start=True, stop=True)
            ins = eng.matmul(pL[0:1, 128:256], cst.ones_bf[:], bqi_bf[:],
                             start=True, stop=True)
            ins.then_inc(semSum.h, 1)
            semSum.inc()
    p.on("tensor", t_g)

    def v_g(eng):
        for c in range(NT_SH):
            eng.wait_ge(semTT.h, 8 * (c + 1))
            if c >= 2:
                eng.wait_ge(semBu.h, c - 1)
            ins = eng.tensor_copy(lnT_stg[c % 2][:], ptp[:])
            ins.then_inc(semVq.h, 1)
            eng.drain()
            eng.wait_ge(semBu.h, c + 1)
            eng.tensor_tensor(bqr[:], pbu[c % 2][:, 0:128], Pr[:], op=ALU.mult)
            eng.tensor_tensor(tmp_w[:], pbu[c % 2][:, 128:256], Pi[:], op=ALU.mult)
            eng.tensor_tensor(bqi[:], pbu[c % 2][:, 128:256], Pr[:], op=ALU.mult)
            eng.tensor_tensor(tmp_w2[:], pbu[c % 2][:, 0:128], Pi[:], op=ALU.mult)
            eng.drain()
            eng.tensor_tensor(bqr[:], bqr[:], tmp_w[:], op=ALU.subtract)
            eng.tensor_tensor(bqi[:], bqi[:], tmp_w2[:], op=ALU.add)
            eng.drain()
            eng.tensor_copy(bqr_bf[:], bqr[:])
            ins = eng.tensor_copy(bqi_bf[:], bqi[:])
            ins.then_inc(semVq.h, 1)
            eng.drain()
            semVq.inc(2)
            eng.wait_ge(semSum.h, c + 1)
            if c >= 1:
                eng.wait_ge(semSD.h, 16 * c)
            eng.tensor_tensor(srow[0:1, 0:128], pL[0:1, 0:128], A127r[:], op=ALU.mult)
            eng.tensor_tensor(tmp_w[0:1, 0:128], pL[0:1, 128:256], A127i[:], op=ALU.mult)
            eng.tensor_tensor(srow[0:1, 128:256], pL[0:1, 128:256], A127r[:], op=ALU.mult)
            eng.tensor_tensor(tmp_w2[0:1, 0:128], pL[0:1, 0:128], A127i[:], op=ALU.mult)
            eng.drain()
            eng.tensor_tensor(srow[0:1, 0:128], srow[0:1, 0:128], tmp_w[0:1, 0:128], op=ALU.subtract)
            ins = eng.tensor_tensor(srow[0:1, 128:256], srow[0:1, 128:256], tmp_w2[0:1, 0:128], op=ALU.add)
            ins.then_inc(semVq.h, 1)
            semVq.inc()
            eng.drain()
    p.on("vector", v_g)

    def g_srow(eng):
        for c in range(NT_SH):
            eng.wait_ge(semVq.h, 3 * (c + 1))
            eng.dma_start(out=sum_piece[c:c + 1, :], in_=srow[:]
                          ).then_inc(semSD.h, 16)
        eng.wait_ge(semSD.h, 16 * NT_SH)
        eng.collective_compute(
            "AllGather", mybir.AluOpType.bypass,
            replica_groups=[list(range(N_CORES))],
            ins=[sum_piece.ap().opt()], outs=[sdram.ap().opt()],
        ).then_inc(cc_sum[0].h, 1)
    p.on("gpsimd", g_srow)
    semSD.inc(16 * NT_SH)
    cc_sum[0].inc()

    def g_sload(eng):
        eng.wait_ge(cc_sum[0].h, 1)
        eng.dma_start(out=SS[0][:], in_=sdram[:, :]).then_inc(sem_sl.h, 16)
    p.on("gpsimd", g_sload)
    sem_sl.inc(16)

    def t_scan(eng):
        eng.wait_ge(sem_sl.h, 16)
        eng.wait_ge(semV.h, NXP)
        eng.transpose(ptf[:, 0:128], SS[0][:, 0:128], cst.ident_f32[:])
        ins = eng.transpose(ptf[:, 128:256], SS[0][:, 128:256], cst.ident_f32[:])
        ins.then_inc(semScan.h, 1)
        semScan.inc()
    p.on("tensor", t_scan)

    def v_scan(eng):
        eng.wait_ge(semScan.h, 1)
        eng.tensor_copy(SS[1][:, 0:128], ptf[:, 0:128])
        eng.tensor_copy(SS[1][:, 128:256], ptf[:, 128:256])
        eng.drain()
        cur = 1
        for k in range(7):
            s = 1 << k
            a0 = SS[cur][:, 0:128]; b0 = SS[cur][:, 128:256]
            a1 = SS[1 - cur][:, 0:128]; b1 = SS[1 - cur][:, 128:256]
            eng.tensor_copy(a1[:, 0:s], a0[:, 0:s])
            eng.tensor_copy(b1[:, 0:s], b0[:, 0:s])
            eng.drain()
            eng.tensor_scalar(tmp_w[:, 0:128 - s], a0[:, 0:128 - s], A2r[:, k:k + 1], None, op0=ALU.mult)
            eng.tensor_scalar(tmp_w2[:, 0:128 - s], b0[:, 0:128 - s], A2i[:, k:k + 1], None, op0=ALU.mult)
            eng.drain()
            eng.tensor_tensor(tmp_w[:, 0:128 - s], tmp_w[:, 0:128 - s], tmp_w2[:, 0:128 - s], op=ALU.subtract)
            eng.drain()
            eng.tensor_tensor(a1[:, s:128], a0[:, s:128], tmp_w[:, 0:128 - s], op=ALU.add)
            eng.drain()
            eng.tensor_scalar(tmp_w[:, 0:128 - s], b0[:, 0:128 - s], A2r[:, k:k + 1], None, op0=ALU.mult)
            eng.tensor_scalar(tmp_w2[:, 0:128 - s], a0[:, 0:128 - s], A2i[:, k:k + 1], None, op0=ALU.mult)
            eng.drain()
            eng.tensor_tensor(tmp_w[:, 0:128 - s], tmp_w[:, 0:128 - s], tmp_w2[:, 0:128 - s], op=ALU.add)
            eng.drain()
            eng.tensor_tensor(b1[:, s:128], b0[:, s:128], tmp_w[:, 0:128 - s], op=ALU.add)
            eng.drain()
            cur = 1 - cur
        ins = eng.tensor_copy(hsr_bf[:], SS[cur][:, 0:128])
        eng.drain()
        ins2 = eng.tensor_copy(bqr_bf[:], SS[cur][:, 128:256])
        ins2.then_inc(semScan.h, 1)
        eng.drain()
    p.on("vector", v_scan)
    semScan.inc()

    def t_hsel(eng):
        eng.wait_ge(semScan.h, 2)
        eng.transpose(ptp[:, 0:128], hsr_bf[:], cst.ident_bf[:])
        ins = eng.transpose(ptp[:, 128:256], bqr_bf[:], cst.ident_bf[:])
        ins.then_inc(semScan.h, 1)
        semScan.inc()
    p.on("tensor", t_hsel)

    def v_hsel(eng):
        eng.wait_ge(semScan.h, 3)
        eng.tensor_copy(bqr_bf[:], ptp[:, 0:128])
        ins = eng.tensor_copy(bqi_bf[:], ptp[:, 128:256])
        ins.then_inc(semScan.h, 1)
        eng.drain()
    p.on("vector", v_hsel)
    semScan.inc()

    def t_hsel2(eng):
        eng.wait_ge(semScan.h, 4)
        eng.matmul(pL[:, 0:NT_SH], bqr_bf[:], ooh_bf[:], start=True, stop=True)
        ins = eng.matmul(pL[:, 64:64 + NT_SH], bqi_bf[:], ooh_bf[:],
                         start=True, stop=True)
        ins.then_inc(semScan.h, 1)
        semScan.inc()
    p.on("tensor", t_hsel2)

    def v_hsel2(eng):
        eng.wait_ge(semScan.h, 5)
        eng.tensor_copy(hinr[:], pL[:, 0:NT_SH])
        ins = eng.tensor_copy(hini[:], pL[:, 64:64 + NT_SH])
        ins.then_inc(semScan.h, 1)
        eng.drain()
    p.on("vector", v_hsel2)
    semScan.inc()

    def s_ol(eng):
        for m in range(NT_SH):
            if m >= 2:
                eng.wait_ge(semOD[m % 2].h, 16 * (m // 2))
            eng.dma_start(out=x_stg[m % 2][:],
                          in_=x_src.ap()[128 * m:128 * (m + 1), :]
                          ).then_inc(semOX[m % 2].h, 16)
    p.on("sync", s_ol)
    for s in range(2):
        semOX[s].inc(16 * ((NT_SH + 1 - s) // 2))

    def t_o(eng):
        eng.wait_ge(semScan.h, 6)
        for m in range(NT_SH):
            if m >= 1:
                eng.wait_ge(semOV.h, 4 * (m - 1) + 4)
            for dc in range(DC):
                ins = eng.transpose(ptp[:, 128 * dc:128 * (dc + 1)],
                                    ln_bf[:, m * D + 128 * dc:m * D + 128 * (dc + 1)],
                                    cst.ident_bf[:])
            ins.then_inc(semOT.h, 8)
            semOT.inc(8)
            eng.wait_ge(semOV.h, 4 * m + 1)
            if m >= 2:
                eng.wait_ge(semOV.h, 4 * (m - 2) + 2)
            for dc in range(DC):
                eng.matmul(pbu[m % 2][:, 0:128],
                           lnT_stg[m % 2][:, 128 * dc:128 * (dc + 1)],
                           btr[:, 128 * dc:128 * (dc + 1)],
                           start=(dc == 0), stop=(dc == DC - 1))
            for dc in range(DC):
                ins = eng.matmul(pbu[m % 2][:, 128:256],
                                 lnT_stg[m % 2][:, 128 * dc:128 * (dc + 1)],
                                 bti[:, 128 * dc:128 * (dc + 1)],
                                 start=(dc == 0), stop=(dc == DC - 1))
            ins.then_inc(semOB.h, 1)
            eng.wait_ge(semOV.h, 4 * m + 2)
            if m >= 1:
                eng.wait_ge(semOV.h, 4 * (m - 1) + 3)
            eng.matmul(pL[:, 0:128], tri_i[:], bqr_bf[:], start=True, stop=True)
            eng.matmul(pL[:, 128:256], tri_i[:], bqi_bf[:], start=True, stop=True)
            col_r = hinr[:, m:m + 1]
            col_i = hini[:, m:m + 1]
            cb_r = bass.AP(col_r.tensor, col_r.offset, [[col_r.ap[0][0], 128], [0, 128]])
            cb_i = bass.AP(col_i.tensor, col_i.offset, [[col_i.ap[0][0], 128], [0, 128]])
            eng.matmul(phb[:, 0:128], cb_r, cst.ident_f32[:], start=True, stop=True)
            ins = eng.matmul(phb[:, 128:256], cb_i, cst.ident_f32[:], start=True, stop=True)
            ins.then_inc(semOB.h, 1)
            semOB.inc(2)
            eng.wait_ge(semOV.h, 4 * m + 3)
            eng.transpose(ptp[:, 0:128], hsr_bf[:], cst.ident_bf[:]).then_inc(semOT.h, 1)
            semOT.inc()
            eng.wait_ge(semOV.h, 4 * m + 4)
            if m >= 1:
                eng.wait_ge(semYE.h, m)
            eng.matmul(py[0][:], hsT_sb[:], ct_bf[:, 0:512],
                       start=True, stop=True)
            ins = eng.matmul(py[1][:], hsT_sb[:], ct_bf[:, 512:1024],
                             start=True, stop=True)
            ins.then_inc(semOY.h, 1)
            semOY.inc()
    p.on("tensor", t_o)

    def v_o(eng):
        for m in range(NT_SH):
            eng.wait_ge(semOT.h, 9 * m + 8)
            if m >= 2:
                eng.wait_ge(semOB.h, 2 * (m - 2) + 1)
            ins = eng.tensor_copy(lnT_stg[m % 2][:], ptp[:])
            ins.then_inc(semOV.h, 1)
            eng.drain()
            eng.wait_ge(semOB.h, 2 * m + 1)
            eng.tensor_tensor(bqr[:], pbu[m % 2][:, 0:128], Pr[:], op=ALU.mult)
            eng.tensor_tensor(tmp_w[:], pbu[m % 2][:, 128:256], Pi[:], op=ALU.mult)
            eng.tensor_tensor(bqi[:], pbu[m % 2][:, 128:256], Pr[:], op=ALU.mult)
            eng.tensor_tensor(tmp_w2[:], pbu[m % 2][:, 0:128], Pi[:], op=ALU.mult)
            eng.drain()
            eng.tensor_tensor(bqr[:], bqr[:], tmp_w[:], op=ALU.subtract)
            eng.tensor_tensor(bqi[:], bqi[:], tmp_w2[:], op=ALU.add)
            eng.drain()
            eng.tensor_copy(bqr_bf[:], bqr[:])
            ins = eng.tensor_copy(bqi_bf[:], bqi[:])
            ins.then_inc(semOV.h, 1)
            eng.drain()
            eng.wait_ge(semOB.h, 2 * m + 2)
            eng.tensor_tensor(tmp_w[:], pL[:, 0:128], Q0r[:], op=ALU.mult)
            eng.tensor_tensor(tmp_w2[:], pL[:, 128:256], Q0i[:], op=ALU.mult)
            eng.drain()
            eng.tensor_tensor(tmp_w[:], tmp_w[:], tmp_w2[:], op=ALU.subtract)
            eng.drain()
            eng.tensor_tensor(tmp_w2[:], phb[:, 0:128], Q1r[:], op=ALU.mult)
            eng.drain()
            eng.tensor_tensor(tmp_w[:], tmp_w[:], tmp_w2[:], op=ALU.add)
            eng.drain()
            eng.tensor_tensor(tmp_w2[:], phb[:, 128:256], Q1i[:], op=ALU.mult)
            eng.drain()
            eng.tensor_tensor(tmp_w[:], tmp_w[:], tmp_w2[:], op=ALU.subtract)
            eng.drain()
            ins = eng.tensor_copy(hsr_bf[:], tmp_w[:])
            ins.then_inc(semOV.h, 1)
            eng.drain()
            eng.wait_ge(semOT.h, 9 * m + 9)
            ins = eng.tensor_copy(hsT_sb[:], ptp[:, 0:128])
            ins.then_inc(semOV.h, 1)
            eng.drain()
            semOV.inc(4)
            eng.wait_ge(semOY.h, m + 1)
            eng.wait_ge(semOX[m % 2].h, 16 * (m // 2 + 1))
            if m >= 2:
                eng.wait_ge(semOD[m % 2].h, 16 * (m // 2))
            eng.scalar_tensor_tensor(x_stg[m % 2][:, 0:512], py[0][:], 1.0,
                                     x_stg[m % 2][:, 0:512], op0=ALU.mult, op1=ALU.add)
            eng.drain()
            ins = eng.scalar_tensor_tensor(x_stg[m % 2][:, 512:1024], py[1][:], 1.0,
                                           x_stg[m % 2][:, 512:1024], op0=ALU.mult, op1=ALU.add)
            ins.then_inc(semYE.h, 1)
            semYE.inc()
            eng.drain()
    p.on("vector", v_o)

    def g_xo(eng):
        for m in range(NT_SH):
            eng.wait_ge(semYE.h, m + 1)
            eng.dma_start(out=x_dst.ap()[128 * m:128 * (m + 1), :],
                          in_=x_stg[m % 2][:]).then_inc(semOD[m % 2].h, 16)
        for s in range(2):
            eng.wait_ge(semOD[s].h, 16 * ((NT_SH + 1 - s) // 2))
    p.on("gpsimd", g_xo)
    for s in range(2):
        semOD[s].inc(16 * ((NT_SH + 1 - s) // 2))

def build_mega(upto="A"):
    p = Prog(pooled=True)
    nc = p.nc
    x_in = p.dram("x_sh", [T_SH, D], F32, kind="ExternalInput")
    rw0_in = p.dram("router_w", [D, E], F32, kind="ExternalInput")
    w1_in = p.dram("w1", [D, FF], F32, kind="ExternalInput")
    b1_in = p.dram("b1", [1, FF], F32, kind="ExternalInput")
    w2_in = p.dram("w2", [FF, D], F32, kind="ExternalInput")
    b2_in = p.dram("b2", [1, D], F32, kind="ExternalInput")
    ecol_in = p.dram("e_col", [128, 1], F32, kind="ExternalInput")
    ohsel_in = p.dram("ohsel", [128, 16], F32, kind="ExternalInput")
    xo = p.dram("xo", [T_SH, D], F32, kind="ExternalOutput")
    x_dram = p.dram("x_resid", [T_SH, D], F32)

    cst = Consts(p)

    def _tcar(eng, _c=cst):
        eng.wait_ge(_c.sem.h, _c.tgt)
        return eng.matmul(p.psb["F0"][0:1, 0:1], _c.ones_bf[0:1, 0:1],
                          _c.ones_bf[0:1, 0:1], start=True, stop=True)
    p.tcarrier = _tcar
    p.psb = {
        "F1024": p.ps("PSF1024", [128, 1024], F32),
        "B0": p.ps("PSB0", [128, 1024], BF16),
        "B1": p.ps("PSB1", [128, 1024], BF16),
        "F0": p.ps("PSF0", [128, 512], F32),
        "F1": p.ps("PSF1", [128, 512], F32),
        "F2": p.ps("PSF2", [128, 512], F32),
        "F3": p.ps("PSF3", [128, 512], F32),
    }
    gcar0 = p.sb("gcar0", [128, 16], F32)
    fpcarA = p.sb("fpcarA", [128, 16], F32)

    def cc_pair(name, shape, dt, nq=1):
        piece = p.dram(name + "_pc", shape, dt)
        full = p.dram(name + "_all", [shape[0] * N_CORES] + list(shape[1:]), dt)
        if nq == 1:
            return piece, (p.sem(), full)
        return piece, ([p.sem() for _ in range(nq)], full)

    lnA_pc, ccA_ln = cc_pair("lnA", [T_SH, D], BF16)
    cntA_pc, ccA_cnt = cc_pair("cntA", [NT_SH, E], F32)
    metaA_pc, ccA_meta = cc_pair("metaA", [128, 48], F32)
    yeA_pc, ccA_ye = cc_pair("yeA", [CAP0, D], BF16)
    slot_tabA = p.dram("slot_tabA", [CAP0, 2], F32)

    lastA = xo if upto == "A" else x_dram
    with p.phase("A1"):
        emit_ln_route0(p, cst, x_in, rw0_in, lnA_pc, cntA_pc, metaA_pc,
                       ccA_ln, ccA_cnt, ccA_meta, gcar0, pfx="a1")
    with p.phase("A2"):
        emit_expert_l0(p, cst, w1_in, b1_in, w2_in, b2_in, ecol_in, ohsel_in,
                       ccA_ln, ccA_cnt, ccA_meta, ccA_ye, yeA_pc, slot_tabA,
                       fpcarA)

    if upto == "A":
        with p.phase("A3"):
            emit_combine(p, cst, x_in, lastA, ccA_ye, [fpcarA], [gcar0])
        p.finalize()
        return nc

    wq_in = p.dram("wq", [D, D], F32, kind="ExternalInput")
    wk_in = p.dram("wk", [D, D], F32, kind="ExternalInput")
    wv_in = p.dram("wv", [D, D], F32, kind="ExternalInput")
    wo_in = p.dram("wo", [D, D], F32, kind="ExternalInput")
    cq_in = p.dram("cq", [128, 16], F32, kind="ExternalInput")
    aidx_in = p.dram("aidx", [128, 16], F32, kind="ExternalInput")
    qB_pc, ccB_q = cc_pair("qB", [T_SH, D], BF16)
    ktB_pc, ccB_kt = cc_pair("ktB", [D, T_SH], BF16)
    vB_pc, ccB_v = cc_pair("vB", [T_SH, D], BF16)
    aoB_pc, ccB_ao = cc_pair("aoB", [T_SH, D], BF16)

    lastB = xo if upto == "B" else x_dram
    with p.phase("Bo"):
        keyrow = p.sb("b_keyrow", [128, 1024], F32)
        cqsc = p.sb("b_cqsc", [128, 256], F32)
        cq_sb = p.sb("b_cq", [128, 16], F32)
        aidx_sb = p.sb("b_aidx", [128, 16], F32)
        sKR = p._mk_sem("sKR")
        sCQ = p._mk_sem("sCQ")
        sCQL = p._mk_sem("sCQL")
        sIDX = p._mk_sem("sIDX")

        def g_kr(eng):
            ins = eng.iota(keyrow[:], pattern=[[1, 1024]], base=0,
                           channel_multiplier=0,
                           allow_small_or_imprecise_dtypes=True)
            ins.then_inc(sKR.h, 1)
            eng.drain()
        p.on("gpsimd", g_kr)
        sKR.inc()

        def s_cql(eng):
            eng.dma_start(out=cq_sb[:], in_=cq_in[:, :]).then_inc(sCQL.h, 16)
            eng.dma_start(out=aidx_sb[:], in_=aidx_in[:, :]).then_inc(sCQL.h, 16)
        p.on("sync", s_cql)
        tCQL = sCQL.inc(32)

        def v_cq(eng):
            eng.wait_ge(sCQL.h, tCQL)
            for sc in range(16):
                ins = eng.tensor_scalar(cqsc[:, 16 * sc:16 * (sc + 1)], cq_sb[:],
                                        float(1 - 1024 * sc), None, op0=ALU.add)
            ins.then_inc(sCQ.h, 1)
            eng.drain()
            eng.tensor_copy(aidx_sb[0:1, 0:1],
                            aidx_sb[0:1, 0:1]).then_inc(sIDX.h, 1)
            eng.drain()
        p.on("vector", v_cq)
        sCQ.inc()
        sIDX.inc()

        with p.phase("BoL"):
            ln_bfB = p.sb("b1_lnbf", [128, NT_SH * D], BF16)
            with p.phase("A3"):
                emit_combine(p, cst, x_in, lastA, ccA_ye, [fpcarA], [gcar0],
                             ln_out=ln_bfB)
            with p.phase("B1"):
                emit_attn_qkv(p, cst, ln_bfB, (cst.sem, 0),
                              (wq_in, wk_in, wv_in),
                              qB_pc, ktB_pc, vB_pc, ccB_q, ccB_kt, ccB_v)
        with p.phase("B2"):
            emit_attn_flash(p, cst,
                            (keyrow, sKR, cqsc, sCQ, cq_sb, aidx_sb, sIDX),
                            ccB_q, ccB_kt, ccB_v, ccB_ao, aoB_pc,
                            wo_in, x_dram, lastB)

    if upto == "B":
        p.finalize()
        return nc

    lnC_pc, ccC_ln = cc_pair("lnC", [T_SH, D], BF16)
    cntC_pc, ccC_cnt = cc_pair("cntC", [NT_SH, E], F32)
    metaC_pc, ccC_meta = cc_pair("metaC", [128, 48], F32)
    yeC_pc, ccC_ye = cc_pair("yeC", [CAP0, D], BF16)
    slot_tabC = p.dram("slot_tabC", [CAP0, 2], F32)
    lastC = xo if upto == "C" else x_dram
    with p.phase("C1"):
        emit_ln_route0(p, cst, x_dram, rw0_in, lnC_pc, cntC_pc, metaC_pc,
                       ccC_ln, ccC_cnt, ccC_meta, gcar0, pfx="c1")
    with p.phase("C2"):
        emit_expert_l0(p, cst, w1_in, b1_in, w2_in, b2_in, ecol_in, ohsel_in,
                       ccC_ln, ccC_cnt, ccC_meta, ccC_ye, yeC_pc, slot_tabC,
                       fpcarA, pfx="e0c")
    if upto == "C":
        with p.phase("C3"):
            emit_combine(p, cst, x_dram, lastC, ccC_ye, [fpcarA], [gcar0],
                         pfx="c0c")
        p.finalize()
        return nc

    rg1_in = p.dram("router_gate", [D, E], F32, kind="ExternalInput")
    gw_in = p.dram("gate_w", [D, FF], F32, kind="ExternalInput")
    uw_in = p.dram("up_w", [D, FF], F32, kind="ExternalInput")
    dw_in = p.dram("down_w", [FF, D], F32, kind="ExternalInput")
    g1carD = p.sb("g1carD", [128, 16], F32)
    g2carD = p.sb("g2carD", [128, 16], F32)
    fpcarD1 = p.sb("fpcarD1", [128, 16], F32)
    fpcarD2 = p.sb("fpcarD2", [128, 16], F32)
    lnD_pc, ccD_ln = cc_pair("lnD", [T_SH, D], BF16)
    metaD_pc, ccD_meta = cc_pair("metaD", [128, 160], F32)
    yeD_pc, ccD_ye = cc_pair("yeD", [CAP1, D], BF16, nq=2)
    slot_tabD = p.dram("slot_tabD", [CAP1, 2], F32)
    cnt_rtD = p.dram("cnt_rtD", [NT_FULL, E], F32)
    xeT_dD = p.dram("xeT_dD", [DC * 128, CAP1], BF16)
    h_dD = p.dram("h_dD", [FF, CAP1], BF16)
    lastD = xo if upto == "D" else x_dram
    with p.phase("Do"):
        ln_sbD = p.sb("lnD_sb", [128, NT_SH * D], F32)
        with p.phase("C3"):
            emit_combine(p, cst, x_dram, lastC, ccC_ye, [fpcarA], [gcar0],
                         pfx="c0c", ln_out=ln_sbD, ln_piece=lnD_pc,
                         cc_ln=ccD_ln)
        with p.phase("D1"):
            emit_route1_only(p, cst, ln_sbD, rg1_in, metaD_pc, ccD_meta,
                             g1carD, g2carD, pfx="d1")
    with p.phase("D2"):
        emit_expert_l1(p, cst, gw_in, uw_in, dw_in, ecol_in, ohsel_in,
                       ccD_ln, ccD_meta, ccD_ye, yeD_pc, slot_tabD, cnt_rtD,
                       xeT_dD, h_dD, fpcarD1, fpcarD2, pfx="e1d")
    if upto == "D":
        with p.phase("D3"):
            emit_combine(p, cst, x_dram, lastD, ccD_ye, [fpcarD1, fpcarD2],
                         [g1carD, g2carD], pfx="c2d")
        p.finalize()
        return nc

    ar_in = p.dram("A_real", [128, 1], F32, kind="ExternalInput")
    ai_in = p.dram("A_imag", [128, 1], F32, kind="ExternalInput")
    b_in = p.dram("Bmat", [128, D], F32, kind="ExternalInput")
    c_in = p.dram("Cmat", [D, 128], F32, kind="ExternalInput")
    ooh_in = p.dram("own_oh", [128, NT_SH], F32, kind="ExternalInput")
    sumE_pc, ccE_sum = cc_pair("sumE", [16, 256], F32)
    lastE = xo if upto == "E" else x_dram
    with p.phase("Eo"):
        ln_bfE = p.sb("lnE_bf", [128, NT_SH * D], BF16)
        with p.phase("D3"):
            emit_combine(p, cst, x_dram, lastD, ccD_ye, [fpcarD1, fpcarD2],
                         [g1carD, g2carD], pfx="c2d", ln_out=ln_bfE)
        with p.phase("E"):
            emit_ssm(p, cst, x_dram, lastE, ar_in, ai_in, b_in, c_in, ooh_in,
                     sumE_pc, ccE_sum, pfx="s", ln_bf_ext=ln_bfE)

    if upto == "E":
        p.finalize()
        return nc

    lnF_pc, ccF_ln = cc_pair("lnF", [T_SH, D], BF16)
    metaF_pc, ccF_meta = cc_pair("metaF", [128, 160], F32)
    yeF_pc, ccF_ye = cc_pair("yeF", [CAP1, D], BF16, nq=2)
    slot_tabF = p.dram("slot_tabF", [CAP1, 2], F32)
    cnt_rtF = p.dram("cnt_rtF", [NT_FULL, E], F32)
    xeT_dF = p.dram("xeT_dF", [DC * 128, CAP1], BF16)
    h_dF = p.dram("h_dF", [FF, CAP1], BF16)
    with p.phase("F1"):
        emit_ln_route1(p, cst, x_dram, rg1_in, lnF_pc, metaF_pc,
                       ccF_ln, ccF_meta, g1carD, g2carD, pfx="f1")
    with p.phase("F2"):
        emit_expert_l1(p, cst, gw_in, uw_in, dw_in, ecol_in, ohsel_in,
                       ccF_ln, ccF_meta, ccF_ye, yeF_pc, slot_tabF, cnt_rtF,
                       xeT_dF, h_dF, fpcarD1, fpcarD2, pfx="e1f")
    with p.phase("F3"):
        emit_combine(p, cst, x_dram, xo, ccF_ye, [fpcarD1, fpcarD2],
                     [g1carD, g2carD], pfx="c2f")

    p.finalize()
    return nc


MEGA_UPTO = "F"
_NC_CACHE = {}


def _get(name, builder, *args):
    key = (name,) + args
    if key not in _NC_CACHE:
        _NC_CACHE[key] = builder(*args)
    return _NC_CACHE[key]


def _pm_cat(parts):
    return np.concatenate(parts, axis=1)


def kernel(L0_attn_in_features, L0_attn_w_q, L0_attn_w_k, L0_attn_w_v,
           L0_attn_w_o, L0_ffn_router_w, L0_ffn_w1, L0_ffn_b1, L0_ffn_w2,
           L0_ffn_b2, L1_attn_A_real, L1_attn_A_imag, L1_attn_B, L1_attn_C,
           L1_ffn_router_gate, L1_ffn_gate_w, L1_ffn_up_w, L1_ffn_down_w):
    from concourse.bass_utils import run_bass_kernel_spmd
    X = np.ascontiguousarray(np.asarray(L0_attn_in_features,
                                        np.float32).reshape(T_FULL, D))
    cores = list(range(N_CORES))

    def shards(x):
        return [np.ascontiguousarray(x[T_SH * k:T_SH * (k + 1)]) for k in cores]

    def run(nc, in_maps, core_ids=None):
        return run_bass_kernel_spmd(nc, in_maps,
                                    core_ids=core_ids or cores).results

    nc_mega = _get("mega" + MEGA_UPTO, build_mega, MEGA_UPTO)
    if MEGA_UPTO < "F":
        nc_r0 = _get("r0", build_route_l0_launch)
        nc_e0 = _get("e0", build_expert_l0_launch)
        nc_c1 = _get("c1", build_combine_launch, 1)
        nc_c2 = _get("c2", build_combine_launch, 2)
        nc_r1 = _get("r1", build_route_l1_launch)
        nc_e1 = _get("e1", build_expert_l1_launch)
        nc_ssm = _get("ssm", build_ssm_launch)
        nc_att = {k: _get("att", build_attn_launch, k) for k in cores}

    ecols = [np.full((128, 1), e, np.float32) for e in range(8)]
    ohsels = []
    for k in cores:
        oh = np.zeros((128, 16), np.float32)
        for m in range(16):
            oh[16 * k + m, m] = 1.0
        ohsels.append(oh)

    def route0(x):
        r = run(nc_r0, [dict(x_sh=s, router_w=L0_ffn_router_w)
                        for s in shards(x)])
        ln = np.concatenate([rr["ln_bf"] for rr in r], 0)
        counts = np.concatenate([rr["counts"].reshape(16, 8) for rr in r], 0)
        idx = _pm_cat([rr["idx_o"] for rr in r])
        gate = _pm_cat([rr["gate_o"] for rr in r])
        wcum = _pm_cat([rr["wcum_o"] for rr in r])
        return ln, counts, idx, gate, wcum

    def moe0(x):
        ln, counts, idx, gate, wcum = route0(x)
        r = run(nc_e0, [dict(ln_full=ln, counts_full=counts, idx_full=idx,
                             gate_full=gate, wcum_full=wcum,
                             w1=L0_ffn_w1[e], b1=L0_ffn_b1[e:e + 1],
                             w2=L0_ffn_w2[e], b2=L0_ffn_b2[e:e + 1],
                             e_col=ecols[e]) for e in range(8)])
        ye = np.concatenate([rr["y_out"] for rr in r], 0)
        fp = r[0]["flatpos_o"]
        r2 = run(nc_c1, [dict(x_sh=s, ye_all=ye,
                              fp0=fp[:, 16 * k:16 * (k + 1)],
                              g0=gate[:, 16 * k:16 * (k + 1)])
                         for k, s in enumerate(shards(x))])
        return np.concatenate([rr["xo"] for rr in r2], 0), ln

    def moe1(x):
        r = run(nc_r1, [dict(x_sh=s, router_gate=L1_ffn_router_gate)
                        for s in shards(x)])
        ln = np.concatenate([rr["ln_bf"] for rr in r], 0)
        gates = _pm_cat([rr["gates_o"] for rr in r])
        g1 = _pm_cat([rr["g1_o"] for rr in r])
        g2 = _pm_cat([rr["g2_o"] for rr in r])
        i1 = _pm_cat([rr["i1_o"] for rr in r])
        i2 = _pm_cat([rr["i2_o"] for rr in r])
        r2 = run(nc_e1, [dict(ln_full=ln, gates_full=gates, i1_full=i1,
                              i2_full=i2, gate_w=L1_ffn_gate_w[e],
                              up_w=L1_ffn_up_w[e], down_w=L1_ffn_down_w[e],
                              e_col=ecols[e]) for e in range(8)])
        ye = np.concatenate([rr["y_out"] for rr in r2], 0)
        fp1 = r2[0]["fp1_o"]
        fp2 = r2[0]["fp2_o"]
        r3 = run(nc_c2, [dict(x_sh=s, ye_all=ye,
                              fp0=fp1[:, 16 * k:16 * (k + 1)],
                              fp1=fp2[:, 16 * k:16 * (k + 1)],
                              g0=g1[:, 16 * k:16 * (k + 1)],
                              g1=g2[:, 16 * k:16 * (k + 1)])
                         for k, s in enumerate(shards(x))])
        return np.concatenate([rr["xo"] for rr in r3], 0)

    mega_maps = [dict(x_sh=shards(X)[k], router_w=L0_ffn_router_w,
                      w1=L0_ffn_w1[k], b1=L0_ffn_b1[k:k + 1],
                      w2=L0_ffn_w2[k], b2=L0_ffn_b2[k:k + 1],
                      e_col=ecols[k], ohsel=ohsels[k]) for k in cores]
    if MEGA_UPTO >= "D":
        for k in cores:
            mega_maps[k].update(router_gate=L1_ffn_router_gate,
                                gate_w=L1_ffn_gate_w[k], up_w=L1_ffn_up_w[k],
                                down_w=L1_ffn_down_w[k])
    if MEGA_UPTO >= "E":
        for k in cores:
            ooh = np.zeros((128, 16), np.float32)
            for m in range(16):
                c = 16 * k + m - 1
                if c >= 0:
                    ooh[c, m] = 1.0
            mega_maps[k].update(
                A_real=np.asarray(L1_attn_A_real, np.float32).reshape(128, 1),
                A_imag=np.asarray(L1_attn_A_imag, np.float32).reshape(128, 1),
                Bmat=L1_attn_B, Cmat=L1_attn_C, own_oh=ooh)
    if MEGA_UPTO >= "B":
        for k in cores:
            cqv = (128 * (8 * np.arange(16) + k)[None, :]
                   + np.arange(128)[:, None]).astype(np.float32)
            g = 16 * k + np.arange(16)
            aidxv = ((2048 * (g % 8) + 128 * (g // 8))[None, :]
                     + np.arange(128)[:, None]).astype(np.float32)
            mega_maps[k].update(wq=L0_attn_w_q, wk=L0_attn_w_k,
                                wv=L0_attn_w_v, wo=L0_attn_w_o, cq=cqv,
                                aidx=aidxv)
    rm = run(nc_mega, mega_maps)
    x = np.concatenate([rr["xo"] for rr in rm], 0)
    if MEGA_UPTO < "B":
        r = run(nc_r0, [dict(x_sh=s, router_w=L0_ffn_router_w) for s in shards(x)])
        ln = np.concatenate([rr["ln_bf"] for rr in r], 0)
        parts = []
        for k in cores:
            rr = run(nc_att[k], [dict(ln_full=ln, ln_own=ln[T_SH * k:T_SH * (k + 1)],
                                      x_sh=shards(x)[k], wq=L0_attn_w_q,
                                      wk=L0_attn_w_k, wv=L0_attn_w_v,
                                      wo=L0_attn_w_o)], core_ids=[k])
            parts.append(rr[0]["xo"])
        x = np.concatenate(parts, 0)
    if MEGA_UPTO < "C":
        x, _ = moe0(x)
    if MEGA_UPTO < "D":
        x = moe1(x)
    if MEGA_UPTO < "E":
        r = run(nc_r0, [dict(x_sh=s, router_w=L0_ffn_router_w) for s in shards(x)])
        ln = np.concatenate([rr["ln_bf"] for rr in r], 0)
        in_maps = []
        for k in cores:
            ooh = np.zeros((128, 16), np.float32)
            for m in range(16):
                c = 16 * k + m - 1
                if c >= 0:
                    ooh[c, m] = 1.0
            in_maps.append(dict(ln_full=ln, ln_own=ln[T_SH * k:T_SH * (k + 1)],
                                x_sh=shards(x)[k],
                                A_real=np.asarray(L1_attn_A_real,
                                                  np.float32).reshape(128, 1),
                                A_imag=np.asarray(L1_attn_A_imag,
                                                  np.float32).reshape(128, 1),
                                Bmat=L1_attn_B, Cmat=L1_attn_C, own_oh=ooh))
        r = run(nc_ssm, in_maps)
        x = np.concatenate([rr["xo"] for rr in r], 0)
    if MEGA_UPTO < "F":
        x = moe1(x)
    return x.reshape(1, T_FULL, D)
